# version 61
# baseline (speedup 1.0000x reference)
import numpy as np
import ml_dtypes
from contextlib import ExitStack
import concourse.bass as bass
import concourse.mybir as mybir
from concourse.bass_utils import run_bass_kernel_spmd

F32 = mybir.dt.float32
BF16 = mybir.dt.bfloat16
I32 = mybir.dt.int32
AF = mybir.ActivationFunctionType
OP = mybir.AluOpType

D = 1024
T = 2048
NSEQ = 2
NT = NSEQ * T
KT = 8
FT = 32
CH = 512
NCH = NT // CH
C3 = 256
NC3 = NT // C3
TWO_PI = float(2.0 * np.pi)
POOL_S = []


class Sem:
    def __init__(self, es, nc, name):
        self.h = es.enter_context(nc.semaphore(name))
        self.n = 0

    def res(self, amt=1):
        self.n += amt
        return self.n


def build_program():
    nc = bass.Bass("TRN2", target_bir_lowering=False)

    xf_d = nc.dram_tensor("xf", [128, NCH, KT, CH], F32, kind="ExternalInput").ap()
    w1t_d = nc.dram_tensor("w1t", [128, KT * D], F32, kind="ExternalInput").ap()
    v1t_d = nc.dram_tensor("v1t", [128, KT * D], F32, kind="ExternalInput").ap()
    w2t_d = nc.dram_tensor("w2t", [128, KT * D], F32, kind="ExternalInput").ap()
    wf1_d = nc.dram_tensor("wf1", [128, KT * 4096], F32, kind="ExternalInput").ap()
    wf2_d = nc.dram_tensor("wf2", [128, FT * D], F32, kind="ExternalInput").ap()
    cc_d = nc.dram_tensor("cc", [128, 448], F32, kind="ExternalInput").ap()
    fc_d = nc.dram_tensor("fc", [128, 128], F32, kind="ExternalInput").ap()
    cst_d = nc.dram_tensor("cst", [128, 8], F32, kind="ExternalInput").ap()
    out_d = nc.dram_tensor("out", [128, NC3, KT, C3], F32, kind="ExternalOutput").ap()
    xbf_d = nc.dram_tensor("xbf_d", [128, NCH, KT, CH], BF16).ap()
    gb_d = nc.dram_tensor("gb_d", [128, NCH, KT, CH], BF16).ap()
    r1f_d = nc.dram_tensor("r1f", [128, NCH, KT, CH], F32).ap()
    y1f_d = nc.dram_tensor("y1f", [128, NCH, KT, CH], F32).ap()
    y1b_d = nc.dram_tensor("y1bd", [128, NCH, KT, CH], BF16).ap()

    es = ExitStack()
    g1 = ExitStack()
    gv = ExitStack()
    gx = ExitStack()
    g2 = ExitStack()
    g2b = ExitStack()
    g2be = ExitStack()
    g2d = ExitStack()
    g3 = ExitStack()

    def sbg(g, name, shape, dt=F32):
        return g.enter_context(nc.sbuf_tensor(name, shape, dt))

    cc = sbg(es, "cc_s", [128, 64])
    fc = sbg(es, "fc_s", [128, 128])
    cst = sbg(es, "cst_s", [128, 8])
    cw = sbg(es, "cw", [128, 32])
    alp = sbg(es, "alp", [128, 32])
    beta = sbg(es, "beta", [128, 32])
    ones_bf = sbg(es, "ones_bf", [128, 1], BF16)
    ones_f = sbg(es, "ones_f", [128, 1])
    ones1 = sbg(es, "ones1", [1, 128])
    cabc = sbg(gv, "cabc", [128, 384])
    abar = sbg(gv, "abar", [128, 128])
    gg = sbg(gv, "gg", [128, 128])
    tfc1 = sbg(gv, "tfc1", [128, 32])
    tfc2 = sbg(gv, "tfc2", [128, 32])
    tfc3 = sbg(gv, "tfc3", [128, 32])
    tfci = sbg(gv, "tfci", [128, 32])
    tfcf = sbg(gv, "tfcf", [128, 32])

    v1tb = sbg(gv, "v1tb", [128, KT * D], BF16)
    xmix = sbg(gx, "xmix", [128, KT * NT], BF16)
    w1tb = sbg(g1, "w1tb", [128, KT * D], BF16)
    xstg = sbg(g1, "xstg", [128, KT * CH])
    xbfst = [sbg(g1, f"xbfst{i}", [128, KT * CH], BF16) for i in range(2)]
    convb = sbg(g1, "convb", [128, T])

    mm_ps = [es.enter_context(nc.psum_tensor(f"mmps{i}", [128, 512], F32)) for i in range(4)]
    st_ps = [es.enter_context(nc.psum_tensor(f"stps{i}", [128, 512], F32)) for i in range(2)]
    bc_ps = [es.enter_context(nc.psum_tensor(f"bcps{i}", [128, 512], F32)) for i in range(2)]

    import os
    if os.environ.get("KDBG3"):
        DBG3 = [nc.dram_tensor("dbg3r", [128, KT * C3], F32, kind="ExternalOutput").ap(),
                nc.dram_tensor("dbg3s", [1, 2 * C3], F32, kind="ExternalOutput").ap(),
                nc.dram_tensor("dbg3a", [128, FT * C3], BF16, kind="ExternalOutput").ap(),
                nc.dram_tensor("dbg3y", [128, KT * C3], BF16, kind="ExternalOutput").ap()]
    else:
        DBG3 = None
    s_ld = Sem(es, nc, "s_ld")
    s_xstg = Sem(es, nc, "s_xstg")
    s_xbv = [Sem(es, nc, f"s_xbv{i}") for i in range(2)]
    s_stg2s = [Sem(es, nc, f"s_stg2s{i}") for i in range(2)]
    s_gbc = [Sem(es, nc, f"s_gbc{i}") for i in range(2)]
    s_r1l = [Sem(es, nc, f"s_r1l{i}") for i in range(2)]
    s_y3 = [Sem(es, nc, f"s_y3{i}") for i in range(2)]
    s_wst = [Sem(es, nc, f"s_wst{i}") for i in range(2)]
    s_stxbf = [Sem(es, nc, f"s_stxbf{i}") for i in range(2)]
    s_stgb = [Sem(es, nc, f"s_stgb{i}") for i in range(2)]
    s_sty1 = [Sem(es, nc, f"s_sty1{i}") for i in range(2)]
    s_str1 = Sem(es, nc, "s_str1")
    s_st = Sem(es, nc, "s_st")
    s_out = Sem(es, nc, "s_out")
    s_mm = Sem(es, nc, "s_mm")
    s_ve = Sem(es, nc, "s_ve")
    s_ac = Sem(es, nc, "s_ac")
    s_po = Sem(es, nc, "s_po")

    E = {k: [] for k in ("sync", "tensor", "vector", "scalar", "gpsimd")}

    def em(eng, fn):
        E[eng].append(fn)

    def wait(eng, sem, val):
        if val > 0:
            em(eng, lambda e, s=sem.h, v=val: e.wait_ge(s, v))

    def dma(eng, out, in_, sem):
        v = sem.res(16)
        em(eng, lambda e, o=out, i=in_, s=sem.h: e.dma_start(out=o, in_=i).then_inc(s, 16))
        return v

    dma("sync", cc[:], cc_d[:, 0:64], s_ld)
    dma("sync", cabc[:], cc_d[:, 64:448], s_ld)
    dma("sync", fc[:], fc_d, s_ld)
    ld_consts = dma("sync", cst[:], cst_d, s_ld)

    ve_wconv = 0
    for wd, wb in ((w1t_d, w1tb), (v1t_d, v1tb)):
        for piece in range(2):
            ldv = dma("sync", xstg[:],
                      wd[:, piece * 4 * D:(piece + 1) * 4 * D], s_xstg)
            wait("vector", s_xstg, ldv)
            cv = s_ve.res()
            em("vector", lambda v, dst=wb, piece=piece: v.tensor_copy(
                dst[:, piece * 4 * D:(piece + 1) * 4 * D], xstg[:]).then_inc(s_ve.h, 1))
            wait("sync", s_ve, cv)
            ve_wconv = cv

    wait("scalar", s_ld, ld_consts)
    wait("vector", s_ld, ld_consts)
    em("scalar", lambda a: a.activation(abar[:], cabc[:, 0:128], AF.Sigmoid).then_inc(s_ac.h, 1))
    s_ac.res()

    def emv(fn):
        v0 = s_ve.res()
        em("vector", lambda e, f=fn: f(e).then_inc(s_ve.h, 1))
        wait("vector", s_ve, v0)
        return v0

    emv(lambda v: v.tensor_mul(gg[:], cabc[:, 128:256], cabc[:, 256:384]))

    TH = fc[:, 32:64]
    WR = fc[:, 64:96]
    WI = fc[:, 96:128]
    B1C = fc[:, 0:32]

    def trig(dst, shift):
        emv(lambda v, sh=shift: v.tensor_scalar(
            tfc3[:], TH, float(262144.0 / TWO_PI),
            float((64.5 + sh / TWO_PI) * 262144.0), OP.mult, OP.add))
        emv(lambda v: v.tensor_copy(tfci[:].bitcast(I32), tfc3[:]))
        emv(lambda v: v.tensor_scalar(tfci[:].bitcast(I32), tfci[:].bitcast(I32),
                                      262143, None, OP.bitwise_and))
        dv = emv(lambda v: v.tensor_copy(tfcf[:], tfci[:].bitcast(I32)))
        wait("scalar", s_ve, dv)
        av = s_ac.res()
        em("scalar", lambda a, d2=dst: a.activation(
            d2, tfcf[:], AF.Sin, scale=cst[:, 3:4],
            bias=cst[:, 2:3]).then_inc(s_ac.h, 1))
        wait("vector", s_ac, av)
        return av

    trig(tfc1[:], 0.0)
    actrig = trig(tfc2[:], float(np.pi / 2.0))
    wait("vector", s_ac, actrig)
    emv(lambda v: v.tensor_mul(tfc3[:], tfc2[:], WR))
    emv(lambda v: v.tensor_mul(tfc1[:], tfc1[:], WI))
    emv(lambda v: v.tensor_add(tfc3[:], tfc3[:], tfc1[:]))
    emv(lambda v: v.tensor_mul(tfc3[:], tfc3[:], WI))
    emv(lambda v: v.tensor_scalar_mul(alp[:], tfc3[:], 0.1))
    emv(lambda v: v.tensor_scalar(tfc2[:], TH, float(262144.0 / TWO_PI),
                                  float(64.5 * 262144.0), OP.mult, OP.add))
    emv(lambda v: v.scalar_tensor_tensor(
        cw[:], B1C, float(0.1 * 262144.0 / TWO_PI), tfc2[:], OP.mult, OP.add))
    emv(lambda v: v.scalar_tensor_tensor(
        beta[:], B1C, float(-0.1 * 262144.0 / TWO_PI), cw[:], OP.mult, OP.add))
    emv(lambda v: v.tensor_scalar(beta[:], beta[:],
                                  float(-TWO_PI / (0.1 * 262144.0)), None, OP.mult))
    emv(lambda v: v.memset(ones_bf[:], 1.0))
    emv(lambda v: v.memset(ones_f[:], 1.0))
    emv(lambda v: v.memset(ones1[:], 1.0))

    bank_free = {b: (None, 0) for b in range(4)}
    xcvt = []
    xbf_store = []
    x1_evac = {}
    for ch in range(NCH):
        if ch >= 1:
            wait("sync", s_po, xcvt[ch - 1])
        else:
            wait("sync", s_ve, ve_wconv)
        ldv = dma("sync", xstg[:], xf_d[:, ch], s_xstg)
        if ch >= 2:
            wait("gpsimd", s_ac, x1_evac[(KT - 1, ch - 2)])
            wait("gpsimd", s_stxbf[ch % 2], xbf_store[ch - 2])
        wait("gpsimd", s_xstg, ldv)
        pv = s_po.res()
        em("gpsimd", lambda p, ch=ch: p.tensor_copy(
            xbfst[ch % 2][:], xstg[:]).then_inc(s_po.h, 1))
        xcvt.append(pv)
        wait("sync", s_po, pv)
        sv = dma("sync", xbf_d[:, ch], xbfst[ch % 2][:].rearrange("p (k j) -> p k j", k=KT),
                 s_stxbf[ch % 2])
        xbf_store.append(sv)

        for ct in range(KT):
            bank = ct % 2
            sem, val = bank_free[bank]
            mmv = s_mm.res()
            def f1(t, bank=bank, ct=ct, ch=ch, sem=sem, val=val,
                   pv=pv, wcv=(ve_wconv if (ch == 0 and ct == 0) else 0)):
                if ct == 0:
                    t.wait_ge(s_po.h, pv)
                if wcv:
                    t.wait_ge(s_ve.h, wcv)
                if sem is not None:
                    t.wait_ge(sem, val)
                for k in range(KT):
                    ins = t.matmul(mm_ps[bank][:, 0:CH],
                                   w1tb[:, k * D + ct * 128:k * D + (ct + 1) * 128],
                                   xbfst[ch % 2][:, k * CH:(k + 1) * CH],
                                   start=(k == 0), stop=(k == KT - 1))
                    if k == KT - 1:
                        ins.then_inc(s_mm.h, 1)
            em("tensor", f1)
            av = s_ac.res()
            def ev(a, mmv=mmv, bank=bank, ct=ct, ch=ch):
                a.wait_ge(s_mm.h, mmv)
                a.add(xmix[:, ct * NT + ch * CH:ct * NT + (ch + 1) * CH],
                      mm_ps[bank][:, 0:CH], cc[:, ct:ct + 1]).then_inc(s_ac.h, 1)
            em("scalar", ev)
            bank_free[bank] = (s_ac.h, av)
            x1_evac[(ct, ch)] = av
    mm_phase1 = s_mm.n

    xc_done_ct = {}
    for ct in range(KT):
        for seq in range(NSEQ):
            need = x1_evac[(ct, (seq + 1) * (NCH // NSEQ) - 1)]
            cv0 = s_ve.res()
            cv1 = s_ve.res()
            cv = s_ve.res()
            def cvf(v, need=need, seq=seq, ct=ct, m0=cv0, m1=cv1):
                base = ct * NT + seq * T
                v.wait_ge(s_ac.h, need)
                v.tensor_scalar(convb[:], xmix[:, base:base + T],
                                cc[:, 48 + ct:49 + ct], None, OP.mult).then_inc(s_ve.h, 1)
                v.wait_ge(s_ve.h, m0)
                v.scalar_tensor_tensor(convb[:, 1:T], xmix[:, base:base + T - 1],
                                       cc[:, 40 + ct:41 + ct], convb[:, 1:T],
                                       OP.mult, OP.add).then_inc(s_ve.h, 1)
                v.wait_ge(s_ve.h, m1)
                v.scalar_tensor_tensor(convb[:, 0:T - 1], xmix[:, base + 1:base + T],
                                       cc[:, 56 + ct:57 + ct], convb[:, 0:T - 1],
                                       OP.mult, OP.add).then_inc(s_ve.h, 1)
            em("vector", cvf)
            av = s_ac.res()
            def slf(a, cv=cv, seq=seq, ct=ct):
                a.wait_ge(s_ve.h, cv)
                a.activation(xmix[:, ct * NT + seq * T:ct * NT + (seq + 1) * T],
                             convb[:], AF.Silu).then_inc(s_ac.h, 1)
            em("scalar", slf)
            wait("vector", s_ac, av)
        xc_done_ct[ct] = s_ac.n
    g1.close()

    hbuf = sbg(g2, "hbuf", [128, T])
    phbuf = [sbg(g2, f"phbuf{i}", [128, T]) for i in range(2)]
    ydve = sbg(g2, "ydve", [128, T])
    ypool = sbg(g2, "ypool", [128, T])
    ptmp = sbg(g2, "ptmp", [128, T])
    xbfv = [sbg(g2, f"xbfv{i}", [128, KT * CH], BF16) for i in range(2)]
    vtile = [sbg(g2, f"vtile{i}", [128, NT], BF16) for i in range(2)]

    wait("vector", s_mm, mm_phase1)
    wait("gpsimd", s_mm, mm_phase1)
    wait("tensor", s_ac, xc_done_ct[KT - 1])

    pb_guard = [0, 0]
    pool_cnt = 0
    v_done_ct = {}
    gb_store = []
    for ct in range(KT):
        for ch in range(NCH):
            gi = (ct * NCH + ch) % 2
            if ct * NCH + ch >= 2:
                prev = v_done_ct.get((ct * NCH + ch - 2))
                wait("sync", s_ac, prev)
            else:
                wait("sync", s_ac, xc_done_ct[KT - 1])
            if ct == 0 and ch == 0:
                wait("sync", s_stxbf[0], s_stxbf[0].n)
                wait("sync", s_stxbf[1], s_stxbf[1].n)
            ldv = dma("sync", xbfv[gi][:].rearrange("p (k j) -> p k j", k=KT),
                      xbf_d[:, ch], s_xbv[gi])
            bank = ch % 2
            sem, val = bank_free[bank]
            mmv = s_mm.res()
            def fv(t, bank=bank, ct=ct, ch=ch, gi=gi, sem=sem, val=val, ldv=ldv):
                t.wait_ge(s_xbv[gi].h, ldv)
                if sem is not None:
                    t.wait_ge(sem, val)
                for k in range(KT):
                    ins = t.matmul(mm_ps[bank][:, 0:CH],
                                   v1tb[:, k * D + ct * 128:k * D + (ct + 1) * 128],
                                   xbfv[gi][:, k * CH:(k + 1) * CH],
                                   start=(k == 0), stop=(k == KT - 1))
                    if k == KT - 1:
                        ins.then_inc(s_mm.h, 1)
            em("tensor", fv)
            av = s_ac.res()
            def evv(a, mmv=mmv, bank=bank, ct=ct, ch=ch,
                    stg=(gb_store[ct - 2] if (ct >= 2 and ch == 0) else 0)):
                a.wait_ge(s_mm.h, mmv)
                if stg:
                    a.wait_ge(s_stgb[ct % 2].h, stg)
                a.activation(vtile[ct % 2][:, ch * CH:(ch + 1) * CH],
                             mm_ps[bank][:, 0:CH], AF.Silu,
                             bias=cc[:, 8 + ct:9 + ct]).then_inc(s_ac.h, 1)
            em("scalar", evv)
            bank_free[bank] = (s_ac.h, av)
            v_done_ct[ct * NCH + ch] = av
        v_ready = v_done_ct[ct * NCH + NCH - 1]

        wait("vector", s_ac, xc_done_ct[ct])
        wait("gpsimd", s_ac, xc_done_ct[ct])
        for seq in range(NSEQ):
            src = xmix[:, ct * NT + seq * T:ct * NT + (seq + 1) * T]
            first_dve = True
            first_pool = True
            for s in range(16):
                col = ct * 16 + s
                if s not in POOL_S:
                    sc0 = s_ve.res()
                    em("vector", lambda v, col=col, src=src, m=sc0: (
                        v.tensor_tensor_scan(
                            hbuf[:], abar[:, col:col + 1].broadcast_to([128, T]), src,
                            0.0, OP.mult, OP.add).then_inc(s_ve.h, 1),
                        v.wait_ge(s_ve.h, m)))
                    if first_dve:
                        em("vector", lambda v, col=col: v.tensor_scalar(
                            ydve[:], hbuf[:], gg[:, col:col + 1], None, OP.mult))
                        first_dve = False
                    else:
                        em("vector", lambda v, col=col: v.scalar_tensor_tensor(
                            ydve[:], hbuf[:], gg[:, col:col + 1], ydve[:], OP.mult, OP.add))
                else:
                    pi = pool_cnt % 2
                    pool_cnt += 1
                    if pb_guard[pi]:
                        wait("vector", s_po, pb_guard[pi])
                    sv = s_ve.res()
                    em("vector", lambda v, col=col, src=src, pi=pi: v.tensor_tensor_scan(
                        phbuf[pi][:], abar[:, col:col + 1].broadcast_to([128, T]), src,
                        0.0, OP.mult, OP.add).then_inc(s_ve.h, 1))
                    pv = s_po.res()
                    if first_pool:
                        def pf(p, n=sv, pi=pi, col=col):
                            p.wait_ge(s_ve.h, n)
                            p.tensor_tensor(ypool[:], phbuf[pi][:],
                                            gg[:, col:col + 1].broadcast_to([128, T]),
                                            OP.mult).then_inc(s_po.h, 1)
                        first_pool = False
                    else:
                        def pf(p, n=sv, pi=pi, col=col):
                            p.wait_ge(s_ve.h, n)
                            p.tensor_tensor(ptmp[:], phbuf[pi][:],
                                            gg[:, col:col + 1].broadcast_to([128, T]), OP.mult)
                            p.tensor_tensor(ypool[:], ypool[:], ptmp[:],
                                            OP.add).then_inc(s_po.h, 1)
                    em("gpsimd", pf)
                    pb_guard[pi] = pv
            dt0 = s_ve.res()
            em("vector", lambda v, ct=ct, src=src, m=dt0: (
                v.scalar_tensor_tensor(
                    ydve[:], src, cc[:, 32 + ct:33 + ct], ydve[:],
                    OP.mult, OP.add).then_inc(s_ve.h, 1),
                v.wait_ge(s_ve.h, m)))
            if POOL_S:
                wait("vector", s_po, s_po.n)
                dt1 = s_ve.res()
                em("vector", lambda v, m=dt1: (
                    v.tensor_add(ydve[:], ydve[:], ypool[:]).then_inc(s_ve.h, 1),
                    v.wait_ge(s_ve.h, m)))
            wait("vector", s_ac, v_ready)
            gvv = s_ve.res()
            em("vector", lambda v, ct=ct, seq=seq: v.tensor_tensor(
                vtile[ct % 2][:, seq * T:(seq + 1) * T], ydve[:],
                vtile[ct % 2][:, seq * T:(seq + 1) * T],
                OP.mult).then_inc(s_ve.h, 1))
            wait("gpsimd", s_ve, gvv)
        gate_ct = s_ve.n
        wait("sync", s_ve, gate_ct)
        sv2 = dma("sync", gb_d[:, :, ct, :],
                  vtile[ct % 2][:].rearrange("p (h j) -> p h j", h=NCH), s_stgb[ct % 2])
        gb_store.append(sv2)
        if ct >= 1:
            pass
    gate_all = s_ve.n
    g2.close()
    gx.close()
    gv.close()

    ssum = sbg(g2b, "ssum", [1, NT])
    ssq = sbg(g2b, "ssq", [1, NT])
    rtmp1 = sbg(g2b, "rtmp1", [1, NT])
    istd_row = sbg(g2b, "istd_row", [1, NT])
    mi_row = sbg(g2b, "mi_row", [1, NT])
    r1ch = [sbg(g2b, f"r1ch{i}", [128, KT * CH]) for i in range(2)]
    r1bch = [sbg(g2b, f"r1bch{i}", [128, KT * CH], BF16) for i in range(2)]
    w2tb = sbg(g2be, "w2tb", [128, KT * D], BF16)
    stg2 = [sbg(g2be, f"stg2{i}", [128, KT * CH]) for i in range(2)]
    gbch = [sbg(g2be, f"gbch{i}", [128, KT * CH], BF16) for i in range(2)]
    sqch = sbg(g2be, "sqch", [128, KT * CH], BF16)

    for par in range(2):
        wait("vector", s_stgb[par], s_stgb[par].n)
        wait("gpsimd", s_stgb[par], s_stgb[par].n)
        wait("sync", s_stgb[par], s_stgb[par].n)
    w2cv = 0
    for piece in range(2):
        ldv = dma("sync", stg2[piece][:, 0:4 * D],
                  w2t_d[:, piece * 4 * D:(piece + 1) * 4 * D], s_stg2s[piece])
        wait("vector", s_stg2s[piece], ldv)
        cv = s_ve.res()
        em("vector", lambda v, piece=piece: v.tensor_copy(
            w2tb[:, piece * 4 * D:(piece + 1) * 4 * D],
            stg2[piece][:, 0:4 * D]).then_inc(s_ve.h, 1))
        w2cv = cv

    bank_free = {b: (None, 0) for b in range(4)}
    wait("tensor", s_ve, w2cv)
    stat_evac = []
    r1_store = []
    gb_loads = []
    for ch in range(NCH):
        if ch >= 2:
            wait("sync", s_str1, r1_store[ch - 2])
        else:
            wait("sync", s_ve, w2cv)
        ld_x = dma("sync", stg2[ch % 2][:], xf_d[:, ch], s_stg2s[ch % 2])
        if ch >= 2:
            wait("sync", s_ve, gb_loads[ch - 2])
        ld_g = dma("sync", gbch[ch % 2][:], gb_d[:, ch], s_gbc[ch % 2])
        for ct in range(KT):
            bank = ct % 2
            sem, val = bank_free[bank]
            mmv = s_mm.res()
            def f(t, bank=bank, ct=ct, ch=ch, sem=sem, val=val, ld_g=ld_g):
                if ct == 0:
                    t.wait_ge(s_gbc[ch % 2].h, ld_g)
                if sem is not None:
                    t.wait_ge(sem, val)
                for k in range(KT):
                    ins = t.matmul(mm_ps[bank][:, 0:CH],
                                   w2tb[:, k * D + ct * 128:k * D + (ct + 1) * 128],
                                   gbch[ch % 2][:, k * CH:(k + 1) * CH],
                                   start=(k == 0), stop=(k == KT - 1))
                    if k == KT - 1:
                        ins.then_inc(s_mm.h, 1)
            em("tensor", f)
            vv = s_ve.res()
            def ev(v, mmv=mmv, bank=bank, ct=ct, ch=ch, ld_x=ld_x):
                v.wait_ge(s_mm.h, mmv)
                v.wait_ge(s_stg2s[ch % 2].h, ld_x)
                v.scalar_tensor_tensor(
                    r1ch[ch % 2][:, ct * CH:(ct + 1) * CH],
                    mm_ps[bank][:, 0:CH], cc[:, 16 + ct:17 + ct],
                    stg2[ch % 2][:, ct * CH:(ct + 1) * CH],
                    OP.add, OP.add).then_inc(s_ve.h, 1)
            em("vector", ev)
            bank_free[bank] = (s_ve.h, vv)
            bv = s_ve.res()
            em("vector", lambda v, ct=ct, ch=ch: v.tensor_copy(
                r1bch[ch % 2][:, ct * CH:(ct + 1) * CH],
                r1ch[ch % 2][:, ct * CH:(ct + 1) * CH]).then_inc(s_ve.h, 1))
            av = s_ac.res()
            em("scalar", lambda a, n=bv, ct=ct, ch=ch: (
                a.wait_ge(s_ve.h, n),
                a.activation(sqch[:, ct * CH:(ct + 1) * CH],
                             r1bch[ch % 2][:, ct * CH:(ct + 1) * CH],
                             AF.Square).then_inc(s_ac.h, 1)))
            sq_last = av
        gb_loads.append(s_ve.n)
        stv = s_mm.res()
        def stmm_b(t, ch=ch, av=sq_last,
                   prev=(stat_evac[ch - 1] if ch > 0 else 0)):
            if prev:
                t.wait_ge(s_ve.h, prev)
            t.wait_ge(s_ac.h, av)
            for ct2 in range(KT):
                t.matmul(st_ps[0][0:1, 0:CH], ones_bf[:],
                         r1bch[ch % 2][:, ct2 * CH:(ct2 + 1) * CH],
                         start=(ct2 == 0), stop=(ct2 == KT - 1))
                ins = t.matmul(st_ps[1][0:1, 0:CH], ones_bf[:],
                               sqch[:, ct2 * CH:(ct2 + 1) * CH],
                               start=(ct2 == 0), stop=(ct2 == KT - 1))
                if ct2 == KT - 1:
                    ins.then_inc(s_mm.h, 1)
        em("tensor", stmm_b)
        st_done = stv
        sev = s_ve.res()
        def stev(v, ch=ch, n=st_done):
            v.wait_ge(s_mm.h, n)
            v.tensor_copy(ssum[0:1, ch * CH:(ch + 1) * CH], st_ps[0][0:1, 0:CH])
            v.tensor_copy(ssq[0:1, ch * CH:(ch + 1) * CH],
                          st_ps[1][0:1, 0:CH]).then_inc(s_ve.h, 1)
        em("vector", stev)
        stat_evac.append(sev)
        wait("sync", s_ve, sev)
        stv2 = dma("sync", r1f_d[:, ch],
                   r1ch[ch % 2][:].rearrange("p (k j) -> p k j", k=KT), s_str1)
        r1_store.append(stv2)

    wait("vector", s_ve, stat_evac[-1])
    r0 = s_ve.res()
    em("vector", lambda v, m=r0: (v.tensor_scalar(ssum[:], ssum[:], float(1.0 / D),
                                                  None, OP.mult).then_inc(s_ve.h, 1),
                                  v.wait_ge(s_ve.h, m)))
    r1v = s_ve.res()
    em("vector", lambda v, m=r1v: (v.tensor_mul(rtmp1[:], ssum[:], ssum[:]).then_inc(s_ve.h, 1),
                                   v.wait_ge(s_ve.h, m)))
    r2v = s_ve.res()
    em("vector", lambda v, m=r2v: (v.tensor_scalar(ssq[:], ssq[:], float(1.0 / D),
                                                   None, OP.mult).then_inc(s_ve.h, 1),
                                   v.wait_ge(s_ve.h, m)))
    varv = s_ve.res()
    em("vector", lambda v, m=varv: (v.tensor_sub(ssq[:], ssq[:], rtmp1[:]).then_inc(s_ve.h, 1),
                                    v.wait_ge(s_ve.h, m)))
    wait("scalar", s_ve, varv)
    sqv = s_ac.res()
    em("scalar", lambda a: a.activation(rtmp1[:], ssq[:], AF.Sqrt,
                                        bias=cst[0:1, 1:2]).then_inc(s_ac.h, 1))
    wait("vector", s_ac, sqv)
    r3v = s_ve.res()
    em("vector", lambda v, m=r3v: (v.reciprocal(istd_row[:], rtmp1[:]).then_inc(s_ve.h, 1),
                                   v.wait_ge(s_ve.h, m)))
    ln1_rows = s_ve.res()
    em("vector", lambda v: v.tensor_mul(mi_row[:], ssum[:], istd_row[:]).then_inc(s_ve.h, 1))

    g2be.close()
    y1ch = [sbg(g2d, f"y1ch{i}", [128, KT * CH]) for i in range(2)]
    y1bch = [sbg(g2d, f"y1bch{i}", [128, KT * CH], BF16) for i in range(2)]

    wait("tensor", s_ve, ln1_rows)
    y1_store = []
    for ch in range(NCH):
        if ch < 2:
            wait("sync", s_str1, r1_store[-1])
        else:
            wait("sync", s_sty1[ch % 2], y1_store[ch - 2])
        ld_r1 = dma("sync", r1ch[ch % 2][:].rearrange("p (h j) -> p h j", h=KT),
                    r1f_d[:, ch], s_r1l[ch % 2])
        bcv = s_mm.res()
        def bc(t, ch=ch):
            t.matmul(bc_ps[0][:, 0:CH], ones1[:], istd_row[0:1, ch * CH:(ch + 1) * CH],
                     start=True, stop=True)
            t.matmul(bc_ps[1][:, 0:CH], ones1[:], mi_row[0:1, ch * CH:(ch + 1) * CH],
                     start=True, stop=True).then_inc(s_mm.h, 1)
        em("tensor", bc)
        for ct in range(KT):
            nz0 = s_ve.res()
            nzv = s_ve.res()
            def nz(v, ct=ct, ch=ch, bcv=bcv, ld_r1=ld_r1, m=nz0):
                v.wait_ge(s_mm.h, bcv)
                v.wait_ge(s_r1l[ch % 2].h, ld_r1)
                sl = slice(ct * CH, (ct + 1) * CH)
                v.tensor_tensor(y1ch[ch % 2][:, sl], r1ch[ch % 2][:, sl],
                                bc_ps[0][:, 0:CH], OP.mult).then_inc(s_ve.h, 1)
                v.wait_ge(s_ve.h, m)
                v.tensor_sub(y1ch[ch % 2][:, sl], y1ch[ch % 2][:, sl],
                             bc_ps[1][:, 0:CH]).then_inc(s_ve.h, 1)
            em("vector", nz)
            pv = s_ve.res()
            em("vector", lambda v, ct=ct, ch=ch: v.tensor_copy(
                y1bch[ch % 2][:, ct * CH:(ct + 1) * CH],
                y1ch[ch % 2][:, ct * CH:(ct + 1) * CH]).then_inc(s_ve.h, 1))
        wait("tensor", s_ve, s_ve.n)
        wait("sync", s_ve, s_ve.n)
        dma("sync", y1f_d[:, ch], y1ch[ch % 2][:].rearrange("p (h j) -> p h j", h=KT),
            s_sty1[ch % 2])
        wait("sync", s_ve, s_ve.n)
        yv = dma("sync", y1b_d[:, ch], y1bch[ch % 2][:].rearrange("p (h j) -> p h j", h=KT),
                 s_sty1[ch % 2])
        y1_store.append(yv)
    g2d.close()
    g2b.close()

    g3w = ExitStack()
    w1fb = sbg(g3, "w1fb", [128, KT * 4096], BF16)
    w2fb = sbg(g3, "w2fb", [128, FT * D], BF16)
    y3b = [sbg(g3, f"y3b{i}", [128, KT * C3], BF16) for i in range(2)]
    y3f = [sbg(g3, f"y3f{i}", [128, KT * C3]) for i in range(2)]
    actb = [sbg(g3, f"actb{i}", [128, FT * C3], BF16) for i in range(2)]
    wq = [sbg(g3, f"wq{i}", [128, C3]) for i in range(3)]
    s1q = [sbg(g3, f"s1q{i}", [128, C3]) for i in range(3)]
    r2ch = sbg(g3, "r2ch", [128, KT * C3])
    sq3f = sbg(g3, "sq3f", [128, KT * C3], BF16)
    wstg3 = [sbg(g3w, f"wstg3{i}", [128, 256]) for i in range(4)]

    wait("sync", s_sty1[0], s_sty1[0].n)
    wait("sync", s_sty1[1], s_sty1[1].n)
    wf_ready = []
    wf1_cv = {}
    wf2_cv = {}
    idx = 0
    pieces = []
    for f2 in range(16):
        for k in range(KT):
            pieces.append(("w1", f2, k * 4096 + f2 * 256))
    for c2 in range(4):
        for f in range(FT):
            pieces.append(("w2", c2, f * D + c2 * 256))
    for idx, (kind, grp, col) in enumerate(pieces):
        buf = wstg3[idx % 4]
        if idx >= 4:
            wait("sync", s_po, wf_ready[idx - 4])
        src_d = wf1_d if kind == "w1" else wf2_d
        dst = w1fb if kind == "w1" else w2fb
        ldv = dma("sync", buf[:], src_d[:, col:col + 256], s_wst[idx % 2])
        wait("gpsimd", s_wst[idx % 2], ldv)

        if idx == 0:
            wait("gpsimd", s_sty1[0], s_sty1[0].n)
            wait("gpsimd", s_sty1[1], s_sty1[1].n)
        cv = s_po.res()
        em("gpsimd", lambda p, col=col, buf=buf, dst=dst: p.tensor_copy(
            dst[:, col:col + 256], buf[:]).then_inc(s_po.h, 1))
        wf_ready.append(cv)
        if kind == "w1" and idx % KT == KT - 1:
            wf1_cv[grp] = cv
        if kind == "w2" and (idx - 128) % FT == FT - 1:
            wf2_cv[grp] = cv
    stream_done = wf_ready[-1]
    w2_all_cv = 0
    w1_all_cv = 0
    g3w.close()
    s2um = sbg(g3, "s2um", [1, C3])
    r2t1 = sbg(g3, "r2t1", [1, C3])
    r2t2 = sbg(g3, "r2t2", [1, C3])

    out_sts = []
    hbanks = [mm_ps[0], mm_ps[1], bc_ps[0], bc_ps[1]]
    hid_bank_free = [(None, 0)] * 4
    w2banks = [mm_ps[2], mm_ps[3]]
    w2bank_free = [(None, 0)] * 2
    prev_stev3 = 0
    wmm_last = {}
    ev3_done = {}

    def emit_fmm(i3, ld_y, ft):
        bank = ft % 4
        sem, val = hid_bank_free[bank]
        mmv = s_mm.res()
        def fmm(t, bank=bank, ft=ft, i3=i3, sem=sem, val=val, ld_y=ld_y,
                wcv=(wf1_cv[ft // 2] if i3 == 0 else 0)):
            if ft < 2:
                t.wait_ge(s_y3[i3 % 2].h, ld_y)
            if wcv:
                t.wait_ge(s_po.h, wcv)
            if sem is not None:
                t.wait_ge(sem, val)
            for k in range(KT):
                ins = t.matmul(hbanks[bank][:, 0:C3],
                               w1fb[:, k * 4096 + ft * 128:k * 4096 + (ft + 1) * 128],
                               y3b[i3 % 2][:, k * C3:(k + 1) * C3],
                               start=(k == 0), stop=(k == KT - 1))
                if k == KT - 1:
                    ins.then_inc(s_mm.h, 1)
        em("tensor", fmm)
        return mmv

    def emit_stage_qv1(i3, ft, mmv):
        bank = ft % 4
        qi = ft % 3
        wv = s_ve.res()
        def qv1(v, n=mmv, ft=ft, qi=qi, bank=bank):
            v.wait_ge(s_mm.h, n)
            v.tensor_scalar(wq[qi][:], hbanks[bank][:, 0:C3],
                            float(0.1 * 262144.0 / TWO_PI),
                            cw[:, ft:ft + 1], OP.mult, OP.add).then_inc(s_ve.h, 1)
        em("vector", qv1)
        hid_bank_free[bank] = (s_ve.h, wv)
        return wv

    def emit_stage_mid(i3, ft, sin_guard):
        qi = ft % 3
        def mid(v, ft=ft, qi=qi, sg=sin_guard):
            if sg:
                v.wait_ge(s_ac.h, sg)
            v.tensor_copy(s1q[qi][:].bitcast(I32), wq[qi][:])
            v.tensor_scalar(wq[qi][:], wq[qi][:], float(TWO_PI / (0.1 * 262144.0)),
                            beta[:, ft:ft + 1], OP.mult, OP.add)
            v.tensor_scalar(s1q[qi][:].bitcast(I32), s1q[qi][:].bitcast(I32),
                            262143, None, OP.bitwise_and).then_inc(s_ve.h, 1)
        em("vector", mid)
        return s_ve.res()

    def emit_stage_cvt2(i3, ft):
        def c2(v, qi=ft % 3):
            v.tensor_copy(s1q[qi][:], s1q[qi][:].bitcast(I32)).then_inc(s_ve.h, 1)
        em("vector", c2)
        return s_ve.res()

    def emit_stage_sin(i3, ft, c2v, aguard):
        av = s_ac.res()
        def qa(a, n=c2v, qi=ft % 3, ft=ft, i3=i3, ag=aguard):
            a.wait_ge(s_ve.h, n)
            if ag:
                a.wait_ge(s_mm.h, ag)
            a.activation(actb[i3 % 2][:, ft * C3:(ft + 1) * C3], s1q[qi][:],
                         AF.Sin, scale=cst[:, 3:4],
                         bias=cst[:, 2:3]).then_inc(s_ac.h, 1)
        em("scalar", qa)
        return av

    def emit_stage_qv3(i3, ft, av):
        a2v0 = s_ve.res()
        a2v = s_ve.res()
        def qv3(v, n=av, qi=ft % 3, ft=ft, i3=i3, m=a2v0):
            v.wait_ge(s_ac.h, n)
            asl = actb[i3 % 2][:, ft * C3:(ft + 1) * C3]
            v.scalar_tensor_tensor(asl, asl, alp[:, ft:ft + 1],
                                   wq[qi][:], OP.mult, OP.add).then_inc(s_ve.h, 1)
            v.wait_ge(s_ve.h, m)
            v.tensor_scalar(asl, asl, 0.0, None, OP.max).then_inc(s_ve.h, 1)
        em("vector", qv3)
        return a2v

    def emit_tail_piece(p_i3, p_act, ct, piece):
        bi = ct % 2
        if piece == 0:
            sem, val = w2bank_free[bi]
        else:
            sem, val = (None, 0)
        mmv = s_mm.res() if piece == 3 else 0
        def wmm(t, bi=bi, ct=ct, p_i3=p_i3, piece=piece, na=p_act,
                sem=sem, val=val,
                w2cv2=(wf2_cv[ct // 2] if (p_i3 == 0 and piece == 0) else 0)):
            if piece == 0:
                t.wait_ge(s_ve.h, na)
            if w2cv2:
                t.wait_ge(s_po.h, w2cv2)
            if sem is not None:
                t.wait_ge(sem, val)
            for f in range(piece * 8, piece * 8 + 8):
                ins = t.matmul(w2banks[bi][:, 0:C3],
                               w2fb[:, f * D + ct * 128:f * D + (ct + 1) * 128],
                               actb[p_i3 % 2][:, f * C3:(f + 1) * C3],
                               start=(f == 0), stop=(f == FT - 1))
                if f == FT - 1:
                    ins.then_inc(s_mm.h, 1)
        em("tensor", wmm)
        if piece == 3:
            wmm_last[p_i3] = mmv
        return mmv

    def emit_tail_unit(p_i3, p_ldy, p_act, ct, mmv):
        bi = ct % 2
        vv = s_ve.res()
        def ev3(v, mmv=mmv, bi=bi, ct=ct, p_i3=p_i3, p_ldy=p_ldy,
                stguard=(out_sts[p_i3 - 1][1] if (p_i3 >= 1 and ct == 0) else 0)):
            v.wait_ge(s_mm.h, mmv)
            v.wait_ge(s_y3[p_i3 % 2].h, p_ldy)
            if stguard:
                v.wait_ge(s_out.h, stguard)
            sl = slice(ct * C3, (ct + 1) * C3)
            v.scalar_tensor_tensor(r2ch[:, sl], w2banks[bi][:, 0:C3],
                                   cc[:, 24 + ct:25 + ct],
                                   y3f[p_i3 % 2][:, sl], OP.add, OP.add).then_inc(s_ve.h, 1)
        em("vector", ev3)
        w2bank_free[bi] = (s_ve.h, vv)
        ev3_done[p_i3] = vv
        av = s_ac.res()
        em("scalar", lambda a, n=vv, ct=ct: (
            a.wait_ge(s_ve.h, n),
            a.activation(sq3f[:, ct * C3:(ct + 1) * C3],
                         r2ch[:, ct * C3:(ct + 1) * C3],
                         AF.Square).then_inc(s_ac.h, 1)))
        stv = s_mm.res()
        def stmm3(t, ct=ct, vv=vv, av=av, prevv=(prev_stev3 if ct == 0 else 0)):
            if prevv:
                t.wait_ge(s_ve.h, prevv)
            t.wait_ge(s_ve.h, vv)
            t.matmul(st_ps[0][0:1, 0:C3], ones_f[:],
                     r2ch[:, ct * C3:(ct + 1) * C3],
                     start=(ct == 0), stop=(ct == KT - 1))
            t.wait_ge(s_ac.h, av)
            t.matmul(st_ps[1][0:1, 0:C3], ones_bf[:],
                     sq3f[:, ct * C3:(ct + 1) * C3],
                     start=(ct == 0), stop=(ct == KT - 1)).then_inc(s_mm.h, 1)
        em("tensor", stmm3)
        return stv

    def emit_ln2_tail(p_i3):
        nonlocal_vals = {}
        sev0 = s_ve.res(1)
        def stev3a(v, n=st3_holder[0], m=sev0,
                   sd=(stream_done if p_i3 == 0 else 0)):
            if sd:
                v.wait_ge(s_po.h, sd)
            v.wait_ge(s_mm.h, n)
            v.tensor_copy(s2um[:], st_ps[0][0:1, 0:C3])
            v.tensor_copy(r2t2[:], st_ps[1][0:1, 0:C3]).then_inc(s_ve.h, 1)
            v.wait_ge(s_ve.h, m)
        em("vector", stev3a)
        sev1 = s_ve.res(1)
        def stev3b(v, m=sev1):
            v.tensor_scalar(s2um[:], s2um[:], float(1.0 / D), None, OP.mult)
            v.tensor_scalar(r2t2[:], r2t2[:], float(1.0 / D), None,
                            OP.mult).then_inc(s_ve.h, 1)
            v.wait_ge(s_ve.h, m)
        em("vector", stev3b)
        sev = s_ve.res(1)
        def stev3c(v, m=sev):
            v.tensor_mul(r2t1[:], s2um[:], s2um[:])
            v.engine_nop()
            v.engine_nop()
            v.tensor_sub(r2t2[:], r2t2[:], r2t1[:]).then_inc(s_ve.h, 1)
            v.wait_ge(s_ve.h, m)
        em("vector", stev3c)
        wait("scalar", s_ve, sev)
        a3 = s_ac.res()
        em("scalar", lambda a: a.activation(r2t1[:], r2t2[:], AF.Sqrt,
                                            bias=cst[0:1, 1:2]).then_inc(s_ac.h, 1))
        wait("vector", s_ac, a3)
        rv0 = s_ve.res()
        em("vector", lambda v, m=rv0: (v.reciprocal(r2t2[:], r2t1[:]).then_inc(s_ve.h, 1),
                                       v.wait_ge(s_ve.h, m)))
        rv = s_ve.res()
        em("vector", lambda v: v.tensor_mul(r2t1[:], s2um[:], r2t2[:]).then_inc(s_ve.h, 1))
        bcv = s_mm.res()
        def bc3(t, n=rv):
            t.wait_ge(s_ve.h, n)
            t.matmul(st_ps[0][:, 0:C3], ones1[:], r2t2[0:1, :], start=True, stop=True)
            t.matmul(st_ps[1][:, 0:C3], ones1[:], r2t1[0:1, :],
                     start=True, stop=True).then_inc(s_mm.h, 1)
        em("tensor", bc3)
        for ct in range(KT):
            nz30 = s_ve.res()
            nzv = s_ve.res()
            def nz3(v, ct=ct, bcv=bcv, m=nz30):
                v.wait_ge(s_mm.h, bcv)
                sl = slice(ct * C3, (ct + 1) * C3)
                v.tensor_tensor(r2ch[:, sl], r2ch[:, sl], st_ps[0][:, 0:C3],
                                OP.mult).then_inc(s_ve.h, 1)
                v.wait_ge(s_ve.h, m)
                v.tensor_sub(r2ch[:, sl], r2ch[:, sl],
                             st_ps[1][:, 0:C3]).then_inc(s_ve.h, 1)
            em("vector", nz3)
        nz_all = s_ve.n
        ov = s_out.res(16)
        def sto(a, n=nz_all, p_i3=p_i3):
            a.wait_ge(s_ve.h, n)
            a.dma_start(out=out_d[:, p_i3],
                        in_=r2ch[:].rearrange("p (h j) -> p h j", h=KT)).then_inc(s_out.h, 16)
        em("scalar", sto)
        out_sts.append((nz_all, ov))
        return max(sev, nz_all)

    st3_holder = [0]
    prev = None
    for i3 in range(NC3):
        if i3 >= 2:
            wait("sync", s_ve, ev3_done[i3 - 2])
        half = i3 % 2
        dma("sync", y3b[i3 % 2][:].rearrange("p (k j) -> p k j", k=KT),
            y1b_d[:, i3 // 2, :, half * C3:(half + 1) * C3], s_y3[i3 % 2])
        ld_y = dma("sync", y3f[i3 % 2][:].rearrange("p (k j) -> p k j", k=KT),
                   y1f_d[:, i3 // 2, :, half * C3:(half + 1) * C3], s_y3[i3 % 2])
        aguard = wmm_last.get(i3 - 2, 0)
        mmvs = {}
        midv = {}
        c2v = {}
        sinv = {}
        acts = {}
        tct = 0
        for slot in range(FT + 3):
            if slot - 3 >= 0:
                acts[slot - 3] = emit_stage_qv3(i3, slot - 3, sinv[slot - 3])
            if slot < FT:
                mmvs[slot] = emit_fmm(i3, ld_y, slot)
                emit_stage_qv1(i3, slot, mmvs[slot])
            if 0 <= slot - 1 < FT:
                ft = slot - 1
                sg = sinv.get(ft - 3, 0)
                midv[ft] = emit_stage_mid(i3, ft, sg)
            if 0 <= slot - 2 < FT:
                ft = slot - 2
                c2v[ft] = emit_stage_cvt2(i3, ft)
                sinv[ft] = emit_stage_sin(i3, ft, c2v[ft], aguard)
            if prev is not None and tct < KT * 4:
                p_i3, p_ldy, p_act = prev
                pm = emit_tail_piece(p_i3, p_act, tct // 4, tct % 4)
                if tct % 4 == 3:
                    st3_holder[0] = emit_tail_unit(p_i3, p_ldy, p_act,
                                                   tct // 4, pm)
                tct += 1
        if prev is not None:
            while tct < KT * 4:
                p_i3, p_ldy, p_act = prev
                pm = emit_tail_piece(p_i3, p_act, tct // 4, tct % 4)
                if tct % 4 == 3:
                    st3_holder[0] = emit_tail_unit(p_i3, p_ldy, p_act,
                                                   tct // 4, pm)
                tct += 1
            t_ln = emit_ln2_tail(p_i3)
            prev_stev3 = t_ln
        act_last = acts[FT - 1]
        prev = (i3, ld_y, act_last)

    p_i3, p_ldy, p_act = prev
    for ct in range(KT):
        for piece in range(4):
            pm = emit_tail_piece(p_i3, p_act, ct, piece)
        st3_holder[0] = emit_tail_unit(p_i3, p_ldy, p_act, ct, pm)
    emit_ln2_tail(p_i3)

    final_out = s_out.n
    em("scalar", lambda a: a.wait_ge(s_out.h, final_out))
    em("sync", lambda e: e.wait_ge(s_out.h, final_out))

    block = es.enter_context(nc.Block())

    @block.sync
    def _(sync):
        for fn in E["sync"]:
            fn(sync)

    @block.tensor
    def _(tensor):
        for fn in E["tensor"]:
            fn(tensor)

    @block.vector
    def _(vector):
        for fn in E["vector"]:
            fn(vector)

    @block.scalar
    def _(scalar):
        for fn in E["scalar"]:
            fn(scalar)

    @block.gpsimd
    def _(gpsimd):
        for fn in E["gpsimd"]:
            fn(gpsimd)

    g3.close()
    es.close()
    return nc


def kernel(**inputs):
    ins = {k: np.asarray(v, dtype=np.float32) for k, v in inputs.items()}
    x = ins["L0_attn_x"]

    def pm(a, nt):
        X = a.shape[1]
        return np.ascontiguousarray(a.reshape(nt, 128, X).transpose(1, 0, 2).reshape(128, nt * X))

    w1t = pm(np.ascontiguousarray(ins["L0_attn_w1_w"].T), KT)
    v1t = pm(np.ascontiguousarray(ins["L0_attn_v1_w"].T), KT)
    w2t = pm(np.ascontiguousarray(ins["L0_attn_w2_w"].T), KT)
    wf1 = pm(ins["L0_ffn_W1"], KT)
    wf2 = pm(ins["L0_ffn_W2"], FT)

    def col8(a):
        return np.ascontiguousarray(a.reshape(KT, 128).T)

    def col128(a):
        return np.ascontiguousarray(a.reshape(KT, 128, 16).transpose(1, 0, 2).reshape(128, 128))

    cc = np.zeros((128, 448), np.float32)
    cc[:, 0:8] = col8(ins["L0_attn_w1_b"])
    cc[:, 8:16] = col8(ins["L0_attn_v1_b"])
    cc[:, 16:24] = col8(ins["L0_attn_w2_b"])
    cc[:, 24:32] = col8(ins["L0_ffn_b2"])
    cc[:, 32:40] = col8(ins["L0_attn_D_ssm"])
    conv_w = ins["L0_attn_conv_w"][:, 0, :]
    cc[:, 40:48] = col8(conv_w[:, 0])
    cc[:, 48:56] = col8(conv_w[:, 1])
    cc[:, 56:64] = col8(conv_w[:, 2])
    cc[:, 64:192] = col128(ins["L0_attn_A"])
    cc[:, 192:320] = col128(ins["L0_attn_B_ssm"])
    cc[:, 320:448] = col128(ins["L0_attn_C_ssm"])

    def fcol(a):
        return np.ascontiguousarray(a.reshape(FT, 128).T)

    fc = np.zeros((128, 128), np.float32)
    fc[:, 0:32] = fcol(ins["L0_ffn_b1"])
    fc[:, 32:64] = fcol(ins["L0_ffn_quantum_phases"])
    fc[:, 64:96] = fcol(ins["L0_ffn_quantum_weights_real"])
    fc[:, 96:128] = fcol(ins["L0_ffn_quantum_weights_imag"])

    cst = np.zeros((128, 8), np.float32)
    cst[:, 0] = TWO_PI
    cst[:, 1] = 1e-5
    cst[:, 2] = -np.pi
    cst[:, 3] = TWO_PI / 262144.0

    nc = build_program()

    in_maps = []
    for core in range(8):
        xs = x[2 * core:2 * core + 2]
        xF = np.ascontiguousarray(xs.transpose(2, 0, 1)).reshape(D, NT)
        xfc = np.ascontiguousarray(
            xF.reshape(KT, 128, NCH, CH).transpose(1, 2, 0, 3))
        in_maps.append({
            "xf": xfc, "w1t": w1t, "v1t": v1t, "w2t": w2t,
            "wf1": wf1, "wf2": wf2, "cc": cc, "fc": fc, "cst": cst,
        })

    res = run_bass_kernel_spmd(nc, in_maps, core_ids=list(range(8)))

    outs = []
    for core in range(8):
        o = np.asarray(res.results[core]["out"], np.float32)
        oF = o.transpose(2, 0, 1, 3).reshape(D, NT)
        o2 = oF.reshape(D, NSEQ, T).transpose(1, 2, 0)
        outs.append(o2)
    return np.ascontiguousarray(np.concatenate(outs, axis=0))


# version 62
# speedup vs baseline: 1.0241x; 1.0241x over previous
import numpy as np
import ml_dtypes
from contextlib import ExitStack
import concourse.bass as bass
import concourse.mybir as mybir
from concourse.bass_utils import run_bass_kernel_spmd

F32 = mybir.dt.float32
BF16 = mybir.dt.bfloat16
I32 = mybir.dt.int32
AF = mybir.ActivationFunctionType
OP = mybir.AluOpType

D = 1024
T = 2048
NSEQ = 2
NT = NSEQ * T
KT = 8
FT = 32
CH = 512
NCH = NT // CH
C3 = 256
NC3 = NT // C3
TWO_PI = float(2.0 * np.pi)
POOL_S = []


class Sem:
    def __init__(self, es, nc, name):
        self.h = es.enter_context(nc.semaphore(name))
        self.n = 0

    def res(self, amt=1):
        self.n += amt
        return self.n


def build_program():
    nc = bass.Bass("TRN2", target_bir_lowering=False)

    xf_d = nc.dram_tensor("xf", [128, NCH, KT, CH], F32, kind="ExternalInput").ap()
    w1t_d = nc.dram_tensor("w1t", [128, KT * D], F32, kind="ExternalInput").ap()
    v1t_d = nc.dram_tensor("v1t", [128, KT * D], F32, kind="ExternalInput").ap()
    w2t_d = nc.dram_tensor("w2t", [128, KT * D], F32, kind="ExternalInput").ap()
    wf1_d = nc.dram_tensor("wf1", [128, KT * 4096], F32, kind="ExternalInput").ap()
    wf2_d = nc.dram_tensor("wf2", [128, FT * D], F32, kind="ExternalInput").ap()
    cc_d = nc.dram_tensor("cc", [128, 448], F32, kind="ExternalInput").ap()
    fc_d = nc.dram_tensor("fc", [128, 128], F32, kind="ExternalInput").ap()
    cst_d = nc.dram_tensor("cst", [128, 8], F32, kind="ExternalInput").ap()
    out_d = nc.dram_tensor("out", [128, NC3, KT, C3], F32, kind="ExternalOutput").ap()
    xbf_d = nc.dram_tensor("xbf_d", [128, NCH, KT, CH], BF16).ap()
    gb_d = nc.dram_tensor("gb_d", [128, NCH, KT, CH], BF16).ap()
    r1f_d = nc.dram_tensor("r1f", [128, NCH, KT, CH], F32).ap()
    y1f_d = nc.dram_tensor("y1f", [128, NCH, KT, CH], F32).ap()
    y1b_d = nc.dram_tensor("y1bd", [128, NCH, KT, CH], BF16).ap()

    es = ExitStack()
    g1 = ExitStack()
    gv = ExitStack()
    gx = ExitStack()
    g2 = ExitStack()
    g2b = ExitStack()
    g2be = ExitStack()
    g2d = ExitStack()
    g3 = ExitStack()

    def sbg(g, name, shape, dt=F32):
        return g.enter_context(nc.sbuf_tensor(name, shape, dt))

    cc = sbg(es, "cc_s", [128, 64])
    fc = sbg(es, "fc_s", [128, 128])
    cst = sbg(es, "cst_s", [128, 8])
    cw = sbg(es, "cw", [128, 32])
    alp = sbg(es, "alp", [128, 32])
    beta = sbg(es, "beta", [128, 32])
    ones_bf = sbg(es, "ones_bf", [128, 1], BF16)
    ones_f = sbg(es, "ones_f", [128, 1])
    ones1 = sbg(es, "ones1", [1, 128])
    cabc = sbg(gv, "cabc", [128, 384])
    abar = sbg(gv, "abar", [128, 128])
    gg = sbg(gv, "gg", [128, 128])
    tfc1 = sbg(gv, "tfc1", [128, 32])
    tfc2 = sbg(gv, "tfc2", [128, 32])
    tfc3 = sbg(gv, "tfc3", [128, 32])
    tfci = sbg(gv, "tfci", [128, 32])
    tfcf = sbg(gv, "tfcf", [128, 32])

    v1tb = sbg(gv, "v1tb", [128, KT * D], BF16)
    xmix = sbg(gx, "xmix", [128, KT * NT], BF16)
    w1tb = sbg(g1, "w1tb", [128, KT * D], BF16)
    xstg = sbg(g1, "xstg", [128, KT * CH])
    xbfst = [sbg(g1, f"xbfst{i}", [128, KT * CH], BF16) for i in range(2)]
    convb = sbg(g1, "convb", [128, T])

    mm_ps = [es.enter_context(nc.psum_tensor(f"mmps{i}", [128, 512], F32)) for i in range(4)]
    st_ps = [es.enter_context(nc.psum_tensor(f"stps{i}", [128, 512], F32)) for i in range(2)]
    bc_ps = [es.enter_context(nc.psum_tensor(f"bcps{i}", [128, 512], F32)) for i in range(2)]

    import os
    if os.environ.get("KDBG3"):
        DBG3 = [nc.dram_tensor("dbg3r", [128, KT * C3], F32, kind="ExternalOutput").ap(),
                nc.dram_tensor("dbg3s", [1, 2 * C3], F32, kind="ExternalOutput").ap(),
                nc.dram_tensor("dbg3a", [128, FT * C3], BF16, kind="ExternalOutput").ap(),
                nc.dram_tensor("dbg3y", [128, KT * C3], BF16, kind="ExternalOutput").ap()]
    else:
        DBG3 = None
    s_ld = Sem(es, nc, "s_ld")
    s_xstg = Sem(es, nc, "s_xstg")
    s_xbv = [Sem(es, nc, f"s_xbv{i}") for i in range(2)]
    s_stg2s = [Sem(es, nc, f"s_stg2s{i}") for i in range(2)]
    s_gbc = [Sem(es, nc, f"s_gbc{i}") for i in range(2)]
    s_r1l = [Sem(es, nc, f"s_r1l{i}") for i in range(2)]
    s_y3 = [Sem(es, nc, f"s_y3{i}") for i in range(2)]
    s_wst = [Sem(es, nc, f"s_wst{i}") for i in range(2)]
    s_stxbf = [Sem(es, nc, f"s_stxbf{i}") for i in range(2)]
    s_stgb = [Sem(es, nc, f"s_stgb{i}") for i in range(2)]
    s_sty1 = [Sem(es, nc, f"s_sty1{i}") for i in range(2)]
    s_str1 = Sem(es, nc, "s_str1")
    s_st = Sem(es, nc, "s_st")
    s_out = Sem(es, nc, "s_out")
    s_mm = Sem(es, nc, "s_mm")
    s_ve = Sem(es, nc, "s_ve")
    s_ac = Sem(es, nc, "s_ac")
    s_po = Sem(es, nc, "s_po")

    E = {k: [] for k in ("sync", "tensor", "vector", "scalar", "gpsimd")}

    def em(eng, fn):
        E[eng].append(fn)

    def wait(eng, sem, val):
        if val > 0:
            em(eng, lambda e, s=sem.h, v=val: e.wait_ge(s, v))

    def dma(eng, out, in_, sem):
        v = sem.res(16)
        em(eng, lambda e, o=out, i=in_, s=sem.h: e.dma_start(out=o, in_=i).then_inc(s, 16))
        return v

    dma("sync", cc[:], cc_d[:, 0:64], s_ld)
    dma("sync", cabc[:], cc_d[:, 64:448], s_ld)
    dma("sync", fc[:], fc_d, s_ld)
    ld_consts = dma("sync", cst[:], cst_d, s_ld)

    ve_wconv = 0
    for wd, wb in ((w1t_d, w1tb), (v1t_d, v1tb)):
        for piece in range(2):
            ldv = dma("sync", xstg[:],
                      wd[:, piece * 4 * D:(piece + 1) * 4 * D], s_xstg)
            wait("vector", s_xstg, ldv)
            cv = s_ve.res()
            em("vector", lambda v, dst=wb, piece=piece: v.tensor_copy(
                dst[:, piece * 4 * D:(piece + 1) * 4 * D], xstg[:]).then_inc(s_ve.h, 1))
            wait("sync", s_ve, cv)
            ve_wconv = cv

    wait("scalar", s_ld, ld_consts)
    wait("vector", s_ld, ld_consts)
    em("scalar", lambda a: a.activation(abar[:], cabc[:, 0:128], AF.Sigmoid).then_inc(s_ac.h, 1))
    s_ac.res()

    def emv(fn):
        v0 = s_ve.res()
        em("vector", lambda e, f=fn: f(e).then_inc(s_ve.h, 1))
        wait("vector", s_ve, v0)
        return v0

    emv(lambda v: v.tensor_mul(gg[:], cabc[:, 128:256], cabc[:, 256:384]))

    TH = fc[:, 32:64]
    WR = fc[:, 64:96]
    WI = fc[:, 96:128]
    B1C = fc[:, 0:32]

    def trig(dst, shift):
        emv(lambda v, sh=shift: v.tensor_scalar(
            tfc3[:], TH, float(262144.0 / TWO_PI),
            float((64.5 + sh / TWO_PI) * 262144.0), OP.mult, OP.add))
        emv(lambda v: v.tensor_copy(tfci[:].bitcast(I32), tfc3[:]))
        emv(lambda v: v.tensor_scalar(tfci[:].bitcast(I32), tfci[:].bitcast(I32),
                                      262143, None, OP.bitwise_and))
        dv = emv(lambda v: v.tensor_copy(tfcf[:], tfci[:].bitcast(I32)))
        wait("scalar", s_ve, dv)
        av = s_ac.res()
        em("scalar", lambda a, d2=dst: a.activation(
            d2, tfcf[:], AF.Sin, scale=cst[:, 3:4],
            bias=cst[:, 2:3]).then_inc(s_ac.h, 1))
        wait("vector", s_ac, av)
        return av

    trig(tfc1[:], 0.0)
    actrig = trig(tfc2[:], float(np.pi / 2.0))
    wait("vector", s_ac, actrig)
    emv(lambda v: v.tensor_mul(tfc3[:], tfc2[:], WR))
    emv(lambda v: v.tensor_mul(tfc1[:], tfc1[:], WI))
    emv(lambda v: v.tensor_add(tfc3[:], tfc3[:], tfc1[:]))
    emv(lambda v: v.tensor_mul(tfc3[:], tfc3[:], WI))
    emv(lambda v: v.tensor_scalar_mul(alp[:], tfc3[:], 0.1))
    emv(lambda v: v.tensor_scalar(tfc2[:], TH, float(262144.0 / TWO_PI),
                                  float(64.5 * 262144.0), OP.mult, OP.add))
    emv(lambda v: v.scalar_tensor_tensor(
        cw[:], B1C, float(0.1 * 262144.0 / TWO_PI), tfc2[:], OP.mult, OP.add))
    emv(lambda v: v.scalar_tensor_tensor(
        beta[:], B1C, float(-0.1 * 262144.0 / TWO_PI), cw[:], OP.mult, OP.add))
    emv(lambda v: v.tensor_scalar(beta[:], beta[:],
                                  float(-TWO_PI / (0.1 * 262144.0)), None, OP.mult))
    emv(lambda v: v.memset(ones_bf[:], 1.0))
    emv(lambda v: v.memset(ones_f[:], 1.0))
    emv(lambda v: v.memset(ones1[:], 1.0))

    bank_free = {b: (None, 0) for b in range(4)}
    xcvt = []
    xbf_store = []
    x1_evac = {}
    for ch in range(NCH):
        if ch >= 1:
            wait("sync", s_po, xcvt[ch - 1])
        else:
            wait("sync", s_ve, ve_wconv)
        ldv = dma("sync", xstg[:], xf_d[:, ch], s_xstg)
        if ch >= 2:
            wait("gpsimd", s_ac, x1_evac[(KT - 1, ch - 2)])
            wait("gpsimd", s_stxbf[ch % 2], xbf_store[ch - 2])
        wait("gpsimd", s_xstg, ldv)
        pv = s_po.res()
        em("gpsimd", lambda p, ch=ch: p.tensor_copy(
            xbfst[ch % 2][:], xstg[:]).then_inc(s_po.h, 1))
        xcvt.append(pv)
        wait("sync", s_po, pv)
        sv = dma("sync", xbf_d[:, ch], xbfst[ch % 2][:].rearrange("p (k j) -> p k j", k=KT),
                 s_stxbf[ch % 2])
        xbf_store.append(sv)

        for ct in range(KT):
            bank = ct % 2
            sem, val = bank_free[bank]
            mmv = s_mm.res()
            def f1(t, bank=bank, ct=ct, ch=ch, sem=sem, val=val,
                   pv=pv, wcv=(ve_wconv if (ch == 0 and ct == 0) else 0)):
                if ct == 0:
                    t.wait_ge(s_po.h, pv)
                if wcv:
                    t.wait_ge(s_ve.h, wcv)
                if sem is not None:
                    t.wait_ge(sem, val)
                for k in range(KT):
                    ins = t.matmul(mm_ps[bank][:, 0:CH],
                                   w1tb[:, k * D + ct * 128:k * D + (ct + 1) * 128],
                                   xbfst[ch % 2][:, k * CH:(k + 1) * CH],
                                   start=(k == 0), stop=(k == KT - 1))
                    if k == KT - 1:
                        ins.then_inc(s_mm.h, 1)
            em("tensor", f1)
            av = s_ac.res()
            def ev(a, mmv=mmv, bank=bank, ct=ct, ch=ch):
                a.wait_ge(s_mm.h, mmv)
                a.add(xmix[:, ct * NT + ch * CH:ct * NT + (ch + 1) * CH],
                      mm_ps[bank][:, 0:CH], cc[:, ct:ct + 1]).then_inc(s_ac.h, 1)
            em("scalar", ev)
            bank_free[bank] = (s_ac.h, av)
            x1_evac[(ct, ch)] = av
    mm_phase1 = s_mm.n

    xc_done_ct = {}
    for ct in range(KT):
        for seq in range(NSEQ):
            need = x1_evac[(ct, (seq + 1) * (NCH // NSEQ) - 1)]
            cv0 = s_ve.res()
            cv1 = s_ve.res()
            cv = s_ve.res()
            def cvf(v, need=need, seq=seq, ct=ct, m0=cv0, m1=cv1):
                base = ct * NT + seq * T
                v.wait_ge(s_ac.h, need)
                v.tensor_scalar(convb[:], xmix[:, base:base + T],
                                cc[:, 48 + ct:49 + ct], None, OP.mult).then_inc(s_ve.h, 1)
                v.wait_ge(s_ve.h, m0)
                v.scalar_tensor_tensor(convb[:, 1:T], xmix[:, base:base + T - 1],
                                       cc[:, 40 + ct:41 + ct], convb[:, 1:T],
                                       OP.mult, OP.add).then_inc(s_ve.h, 1)
                v.wait_ge(s_ve.h, m1)
                v.scalar_tensor_tensor(convb[:, 0:T - 1], xmix[:, base + 1:base + T],
                                       cc[:, 56 + ct:57 + ct], convb[:, 0:T - 1],
                                       OP.mult, OP.add).then_inc(s_ve.h, 1)
            em("vector", cvf)
            av = s_ac.res()
            def slf(a, cv=cv, seq=seq, ct=ct):
                a.wait_ge(s_ve.h, cv)
                a.activation(xmix[:, ct * NT + seq * T:ct * NT + (seq + 1) * T],
                             convb[:], AF.Silu).then_inc(s_ac.h, 1)
            em("scalar", slf)
            wait("vector", s_ac, av)
        xc_done_ct[ct] = s_ac.n
    g1.close()

    hbuf = sbg(g2, "hbuf", [128, T])
    phbuf = [sbg(g2, f"phbuf{i}", [128, T]) for i in range(2)]
    ydve = sbg(g2, "ydve", [128, T])
    ypool = sbg(g2, "ypool", [128, T])
    ptmp = sbg(g2, "ptmp", [128, T])
    xbfv = [sbg(g2, f"xbfv{i}", [128, KT * CH], BF16) for i in range(2)]
    vtile = [sbg(g2, f"vtile{i}", [128, NT], BF16) for i in range(2)]

    wait("vector", s_mm, mm_phase1)
    wait("gpsimd", s_mm, mm_phase1)
    wait("tensor", s_ac, xc_done_ct[KT - 1])

    pb_guard = [0, 0]
    pool_cnt = 0
    v_done_ct = {}
    gb_store = []
    for ct in range(KT):
        for ch in range(NCH):
            gi = (ct * NCH + ch) % 2
            if ct * NCH + ch >= 2:
                prev = v_done_ct.get((ct * NCH + ch - 2))
                wait("sync", s_ac, prev)
            else:
                wait("sync", s_ac, xc_done_ct[KT - 1])
            if ct == 0 and ch == 0:
                wait("sync", s_stxbf[0], s_stxbf[0].n)
                wait("sync", s_stxbf[1], s_stxbf[1].n)
            ldv = dma("sync", xbfv[gi][:].rearrange("p (k j) -> p k j", k=KT),
                      xbf_d[:, ch], s_xbv[gi])
            bank = ch % 2
            sem, val = bank_free[bank]
            mmv = s_mm.res()
            def fv(t, bank=bank, ct=ct, ch=ch, gi=gi, sem=sem, val=val, ldv=ldv):
                t.wait_ge(s_xbv[gi].h, ldv)
                if sem is not None:
                    t.wait_ge(sem, val)
                for k in range(KT):
                    ins = t.matmul(mm_ps[bank][:, 0:CH],
                                   v1tb[:, k * D + ct * 128:k * D + (ct + 1) * 128],
                                   xbfv[gi][:, k * CH:(k + 1) * CH],
                                   start=(k == 0), stop=(k == KT - 1))
                    if k == KT - 1:
                        ins.then_inc(s_mm.h, 1)
            em("tensor", fv)
            av = s_ac.res()
            def evv(a, mmv=mmv, bank=bank, ct=ct, ch=ch,
                    stg=(gb_store[ct - 2] if (ct >= 2 and ch == 0) else 0)):
                a.wait_ge(s_mm.h, mmv)
                if stg:
                    a.wait_ge(s_stgb[ct % 2].h, stg)
                a.activation(vtile[ct % 2][:, ch * CH:(ch + 1) * CH],
                             mm_ps[bank][:, 0:CH], AF.Silu,
                             bias=cc[:, 8 + ct:9 + ct]).then_inc(s_ac.h, 1)
            em("scalar", evv)
            bank_free[bank] = (s_ac.h, av)
            v_done_ct[ct * NCH + ch] = av
        v_ready = v_done_ct[ct * NCH + NCH - 1]

        wait("vector", s_ac, xc_done_ct[ct])
        wait("gpsimd", s_ac, xc_done_ct[ct])
        for seq in range(NSEQ):
            src = xmix[:, ct * NT + seq * T:ct * NT + (seq + 1) * T]
            first_dve = True
            first_pool = True
            for s in range(16):
                col = ct * 16 + s
                if s not in POOL_S:
                    sc0 = s_ve.res()
                    em("vector", lambda v, col=col, src=src, m=sc0: (
                        v.tensor_tensor_scan(
                            hbuf[:], abar[:, col:col + 1].broadcast_to([128, T]), src,
                            0.0, OP.mult, OP.add).then_inc(s_ve.h, 1),
                        v.wait_ge(s_ve.h, m)))
                    if first_dve:
                        em("vector", lambda v, col=col: v.tensor_scalar(
                            ydve[:], hbuf[:], gg[:, col:col + 1], None, OP.mult))
                        first_dve = False
                    else:
                        em("vector", lambda v, col=col: v.scalar_tensor_tensor(
                            ydve[:], hbuf[:], gg[:, col:col + 1], ydve[:], OP.mult, OP.add))
                else:
                    pi = pool_cnt % 2
                    pool_cnt += 1
                    if pb_guard[pi]:
                        wait("vector", s_po, pb_guard[pi])
                    sv = s_ve.res()
                    em("vector", lambda v, col=col, src=src, pi=pi: v.tensor_tensor_scan(
                        phbuf[pi][:], abar[:, col:col + 1].broadcast_to([128, T]), src,
                        0.0, OP.mult, OP.add).then_inc(s_ve.h, 1))
                    pv = s_po.res()
                    if first_pool:
                        def pf(p, n=sv, pi=pi, col=col):
                            p.wait_ge(s_ve.h, n)
                            p.tensor_tensor(ypool[:], phbuf[pi][:],
                                            gg[:, col:col + 1].broadcast_to([128, T]),
                                            OP.mult).then_inc(s_po.h, 1)
                        first_pool = False
                    else:
                        def pf(p, n=sv, pi=pi, col=col):
                            p.wait_ge(s_ve.h, n)
                            p.tensor_tensor(ptmp[:], phbuf[pi][:],
                                            gg[:, col:col + 1].broadcast_to([128, T]), OP.mult)
                            p.tensor_tensor(ypool[:], ypool[:], ptmp[:],
                                            OP.add).then_inc(s_po.h, 1)
                    em("gpsimd", pf)
                    pb_guard[pi] = pv
            dt0 = s_ve.res()
            em("vector", lambda v, ct=ct, src=src, m=dt0: (
                v.scalar_tensor_tensor(
                    ydve[:], src, cc[:, 32 + ct:33 + ct], ydve[:],
                    OP.mult, OP.add).then_inc(s_ve.h, 1),
                v.wait_ge(s_ve.h, m)))
            if POOL_S:
                wait("vector", s_po, s_po.n)
                dt1 = s_ve.res()
                em("vector", lambda v, m=dt1: (
                    v.tensor_add(ydve[:], ydve[:], ypool[:]).then_inc(s_ve.h, 1),
                    v.wait_ge(s_ve.h, m)))
            wait("vector", s_ac, v_ready)
            gvv = s_ve.res()
            em("vector", lambda v, ct=ct, seq=seq: v.tensor_tensor(
                vtile[ct % 2][:, seq * T:(seq + 1) * T], ydve[:],
                vtile[ct % 2][:, seq * T:(seq + 1) * T],
                OP.mult).then_inc(s_ve.h, 1))
            wait("gpsimd", s_ve, gvv)
        gate_ct = s_ve.n
        wait("sync", s_ve, gate_ct)
        sv2 = dma("sync", gb_d[:, :, ct, :],
                  vtile[ct % 2][:].rearrange("p (h j) -> p h j", h=NCH), s_stgb[ct % 2])
        gb_store.append(sv2)
        if ct >= 1:
            pass
    gate_all = s_ve.n
    g2.close()
    gx.close()
    gv.close()

    ssum = sbg(g2b, "ssum", [1, NT])
    ssq = sbg(g2b, "ssq", [1, NT])
    rtmp1 = sbg(g2b, "rtmp1", [1, NT])
    istd_row = sbg(g2b, "istd_row", [1, NT])
    mi_row = sbg(g2b, "mi_row", [1, NT])
    r1ch = [sbg(g2b, f"r1ch{i}", [128, KT * CH]) for i in range(2)]
    r1bch = [sbg(g2b, f"r1bch{i}", [128, KT * CH], BF16) for i in range(2)]
    w2tb = sbg(g2be, "w2tb", [128, KT * D], BF16)
    stg2 = [sbg(g2be, f"stg2{i}", [128, KT * CH]) for i in range(2)]
    gbch = [sbg(g2be, f"gbch{i}", [128, KT * CH], BF16) for i in range(2)]
    sqch = sbg(g2be, "sqch", [128, KT * CH], BF16)

    for par in range(2):
        wait("vector", s_stgb[par], s_stgb[par].n)
        wait("gpsimd", s_stgb[par], s_stgb[par].n)
        wait("sync", s_stgb[par], s_stgb[par].n)
    w2cv = 0
    for piece in range(2):
        ldv = dma("sync", stg2[piece][:, 0:4 * D],
                  w2t_d[:, piece * 4 * D:(piece + 1) * 4 * D], s_stg2s[piece])
        wait("vector", s_stg2s[piece], ldv)
        cv = s_ve.res()
        em("vector", lambda v, piece=piece: v.tensor_copy(
            w2tb[:, piece * 4 * D:(piece + 1) * 4 * D],
            stg2[piece][:, 0:4 * D]).then_inc(s_ve.h, 1))
        w2cv = cv

    bank_free = {b: (None, 0) for b in range(4)}
    wait("tensor", s_ve, w2cv)
    stat_evac = []
    r1_store = []
    gb_loads = []
    for ch in range(NCH):
        if ch >= 2:
            wait("sync", s_str1, r1_store[ch - 2])
        else:
            wait("sync", s_ve, w2cv)
        ld_x = dma("sync", stg2[ch % 2][:], xf_d[:, ch], s_stg2s[ch % 2])
        if ch >= 2:
            wait("sync", s_ve, gb_loads[ch - 2])
        ld_g = dma("sync", gbch[ch % 2][:], gb_d[:, ch], s_gbc[ch % 2])
        for ct in range(KT):
            bank = ct % 2
            sem, val = bank_free[bank]
            mmv = s_mm.res()
            def f(t, bank=bank, ct=ct, ch=ch, sem=sem, val=val, ld_g=ld_g):
                if ct == 0:
                    t.wait_ge(s_gbc[ch % 2].h, ld_g)
                if sem is not None:
                    t.wait_ge(sem, val)
                for k in range(KT):
                    ins = t.matmul(mm_ps[bank][:, 0:CH],
                                   w2tb[:, k * D + ct * 128:k * D + (ct + 1) * 128],
                                   gbch[ch % 2][:, k * CH:(k + 1) * CH],
                                   start=(k == 0), stop=(k == KT - 1))
                    if k == KT - 1:
                        ins.then_inc(s_mm.h, 1)
            em("tensor", f)
            vv = s_ve.res()
            def ev(v, mmv=mmv, bank=bank, ct=ct, ch=ch, ld_x=ld_x):
                v.wait_ge(s_mm.h, mmv)
                v.wait_ge(s_stg2s[ch % 2].h, ld_x)
                v.scalar_tensor_tensor(
                    r1ch[ch % 2][:, ct * CH:(ct + 1) * CH],
                    mm_ps[bank][:, 0:CH], cc[:, 16 + ct:17 + ct],
                    stg2[ch % 2][:, ct * CH:(ct + 1) * CH],
                    OP.add, OP.add).then_inc(s_ve.h, 1)
            em("vector", ev)
            bank_free[bank] = (s_ve.h, vv)
            bv = s_ve.res()
            em("vector", lambda v, ct=ct, ch=ch: v.tensor_copy(
                r1bch[ch % 2][:, ct * CH:(ct + 1) * CH],
                r1ch[ch % 2][:, ct * CH:(ct + 1) * CH]).then_inc(s_ve.h, 1))
            av = s_ac.res()
            em("scalar", lambda a, n=bv, ct=ct, ch=ch: (
                a.wait_ge(s_ve.h, n),
                a.activation(sqch[:, ct * CH:(ct + 1) * CH],
                             r1bch[ch % 2][:, ct * CH:(ct + 1) * CH],
                             AF.Square).then_inc(s_ac.h, 1)))
            sq_last = av
        gb_loads.append(s_ve.n)
        stv = s_mm.res()
        def stmm_b(t, ch=ch, av=sq_last,
                   prev=(stat_evac[ch - 1] if ch > 0 else 0)):
            if prev:
                t.wait_ge(s_ve.h, prev)
            t.wait_ge(s_ac.h, av)
            for ct2 in range(KT):
                t.matmul(st_ps[0][0:1, 0:CH], ones_bf[:],
                         r1bch[ch % 2][:, ct2 * CH:(ct2 + 1) * CH],
                         start=(ct2 == 0), stop=(ct2 == KT - 1))
                ins = t.matmul(st_ps[1][0:1, 0:CH], ones_bf[:],
                               sqch[:, ct2 * CH:(ct2 + 1) * CH],
                               start=(ct2 == 0), stop=(ct2 == KT - 1))
                if ct2 == KT - 1:
                    ins.then_inc(s_mm.h, 1)
        em("tensor", stmm_b)
        st_done = stv
        sev = s_ve.res()
        def stev(v, ch=ch, n=st_done):
            v.wait_ge(s_mm.h, n)
            v.tensor_copy(ssum[0:1, ch * CH:(ch + 1) * CH], st_ps[0][0:1, 0:CH])
            v.tensor_copy(ssq[0:1, ch * CH:(ch + 1) * CH],
                          st_ps[1][0:1, 0:CH]).then_inc(s_ve.h, 1)
        em("vector", stev)
        stat_evac.append(sev)
        wait("sync", s_ve, sev)
        stv2 = dma("sync", r1f_d[:, ch],
                   r1ch[ch % 2][:].rearrange("p (k j) -> p k j", k=KT), s_str1)
        r1_store.append(stv2)

    wait("vector", s_ve, stat_evac[-1])
    r0 = s_ve.res()
    em("vector", lambda v, m=r0: (v.tensor_scalar(ssum[:], ssum[:], float(1.0 / D),
                                                  None, OP.mult).then_inc(s_ve.h, 1),
                                  v.wait_ge(s_ve.h, m)))
    r1v = s_ve.res()
    em("vector", lambda v, m=r1v: (v.tensor_mul(rtmp1[:], ssum[:], ssum[:]).then_inc(s_ve.h, 1),
                                   v.wait_ge(s_ve.h, m)))
    r2v = s_ve.res()
    em("vector", lambda v, m=r2v: (v.tensor_scalar(ssq[:], ssq[:], float(1.0 / D),
                                                   None, OP.mult).then_inc(s_ve.h, 1),
                                   v.wait_ge(s_ve.h, m)))
    varv = s_ve.res()
    em("vector", lambda v, m=varv: (v.tensor_sub(ssq[:], ssq[:], rtmp1[:]).then_inc(s_ve.h, 1),
                                    v.wait_ge(s_ve.h, m)))
    wait("scalar", s_ve, varv)
    sqv = s_ac.res()
    em("scalar", lambda a: a.activation(rtmp1[:], ssq[:], AF.Sqrt,
                                        bias=cst[0:1, 1:2]).then_inc(s_ac.h, 1))
    wait("vector", s_ac, sqv)
    r3v = s_ve.res()
    em("vector", lambda v, m=r3v: (v.reciprocal(istd_row[:], rtmp1[:]).then_inc(s_ve.h, 1),
                                   v.wait_ge(s_ve.h, m)))
    ln1_rows = s_ve.res()
    em("vector", lambda v: v.tensor_mul(mi_row[:], ssum[:], istd_row[:]).then_inc(s_ve.h, 1))

    g2be.close()
    y1ch = [sbg(g2d, f"y1ch{i}", [128, KT * CH]) for i in range(2)]
    y1bch = [sbg(g2d, f"y1bch{i}", [128, KT * CH], BF16) for i in range(2)]

    wait("tensor", s_ve, ln1_rows)
    y1_store = []
    for ch in range(NCH):
        if ch < 2:
            wait("sync", s_str1, r1_store[-1])
        else:
            wait("sync", s_sty1[ch % 2], y1_store[ch - 2])
        ld_r1 = dma("sync", r1ch[ch % 2][:].rearrange("p (h j) -> p h j", h=KT),
                    r1f_d[:, ch], s_r1l[ch % 2])
        bcv = s_mm.res()
        def bc(t, ch=ch):
            t.matmul(bc_ps[0][:, 0:CH], ones1[:], istd_row[0:1, ch * CH:(ch + 1) * CH],
                     start=True, stop=True)
            t.matmul(bc_ps[1][:, 0:CH], ones1[:], mi_row[0:1, ch * CH:(ch + 1) * CH],
                     start=True, stop=True).then_inc(s_mm.h, 1)
        em("tensor", bc)
        for ct in range(KT):
            nz0 = s_ve.res()
            nzv = s_ve.res()
            def nz(v, ct=ct, ch=ch, bcv=bcv, ld_r1=ld_r1, m=nz0):
                v.wait_ge(s_mm.h, bcv)
                v.wait_ge(s_r1l[ch % 2].h, ld_r1)
                sl = slice(ct * CH, (ct + 1) * CH)
                v.tensor_tensor(y1ch[ch % 2][:, sl], r1ch[ch % 2][:, sl],
                                bc_ps[0][:, 0:CH], OP.mult).then_inc(s_ve.h, 1)
                v.wait_ge(s_ve.h, m)
                v.tensor_sub(y1ch[ch % 2][:, sl], y1ch[ch % 2][:, sl],
                             bc_ps[1][:, 0:CH]).then_inc(s_ve.h, 1)
            em("vector", nz)
            pv = s_ve.res()
            em("vector", lambda v, ct=ct, ch=ch: v.tensor_copy(
                y1bch[ch % 2][:, ct * CH:(ct + 1) * CH],
                y1ch[ch % 2][:, ct * CH:(ct + 1) * CH]).then_inc(s_ve.h, 1))
        wait("tensor", s_ve, s_ve.n)
        wait("sync", s_ve, s_ve.n)
        dma("sync", y1f_d[:, ch], y1ch[ch % 2][:].rearrange("p (h j) -> p h j", h=KT),
            s_sty1[ch % 2])
        wait("sync", s_ve, s_ve.n)
        yv = dma("sync", y1b_d[:, ch], y1bch[ch % 2][:].rearrange("p (h j) -> p h j", h=KT),
                 s_sty1[ch % 2])
        y1_store.append(yv)
    g2d.close()
    g2b.close()

    g3w = ExitStack()
    w1fb = sbg(g3, "w1fb", [128, KT * 4096], BF16)
    w2fb = sbg(g3, "w2fb", [128, FT * D], BF16)
    y3b = [sbg(g3, f"y3b{i}", [128, KT * C3], BF16) for i in range(2)]
    y3f = [sbg(g3, f"y3f{i}", [128, KT * C3]) for i in range(2)]
    actb = [sbg(g3, f"actb{i}", [128, FT * C3], BF16) for i in range(2)]
    wq = [sbg(g3, f"wq{i}", [128, C3]) for i in range(3)]
    s1q = [sbg(g3, f"s1q{i}", [128, C3]) for i in range(3)]
    r2ch = sbg(g3, "r2ch", [128, KT * C3])
    sq3f = sbg(g3, "sq3f", [128, KT * C3], BF16)
    wstg3 = [sbg(g3w, f"wstg3{i}", [128, 256]) for i in range(4)]

    wait("sync", s_sty1[0], s_sty1[0].n)
    wait("sync", s_sty1[1], s_sty1[1].n)
    wf_ready = []
    wf1_cv = {}
    wf2_cv = {}
    idx = 0
    pieces = []
    for f2 in range(16):
        for k in range(KT):
            pieces.append(("w1", f2, k * 4096 + f2 * 256))
    for c2 in range(4):
        for f in range(FT):
            pieces.append(("w2", c2, f * D + c2 * 256))
    for idx, (kind, grp, col) in enumerate(pieces):
        buf = wstg3[idx % 4]
        if idx >= 4:
            wait("sync", s_po, wf_ready[idx - 4])
        src_d = wf1_d if kind == "w1" else wf2_d
        dst = w1fb if kind == "w1" else w2fb
        ldv = dma("sync", buf[:], src_d[:, col:col + 256], s_wst[idx % 2])
        wait("gpsimd", s_wst[idx % 2], ldv)

        if idx == 0:
            wait("gpsimd", s_sty1[0], s_sty1[0].n)
            wait("gpsimd", s_sty1[1], s_sty1[1].n)
        cv = s_po.res()
        em("gpsimd", lambda p, col=col, buf=buf, dst=dst: p.tensor_copy(
            dst[:, col:col + 256], buf[:]).then_inc(s_po.h, 1))
        wf_ready.append(cv)
        if kind == "w1" and idx % KT == KT - 1:
            wf1_cv[grp] = cv
        if kind == "w2" and (idx - 128) % FT == FT - 1:
            wf2_cv[grp] = cv
    stream_done = wf_ready[-1]
    w2_all_cv = 0
    w1_all_cv = 0
    g3w.close()
    s2um = sbg(g3, "s2um", [1, C3])
    r2t1 = sbg(g3, "r2t1", [1, C3])
    r2t2 = sbg(g3, "r2t2", [1, C3])

    out_sts = []
    hbanks = [mm_ps[0], mm_ps[1], bc_ps[0], bc_ps[1]]
    hid_bank_free = [(None, 0)] * 4
    w2banks = [mm_ps[2], mm_ps[3]]
    w2bank_free = [(None, 0)] * 2
    prev_stev3 = 0
    wmm_last = {}
    ev3_done = {}

    def emit_fmm(i3, ld_y, ft):
        bank = ft % 4
        sem, val = hid_bank_free[bank]
        mmv = s_mm.res()
        def fmm(t, bank=bank, ft=ft, i3=i3, sem=sem, val=val, ld_y=ld_y,
                wcv=(wf1_cv[ft // 2] if i3 == 0 else 0)):
            if ft < 2:
                t.wait_ge(s_y3[i3 % 2].h, ld_y)
            if wcv:
                t.wait_ge(s_po.h, wcv)
            if sem is not None:
                t.wait_ge(sem, val)
            for k in range(KT):
                ins = t.matmul(hbanks[bank][:, 0:C3],
                               w1fb[:, k * 4096 + ft * 128:k * 4096 + (ft + 1) * 128],
                               y3b[i3 % 2][:, k * C3:(k + 1) * C3],
                               start=(k == 0), stop=(k == KT - 1))
                if k == KT - 1:
                    ins.then_inc(s_mm.h, 1)
        em("tensor", fmm)
        return mmv

    def emit_stage_qv1(i3, ft, mmv):
        bank = ft % 4
        qi = ft % 3
        wv = s_ve.res()
        def qv1(v, n=mmv, ft=ft, qi=qi, bank=bank):
            v.wait_ge(s_mm.h, n)
            v.tensor_scalar(wq[qi][:], hbanks[bank][:, 0:C3],
                            float(0.1 * 262144.0 / TWO_PI),
                            cw[:, ft:ft + 1], OP.mult, OP.add).then_inc(s_ve.h, 1)
        em("vector", qv1)
        return wv

    def emit_stage_mid(i3, ft, sin_guard):
        qi = ft % 3
        def mid(v, ft=ft, qi=qi, sg=sin_guard):
            if sg:
                v.wait_ge(s_ac.h, sg)
            v.tensor_copy(s1q[qi][:].bitcast(I32), wq[qi][:])
            v.tensor_scalar(s1q[qi][:].bitcast(I32), s1q[qi][:].bitcast(I32),
                            262143, None, OP.bitwise_and).then_inc(s_ve.h, 1)
        em("vector", mid)
        return s_ve.res()

    def emit_stage_cvt2(i3, ft):
        def c2(v, qi=ft % 3):
            v.tensor_copy(s1q[qi][:], s1q[qi][:].bitcast(I32)).then_inc(s_ve.h, 1)
        em("vector", c2)
        return s_ve.res()

    def emit_stage_sin(i3, ft, c2v, aguard):
        av = s_ac.res()
        def qa(a, n=c2v, qi=ft % 3, ft=ft, i3=i3, ag=aguard):
            a.wait_ge(s_ve.h, n)
            if ag:
                a.wait_ge(s_mm.h, ag)
            a.activation(actb[i3 % 2][:, ft * C3:(ft + 1) * C3], s1q[qi][:],
                         AF.Sin, scale=cst[:, 3:4],
                         bias=cst[:, 2:3]).then_inc(s_ac.h, 1)
        em("scalar", qa)
        return av

    def emit_stage_qv3(i3, ft, av):
        bank = ft % 4
        a2v0 = s_ve.res()
        a2v = s_ve.res()
        def qv3(v, n=av, bank=bank, ft=ft, i3=i3, m=a2v0):
            v.wait_ge(s_ac.h, n)
            asl = actb[i3 % 2][:, ft * C3:(ft + 1) * C3]
            v.scalar_tensor_tensor(asl, asl, alp[:, ft:ft + 1],
                                   hbanks[bank][:, 0:C3],
                                   OP.mult, OP.add).then_inc(s_ve.h, 1)
            v.wait_ge(s_ve.h, m)
            v.tensor_scalar(asl, asl, fc[:, ft:ft + 1], 0.0,
                            OP.add, OP.max).then_inc(s_ve.h, 1)
        em("vector", qv3)
        hid_bank_free[bank] = (s_ve.h, a2v)
        return a2v

    def emit_tail_piece(p_i3, p_act, ct, piece):
        bi = ct % 2
        if piece == 0:
            sem, val = w2bank_free[bi]
        else:
            sem, val = (None, 0)
        mmv = s_mm.res() if piece == 3 else 0
        def wmm(t, bi=bi, ct=ct, p_i3=p_i3, piece=piece, na=p_act,
                sem=sem, val=val,
                w2cv2=(wf2_cv[ct // 2] if (p_i3 == 0 and piece == 0) else 0)):
            if piece == 0:
                t.wait_ge(s_ve.h, na)
            if w2cv2:
                t.wait_ge(s_po.h, w2cv2)
            if sem is not None:
                t.wait_ge(sem, val)
            for f in range(piece * 8, piece * 8 + 8):
                ins = t.matmul(w2banks[bi][:, 0:C3],
                               w2fb[:, f * D + ct * 128:f * D + (ct + 1) * 128],
                               actb[p_i3 % 2][:, f * C3:(f + 1) * C3],
                               start=(f == 0), stop=(f == FT - 1))
                if f == FT - 1:
                    ins.then_inc(s_mm.h, 1)
        em("tensor", wmm)
        if piece == 3:
            wmm_last[p_i3] = mmv
        return mmv

    def emit_tail_unit(p_i3, p_ldy, p_act, ct, mmv):
        bi = ct % 2
        vv = s_ve.res()
        def ev3(v, mmv=mmv, bi=bi, ct=ct, p_i3=p_i3, p_ldy=p_ldy,
                stguard=(out_sts[p_i3 - 1][1] if (p_i3 >= 1 and ct == 0) else 0)):
            v.wait_ge(s_mm.h, mmv)
            v.wait_ge(s_y3[p_i3 % 2].h, p_ldy)
            if stguard:
                v.wait_ge(s_out.h, stguard)
            sl = slice(ct * C3, (ct + 1) * C3)
            v.scalar_tensor_tensor(r2ch[:, sl], w2banks[bi][:, 0:C3],
                                   cc[:, 24 + ct:25 + ct],
                                   y3f[p_i3 % 2][:, sl], OP.add, OP.add).then_inc(s_ve.h, 1)
        em("vector", ev3)
        w2bank_free[bi] = (s_ve.h, vv)
        ev3_done[p_i3] = vv
        av = s_ac.res()
        em("scalar", lambda a, n=vv, ct=ct: (
            a.wait_ge(s_ve.h, n),
            a.activation(sq3f[:, ct * C3:(ct + 1) * C3],
                         r2ch[:, ct * C3:(ct + 1) * C3],
                         AF.Square).then_inc(s_ac.h, 1)))
        stv = s_mm.res()
        def stmm3(t, ct=ct, vv=vv, av=av, prevv=(prev_stev3 if ct == 0 else 0)):
            if prevv:
                t.wait_ge(s_ve.h, prevv)
            t.wait_ge(s_ve.h, vv)
            t.matmul(st_ps[0][0:1, 0:C3], ones_f[:],
                     r2ch[:, ct * C3:(ct + 1) * C3],
                     start=(ct == 0), stop=(ct == KT - 1))
            t.wait_ge(s_ac.h, av)
            t.matmul(st_ps[1][0:1, 0:C3], ones_bf[:],
                     sq3f[:, ct * C3:(ct + 1) * C3],
                     start=(ct == 0), stop=(ct == KT - 1)).then_inc(s_mm.h, 1)
        em("tensor", stmm3)
        return stv

    def emit_ln2_tail(p_i3):
        nonlocal_vals = {}
        sev0 = s_ve.res(1)
        def stev3a(v, n=st3_holder[0], m=sev0,
                   sd=(stream_done if p_i3 == 0 else 0)):
            if sd:
                v.wait_ge(s_po.h, sd)
            v.wait_ge(s_mm.h, n)
            v.tensor_copy(s2um[:], st_ps[0][0:1, 0:C3])
            v.tensor_copy(r2t2[:], st_ps[1][0:1, 0:C3]).then_inc(s_ve.h, 1)
            v.wait_ge(s_ve.h, m)
        em("vector", stev3a)
        sev1 = s_ve.res(1)
        def stev3b(v, m=sev1):
            v.tensor_scalar(s2um[:], s2um[:], float(1.0 / D), None, OP.mult)
            v.tensor_scalar(r2t2[:], r2t2[:], float(1.0 / D), None,
                            OP.mult).then_inc(s_ve.h, 1)
            v.wait_ge(s_ve.h, m)
        em("vector", stev3b)
        sev = s_ve.res(1)
        def stev3c(v, m=sev):
            v.tensor_mul(r2t1[:], s2um[:], s2um[:])
            v.engine_nop()
            v.engine_nop()
            v.tensor_sub(r2t2[:], r2t2[:], r2t1[:]).then_inc(s_ve.h, 1)
            v.wait_ge(s_ve.h, m)
        em("vector", stev3c)
        wait("scalar", s_ve, sev)
        a3 = s_ac.res()
        em("scalar", lambda a: a.activation(r2t1[:], r2t2[:], AF.Sqrt,
                                            bias=cst[0:1, 1:2]).then_inc(s_ac.h, 1))
        wait("vector", s_ac, a3)
        rv0 = s_ve.res()
        em("vector", lambda v, m=rv0: (v.reciprocal(r2t2[:], r2t1[:]).then_inc(s_ve.h, 1),
                                       v.wait_ge(s_ve.h, m)))
        rv = s_ve.res()
        em("vector", lambda v: v.tensor_mul(r2t1[:], s2um[:], r2t2[:]).then_inc(s_ve.h, 1))
        bcv = s_mm.res()
        def bc3(t, n=rv):
            t.wait_ge(s_ve.h, n)
            t.matmul(st_ps[0][:, 0:C3], ones1[:], r2t2[0:1, :], start=True, stop=True)
            t.matmul(st_ps[1][:, 0:C3], ones1[:], r2t1[0:1, :],
                     start=True, stop=True).then_inc(s_mm.h, 1)
        em("tensor", bc3)
        for ct in range(KT):
            nz30 = s_ve.res()
            nzv = s_ve.res()
            def nz3(v, ct=ct, bcv=bcv, m=nz30):
                v.wait_ge(s_mm.h, bcv)
                sl = slice(ct * C3, (ct + 1) * C3)
                v.tensor_tensor(r2ch[:, sl], r2ch[:, sl], st_ps[0][:, 0:C3],
                                OP.mult).then_inc(s_ve.h, 1)
                v.wait_ge(s_ve.h, m)
                v.tensor_sub(r2ch[:, sl], r2ch[:, sl],
                             st_ps[1][:, 0:C3]).then_inc(s_ve.h, 1)
            em("vector", nz3)
        nz_all = s_ve.n
        ov = s_out.res(16)
        def sto(a, n=nz_all, p_i3=p_i3):
            a.wait_ge(s_ve.h, n)
            a.dma_start(out=out_d[:, p_i3],
                        in_=r2ch[:].rearrange("p (h j) -> p h j", h=KT)).then_inc(s_out.h, 16)
        em("scalar", sto)
        out_sts.append((nz_all, ov))
        return max(sev, nz_all)

    st3_holder = [0]
    prev = None
    for i3 in range(NC3):
        if i3 >= 2:
            wait("sync", s_ve, ev3_done[i3 - 2])
        half = i3 % 2
        dma("sync", y3b[i3 % 2][:].rearrange("p (k j) -> p k j", k=KT),
            y1b_d[:, i3 // 2, :, half * C3:(half + 1) * C3], s_y3[i3 % 2])
        ld_y = dma("sync", y3f[i3 % 2][:].rearrange("p (k j) -> p k j", k=KT),
                   y1f_d[:, i3 // 2, :, half * C3:(half + 1) * C3], s_y3[i3 % 2])
        aguard = wmm_last.get(i3 - 2, 0)
        mmvs = {}
        midv = {}
        c2v = {}
        sinv = {}
        acts = {}
        tct = 0
        for slot in range(FT + 3):
            if slot - 3 >= 0:
                acts[slot - 3] = emit_stage_qv3(i3, slot - 3, sinv[slot - 3])
            if slot < FT:
                mmvs[slot] = emit_fmm(i3, ld_y, slot)
                emit_stage_qv1(i3, slot, mmvs[slot])
            if 0 <= slot - 1 < FT:
                ft = slot - 1
                sg = sinv.get(ft - 3, 0)
                midv[ft] = emit_stage_mid(i3, ft, sg)
            if 0 <= slot - 2 < FT:
                ft = slot - 2
                c2v[ft] = emit_stage_cvt2(i3, ft)
                sinv[ft] = emit_stage_sin(i3, ft, c2v[ft], aguard)
            if prev is not None and tct < KT * 4:
                p_i3, p_ldy, p_act = prev
                pm = emit_tail_piece(p_i3, p_act, tct // 4, tct % 4)
                if tct % 4 == 3:
                    st3_holder[0] = emit_tail_unit(p_i3, p_ldy, p_act,
                                                   tct // 4, pm)
                tct += 1
        if prev is not None:
            while tct < KT * 4:
                p_i3, p_ldy, p_act = prev
                pm = emit_tail_piece(p_i3, p_act, tct // 4, tct % 4)
                if tct % 4 == 3:
                    st3_holder[0] = emit_tail_unit(p_i3, p_ldy, p_act,
                                                   tct // 4, pm)
                tct += 1
            t_ln = emit_ln2_tail(p_i3)
            prev_stev3 = t_ln
        act_last = acts[FT - 1]
        prev = (i3, ld_y, act_last)

    p_i3, p_ldy, p_act = prev
    for ct in range(KT):
        for piece in range(4):
            pm = emit_tail_piece(p_i3, p_act, ct, piece)
        st3_holder[0] = emit_tail_unit(p_i3, p_ldy, p_act, ct, pm)
    emit_ln2_tail(p_i3)

    final_out = s_out.n
    em("scalar", lambda a: a.wait_ge(s_out.h, final_out))
    em("sync", lambda e: e.wait_ge(s_out.h, final_out))

    block = es.enter_context(nc.Block())

    @block.sync
    def _(sync):
        for fn in E["sync"]:
            fn(sync)

    @block.tensor
    def _(tensor):
        for fn in E["tensor"]:
            fn(tensor)

    @block.vector
    def _(vector):
        for fn in E["vector"]:
            fn(vector)

    @block.scalar
    def _(scalar):
        for fn in E["scalar"]:
            fn(scalar)

    @block.gpsimd
    def _(gpsimd):
        for fn in E["gpsimd"]:
            fn(gpsimd)

    g3.close()
    es.close()
    return nc


def kernel(**inputs):
    ins = {k: np.asarray(v, dtype=np.float32) for k, v in inputs.items()}
    x = ins["L0_attn_x"]

    def pm(a, nt):
        X = a.shape[1]
        return np.ascontiguousarray(a.reshape(nt, 128, X).transpose(1, 0, 2).reshape(128, nt * X))

    w1t = pm(np.ascontiguousarray(ins["L0_attn_w1_w"].T), KT)
    v1t = pm(np.ascontiguousarray(ins["L0_attn_v1_w"].T), KT)
    w2t = pm(np.ascontiguousarray(ins["L0_attn_w2_w"].T), KT)
    wf1 = pm(ins["L0_ffn_W1"], KT)
    wf2 = pm(ins["L0_ffn_W2"], FT)

    def col8(a):
        return np.ascontiguousarray(a.reshape(KT, 128).T)

    def col128(a):
        return np.ascontiguousarray(a.reshape(KT, 128, 16).transpose(1, 0, 2).reshape(128, 128))

    cc = np.zeros((128, 448), np.float32)
    cc[:, 0:8] = col8(ins["L0_attn_w1_b"])
    cc[:, 8:16] = col8(ins["L0_attn_v1_b"])
    cc[:, 16:24] = col8(ins["L0_attn_w2_b"])
    cc[:, 24:32] = col8(ins["L0_ffn_b2"])
    cc[:, 32:40] = col8(ins["L0_attn_D_ssm"])
    conv_w = ins["L0_attn_conv_w"][:, 0, :]
    cc[:, 40:48] = col8(conv_w[:, 0])
    cc[:, 48:56] = col8(conv_w[:, 1])
    cc[:, 56:64] = col8(conv_w[:, 2])
    cc[:, 64:192] = col128(ins["L0_attn_A"])
    cc[:, 192:320] = col128(ins["L0_attn_B_ssm"])
    cc[:, 320:448] = col128(ins["L0_attn_C_ssm"])

    def fcol(a):
        return np.ascontiguousarray(a.reshape(FT, 128).T)

    fc = np.zeros((128, 128), np.float32)
    fc[:, 0:32] = fcol(ins["L0_ffn_b1"])
    fc[:, 32:64] = fcol(ins["L0_ffn_quantum_phases"])
    fc[:, 64:96] = fcol(ins["L0_ffn_quantum_weights_real"])
    fc[:, 96:128] = fcol(ins["L0_ffn_quantum_weights_imag"])

    cst = np.zeros((128, 8), np.float32)
    cst[:, 0] = TWO_PI
    cst[:, 1] = 1e-5
    cst[:, 2] = -np.pi
    cst[:, 3] = TWO_PI / 262144.0

    nc = build_program()

    in_maps = []
    for core in range(8):
        xs = x[2 * core:2 * core + 2]
        xF = np.ascontiguousarray(xs.transpose(2, 0, 1)).reshape(D, NT)
        xfc = np.ascontiguousarray(
            xF.reshape(KT, 128, NCH, CH).transpose(1, 2, 0, 3))
        in_maps.append({
            "xf": xfc, "w1t": w1t, "v1t": v1t, "w2t": w2t,
            "wf1": wf1, "wf2": wf2, "cc": cc, "fc": fc, "cst": cst,
        })

    res = run_bass_kernel_spmd(nc, in_maps, core_ids=list(range(8)))

    outs = []
    for core in range(8):
        o = np.asarray(res.results[core]["out"], np.float32)
        oF = o.transpose(2, 0, 1, 3).reshape(D, NT)
        o2 = oF.reshape(D, NSEQ, T).transpose(1, 2, 0)
        outs.append(o2)
    return np.ascontiguousarray(np.concatenate(outs, axis=0))


# version 67
# speedup vs baseline: 1.1152x; 1.0889x over previous
import numpy as np
import ml_dtypes
from contextlib import ExitStack
import concourse.bass as bass
import concourse.mybir as mybir
from concourse.bass_utils import run_bass_kernel_spmd

F32 = mybir.dt.float32
BF16 = mybir.dt.bfloat16
I32 = mybir.dt.int32
AF = mybir.ActivationFunctionType
OP = mybir.AluOpType

D = 1024
T = 2048
NSEQ = 2
NT = NSEQ * T
KT = 8
FT = 32
CH = 512
NCH = NT // CH
C3 = 256
NC3 = NT // C3
TWO_PI = float(2.0 * np.pi)
POOL_S = []


class Sem:
    def __init__(self, es, nc, name):
        self.h = es.enter_context(nc.semaphore(name))
        self.n = 0

    def res(self, amt=1):
        self.n += amt
        return self.n


def build_program():
    nc = bass.Bass("TRN2", target_bir_lowering=False)

    xf_d = nc.dram_tensor("xf", [128, NCH, KT, CH], F32, kind="ExternalInput").ap()
    w1t_d = nc.dram_tensor("w1t", [128, KT * D], F32, kind="ExternalInput").ap()
    v1t_d = nc.dram_tensor("v1t", [128, KT * D], F32, kind="ExternalInput").ap()
    w2t_d = nc.dram_tensor("w2t", [128, KT * D], F32, kind="ExternalInput").ap()
    wf1_d = nc.dram_tensor("wf1", [128, KT * 4096], F32, kind="ExternalInput").ap()
    wf2_d = nc.dram_tensor("wf2", [128, FT * D], F32, kind="ExternalInput").ap()
    cc_d = nc.dram_tensor("cc", [128, 448], F32, kind="ExternalInput").ap()
    fc_d = nc.dram_tensor("fc", [128, 128], F32, kind="ExternalInput").ap()
    cst_d = nc.dram_tensor("cst", [128, 8], F32, kind="ExternalInput").ap()
    out_d = nc.dram_tensor("out", [128, NC3, KT, C3], F32, kind="ExternalOutput").ap()
    xbf_d = nc.dram_tensor("xbf_d", [128, NCH, KT, CH], BF16).ap()
    gb_d = nc.dram_tensor("gb_d", [128, NCH, KT, CH], BF16).ap()
    r1f_d = nc.dram_tensor("r1f", [128, NCH, KT, CH], F32).ap()
    y1f_d = nc.dram_tensor("y1f", [128, NCH, KT, CH], F32).ap()
    y1b_d = nc.dram_tensor("y1bd", [128, NCH, KT, CH], BF16).ap()

    es = ExitStack()
    g1 = ExitStack()
    gv = ExitStack()
    gx = ExitStack()
    g2 = ExitStack()
    g2b = ExitStack()
    g2be = ExitStack()
    g2d = ExitStack()
    g3 = ExitStack()

    def sbg(g, name, shape, dt=F32):
        return g.enter_context(nc.sbuf_tensor(name, shape, dt))

    cc = sbg(es, "cc_s", [128, 64])
    fc = sbg(es, "fc_s", [128, 128])
    cst = sbg(es, "cst_s", [128, 8])
    cw = sbg(es, "cw", [128, 32])
    alp = sbg(es, "alp", [128, 32])
    beta = sbg(es, "beta", [128, 32])
    ones_bf = sbg(es, "ones_bf", [128, 1], BF16)
    ones_f = sbg(es, "ones_f", [128, 1])
    ones1 = sbg(es, "ones1", [1, 128])
    cabc = sbg(gv, "cabc", [128, 384])
    abar = sbg(gv, "abar", [128, 128])
    gg = sbg(gv, "gg", [128, 128])
    tfc1 = sbg(gv, "tfc1", [128, 32])
    tfc2 = sbg(gv, "tfc2", [128, 32])
    tfc3 = sbg(gv, "tfc3", [128, 32])
    tfci = sbg(gv, "tfci", [128, 32])
    tfcf = sbg(gv, "tfcf", [128, 32])

    v1tb = sbg(gv, "v1tb", [128, KT * D], BF16)
    xmix = sbg(gx, "xmix", [128, KT * NT], BF16)
    w1tb = sbg(g1, "w1tb", [128, KT * D], BF16)
    xstg = sbg(g1, "xstg", [128, KT * CH])
    xbfst = [sbg(g1, f"xbfst{i}", [128, KT * CH], BF16) for i in range(2)]
    convb = sbg(g1, "convb", [128, T])

    mm_ps = [es.enter_context(nc.psum_tensor(f"mmps{i}", [128, 512], F32)) for i in range(4)]
    st_ps = [es.enter_context(nc.psum_tensor(f"stps{i}", [128, 512], F32)) for i in range(2)]
    bc_ps = [es.enter_context(nc.psum_tensor(f"bcps{i}", [128, 512], F32)) for i in range(2)]

    import os
    if os.environ.get("KDBG3"):
        DBG3 = [nc.dram_tensor("dbg3r", [128, KT * C3], F32, kind="ExternalOutput").ap(),
                nc.dram_tensor("dbg3s", [1, 2 * C3], F32, kind="ExternalOutput").ap(),
                nc.dram_tensor("dbg3a", [128, FT * C3], BF16, kind="ExternalOutput").ap(),
                nc.dram_tensor("dbg3y", [128, KT * C3], BF16, kind="ExternalOutput").ap()]
    else:
        DBG3 = None
    s_ld = Sem(es, nc, "s_ld")
    s_xstg = Sem(es, nc, "s_xstg")
    s_xbv = [Sem(es, nc, f"s_xbv{i}") for i in range(2)]
    s_stg2s = [Sem(es, nc, f"s_stg2s{i}") for i in range(2)]
    s_gbc = [Sem(es, nc, f"s_gbc{i}") for i in range(2)]
    s_r1l = [Sem(es, nc, f"s_r1l{i}") for i in range(2)]
    s_y3 = [Sem(es, nc, f"s_y3{i}") for i in range(2)]
    s_wst = [Sem(es, nc, f"s_wst{i}") for i in range(2)]
    s_stxbf = [Sem(es, nc, f"s_stxbf{i}") for i in range(2)]
    s_stgb = [Sem(es, nc, f"s_stgb{i}") for i in range(2)]
    s_sty1 = [Sem(es, nc, f"s_sty1{i}") for i in range(2)]
    s_str1 = Sem(es, nc, "s_str1")
    s_st = Sem(es, nc, "s_st")
    s_out = Sem(es, nc, "s_out")
    s_mm = Sem(es, nc, "s_mm")
    s_ve = Sem(es, nc, "s_ve")
    s_ac = Sem(es, nc, "s_ac")
    s_po = Sem(es, nc, "s_po")

    E = {k: [] for k in ("sync", "tensor", "vector", "scalar", "gpsimd")}

    def em(eng, fn):
        E[eng].append(fn)

    def wait(eng, sem, val):
        if val > 0:
            em(eng, lambda e, s=sem.h, v=val: e.wait_ge(s, v))

    def dma(eng, out, in_, sem):
        v = sem.res(16)
        em(eng, lambda e, o=out, i=in_, s=sem.h: e.dma_start(out=o, in_=i).then_inc(s, 16))
        return v

    dma("sync", cc[:], cc_d[:, 0:64], s_ld)
    dma("sync", cabc[:], cc_d[:, 64:448], s_ld)
    dma("sync", fc[:], fc_d, s_ld)
    ld_consts = dma("sync", cst[:], cst_d, s_ld)

    ve_wconv = 0
    for wd, wb in ((w1t_d, w1tb), (v1t_d, v1tb)):
        for piece in range(2):
            ldv = dma("sync", xstg[:],
                      wd[:, piece * 4 * D:(piece + 1) * 4 * D], s_xstg)
            wait("vector", s_xstg, ldv)
            cv = s_ve.res()
            em("vector", lambda v, dst=wb, piece=piece: v.tensor_copy(
                dst[:, piece * 4 * D:(piece + 1) * 4 * D], xstg[:]).then_inc(s_ve.h, 1))
            wait("sync", s_ve, cv)
            ve_wconv = cv

    wait("scalar", s_ld, ld_consts)
    wait("vector", s_ld, ld_consts)
    em("scalar", lambda a: a.activation(abar[:], cabc[:, 0:128], AF.Sigmoid).then_inc(s_ac.h, 1))
    s_ac.res()

    def emv(fn):
        v0 = s_ve.res()
        em("vector", lambda e, f=fn: f(e).then_inc(s_ve.h, 1))
        wait("vector", s_ve, v0)
        return v0

    emv(lambda v: v.tensor_mul(gg[:], cabc[:, 128:256], cabc[:, 256:384]))

    TH = fc[:, 32:64]
    WR = fc[:, 64:96]
    WI = fc[:, 96:128]
    B1C = fc[:, 0:32]

    def trig(dst, shift):
        emv(lambda v, sh=shift: v.tensor_scalar(
            tfc3[:], TH, float(262144.0 / TWO_PI),
            float((64.5 + sh / TWO_PI) * 262144.0), OP.mult, OP.add))
        emv(lambda v: v.tensor_copy(tfci[:].bitcast(I32), tfc3[:]))
        emv(lambda v: v.tensor_scalar(tfci[:].bitcast(I32), tfci[:].bitcast(I32),
                                      262143, None, OP.bitwise_and))
        dv = emv(lambda v: v.tensor_copy(tfcf[:], tfci[:].bitcast(I32)))
        wait("scalar", s_ve, dv)
        av = s_ac.res()
        em("scalar", lambda a, d2=dst: a.activation(
            d2, tfcf[:], AF.Sin, scale=cst[:, 3:4],
            bias=cst[:, 2:3]).then_inc(s_ac.h, 1))
        wait("vector", s_ac, av)
        return av

    trig(tfc1[:], 0.0)
    actrig = trig(tfc2[:], float(np.pi / 2.0))
    wait("vector", s_ac, actrig)
    emv(lambda v: v.tensor_mul(tfc3[:], tfc2[:], WR))
    emv(lambda v: v.tensor_mul(tfc1[:], tfc1[:], WI))
    emv(lambda v: v.tensor_add(tfc3[:], tfc3[:], tfc1[:]))
    emv(lambda v: v.tensor_mul(tfc3[:], tfc3[:], WI))
    emv(lambda v: v.tensor_scalar_mul(alp[:], tfc3[:], 0.1))
    emv(lambda v: v.tensor_scalar(tfc2[:], TH, float(262144.0 / TWO_PI),
                                  float(64.5 * 262144.0), OP.mult, OP.add))
    emv(lambda v: v.scalar_tensor_tensor(
        cw[:], B1C, float(0.1 * 262144.0 / TWO_PI), tfc2[:], OP.mult, OP.add))
    emv(lambda v: v.scalar_tensor_tensor(
        beta[:], B1C, float(-0.1 * 262144.0 / TWO_PI), cw[:], OP.mult, OP.add))
    emv(lambda v: v.tensor_scalar(beta[:], beta[:],
                                  float(-TWO_PI / (0.1 * 262144.0)), None, OP.mult))
    emv(lambda v: v.memset(ones_bf[:], 1.0))
    emv(lambda v: v.memset(ones_f[:], 1.0))
    emv(lambda v: v.memset(ones1[:], 1.0))

    bank_free = {b: (None, 0) for b in range(4)}
    xcvt = []
    xbf_store = []
    x1_evac = {}
    for ch in range(NCH):
        if ch >= 1:
            wait("sync", s_po, xcvt[ch - 1])
        else:
            wait("sync", s_ve, ve_wconv)
        ldv = dma("sync", xstg[:], xf_d[:, ch], s_xstg)
        if ch >= 2:
            wait("gpsimd", s_ac, x1_evac[(KT - 1, ch - 2)])
            wait("gpsimd", s_stxbf[ch % 2], xbf_store[ch - 2])
        wait("gpsimd", s_xstg, ldv)
        pv = s_po.res()
        em("gpsimd", lambda p, ch=ch: p.tensor_copy(
            xbfst[ch % 2][:], xstg[:]).then_inc(s_po.h, 1))
        xcvt.append(pv)
        wait("sync", s_po, pv)
        sv = dma("sync", xbf_d[:, ch], xbfst[ch % 2][:].rearrange("p (k j) -> p k j", k=KT),
                 s_stxbf[ch % 2])
        xbf_store.append(sv)

        for ct in range(KT):
            bank = ct % 2
            sem, val = bank_free[bank]
            mmv = s_mm.res()
            def f1(t, bank=bank, ct=ct, ch=ch, sem=sem, val=val,
                   pv=pv, wcv=(ve_wconv if (ch == 0 and ct == 0) else 0)):
                if ct == 0:
                    t.wait_ge(s_po.h, pv)
                if wcv:
                    t.wait_ge(s_ve.h, wcv)
                if sem is not None:
                    t.wait_ge(sem, val)
                for k in range(KT):
                    ins = t.matmul(mm_ps[bank][:, 0:CH],
                                   w1tb[:, k * D + ct * 128:k * D + (ct + 1) * 128],
                                   xbfst[ch % 2][:, k * CH:(k + 1) * CH],
                                   start=(k == 0), stop=(k == KT - 1))
                    if k == KT - 1:
                        ins.then_inc(s_mm.h, 1)
            em("tensor", f1)
            av = s_ac.res()
            def ev(a, mmv=mmv, bank=bank, ct=ct, ch=ch):
                a.wait_ge(s_mm.h, mmv)
                a.add(xmix[:, ct * NT + ch * CH:ct * NT + (ch + 1) * CH],
                      mm_ps[bank][:, 0:CH], cc[:, ct:ct + 1]).then_inc(s_ac.h, 1)
            em("scalar", ev)
            bank_free[bank] = (s_ac.h, av)
            x1_evac[(ct, ch)] = av
    mm_phase1 = s_mm.n

    xc_done_ct = {}
    for ct in range(KT):
        for seq in range(NSEQ):
            need = x1_evac[(ct, (seq + 1) * (NCH // NSEQ) - 1)]
            cv0 = s_ve.res()
            cv1 = s_ve.res()
            cv = s_ve.res()
            def cvf(v, need=need, seq=seq, ct=ct, m0=cv0, m1=cv1):
                base = ct * NT + seq * T
                v.wait_ge(s_ac.h, need)
                v.tensor_scalar(convb[:], xmix[:, base:base + T],
                                cc[:, 48 + ct:49 + ct], None, OP.mult).then_inc(s_ve.h, 1)
                v.wait_ge(s_ve.h, m0)
                v.scalar_tensor_tensor(convb[:, 1:T], xmix[:, base:base + T - 1],
                                       cc[:, 40 + ct:41 + ct], convb[:, 1:T],
                                       OP.mult, OP.add).then_inc(s_ve.h, 1)
                v.wait_ge(s_ve.h, m1)
                v.scalar_tensor_tensor(convb[:, 0:T - 1], xmix[:, base + 1:base + T],
                                       cc[:, 56 + ct:57 + ct], convb[:, 0:T - 1],
                                       OP.mult, OP.add).then_inc(s_ve.h, 1)
            em("vector", cvf)
            av = s_ac.res()
            def slf(a, cv=cv, seq=seq, ct=ct):
                a.wait_ge(s_ve.h, cv)
                a.activation(xmix[:, ct * NT + seq * T:ct * NT + (seq + 1) * T],
                             convb[:], AF.Silu).then_inc(s_ac.h, 1)
            em("scalar", slf)
            wait("vector", s_ac, av)
        xc_done_ct[ct] = s_ac.n
    g1.close()

    hbuf = sbg(g2, "hbuf", [128, T])
    ydve = sbg(g2, "ydve", [128, T])
    if POOL_S:
        phbuf = [sbg(g2, f"phbuf{i}", [128, T]) for i in range(2)]
        ypool = sbg(g2, "ypool", [128, T])
        ptmp = sbg(g2, "ptmp", [128, T])
    xbfv = [sbg(g2, f"xbfv{i}", [128, KT * CH], BF16) for i in range(2)]
    vtile = [sbg(g2, f"vtile{i}", [128, NT], BF16) for i in range(2)]

    wait("vector", s_mm, mm_phase1)
    wait("gpsimd", s_mm, mm_phase1)
    wait("tensor", s_ac, xc_done_ct[KT - 1])

    pb_guard = [0, 0]
    pool_cnt = 0
    v_done_ct = {}
    gb_store = []
    for ct in range(KT):
        for ch in range(NCH):
            gi = (ct * NCH + ch) % 2
            if ct * NCH + ch >= 2:
                prev = v_done_ct.get((ct * NCH + ch - 2))
                wait("sync", s_ac, prev)
            else:
                wait("sync", s_ac, xc_done_ct[KT - 1])
            if ct == 0 and ch == 0:
                wait("sync", s_stxbf[0], s_stxbf[0].n)
                wait("sync", s_stxbf[1], s_stxbf[1].n)
            ldv = dma("sync", xbfv[gi][:].rearrange("p (k j) -> p k j", k=KT),
                      xbf_d[:, ch], s_xbv[gi])
            bank = ch % 2
            sem, val = bank_free[bank]
            mmv = s_mm.res()
            def fv(t, bank=bank, ct=ct, ch=ch, gi=gi, sem=sem, val=val, ldv=ldv):
                t.wait_ge(s_xbv[gi].h, ldv)
                if sem is not None:
                    t.wait_ge(sem, val)
                for k in range(KT):
                    ins = t.matmul(mm_ps[bank][:, 0:CH],
                                   v1tb[:, k * D + ct * 128:k * D + (ct + 1) * 128],
                                   xbfv[gi][:, k * CH:(k + 1) * CH],
                                   start=(k == 0), stop=(k == KT - 1))
                    if k == KT - 1:
                        ins.then_inc(s_mm.h, 1)
            em("tensor", fv)
            av = s_ac.res()
            def evv(a, mmv=mmv, bank=bank, ct=ct, ch=ch,
                    stg=(gb_store[ct - 2] if (ct >= 2 and ch == 0) else 0)):
                a.wait_ge(s_mm.h, mmv)
                if stg:
                    a.wait_ge(s_stgb[ct % 2].h, stg)
                a.activation(vtile[ct % 2][:, ch * CH:(ch + 1) * CH],
                             mm_ps[bank][:, 0:CH], AF.Silu,
                             bias=cc[:, 8 + ct:9 + ct]).then_inc(s_ac.h, 1)
            em("scalar", evv)
            bank_free[bank] = (s_ac.h, av)
            v_done_ct[ct * NCH + ch] = av
        v_ready = v_done_ct[ct * NCH + NCH - 1]

        wait("vector", s_ac, xc_done_ct[ct])
        wait("gpsimd", s_ac, xc_done_ct[ct])
        for seq in range(NSEQ):
            src = xmix[:, ct * NT + seq * T:ct * NT + (seq + 1) * T]
            first_dve = True
            first_pool = True
            for s in range(16):
                col = ct * 16 + s
                if s not in POOL_S:
                    sc0 = s_ve.res()
                    em("vector", lambda v, col=col, src=src, m=sc0: (
                        v.tensor_tensor_scan(
                            hbuf[:], abar[:, col:col + 1].broadcast_to([128, T]), src,
                            0.0, OP.mult, OP.add).then_inc(s_ve.h, 1),
                        v.wait_ge(s_ve.h, m)))
                    if first_dve:
                        em("vector", lambda v, col=col: v.tensor_scalar(
                            ydve[:], hbuf[:], gg[:, col:col + 1], None, OP.mult))
                        first_dve = False
                    else:
                        em("vector", lambda v, col=col: v.scalar_tensor_tensor(
                            ydve[:], hbuf[:], gg[:, col:col + 1], ydve[:], OP.mult, OP.add))
                else:
                    pi = pool_cnt % 2
                    pool_cnt += 1
                    if pb_guard[pi]:
                        wait("vector", s_po, pb_guard[pi])
                    sv = s_ve.res()
                    em("vector", lambda v, col=col, src=src, pi=pi: v.tensor_tensor_scan(
                        phbuf[pi][:], abar[:, col:col + 1].broadcast_to([128, T]), src,
                        0.0, OP.mult, OP.add).then_inc(s_ve.h, 1))
                    pv = s_po.res()
                    if first_pool:
                        def pf(p, n=sv, pi=pi, col=col):
                            p.wait_ge(s_ve.h, n)
                            p.tensor_tensor(ypool[:], phbuf[pi][:],
                                            gg[:, col:col + 1].broadcast_to([128, T]),
                                            OP.mult).then_inc(s_po.h, 1)
                        first_pool = False
                    else:
                        def pf(p, n=sv, pi=pi, col=col):
                            p.wait_ge(s_ve.h, n)
                            p.tensor_tensor(ptmp[:], phbuf[pi][:],
                                            gg[:, col:col + 1].broadcast_to([128, T]), OP.mult)
                            p.tensor_tensor(ypool[:], ypool[:], ptmp[:],
                                            OP.add).then_inc(s_po.h, 1)
                    em("gpsimd", pf)
                    pb_guard[pi] = pv
            dt0 = s_ve.res()
            em("vector", lambda v, ct=ct, src=src, m=dt0: (
                v.scalar_tensor_tensor(
                    ydve[:], src, cc[:, 32 + ct:33 + ct], ydve[:],
                    OP.mult, OP.add).then_inc(s_ve.h, 1),
                v.wait_ge(s_ve.h, m)))
            if POOL_S:
                wait("vector", s_po, s_po.n)
                dt1 = s_ve.res()
                em("vector", lambda v, m=dt1: (
                    v.tensor_add(ydve[:], ydve[:], ypool[:]).then_inc(s_ve.h, 1),
                    v.wait_ge(s_ve.h, m)))
            wait("vector", s_ac, v_ready)
            gvv = s_ve.res()
            em("vector", lambda v, ct=ct, seq=seq: v.tensor_tensor(
                vtile[ct % 2][:, seq * T:(seq + 1) * T], ydve[:],
                vtile[ct % 2][:, seq * T:(seq + 1) * T],
                OP.mult).then_inc(s_ve.h, 1))
            wait("gpsimd", s_ve, gvv)
        gate_ct = s_ve.n
        wait("sync", s_ve, gate_ct)
        sv2 = dma("sync", gb_d[:, :, ct, :],
                  vtile[ct % 2][:].rearrange("p (h j) -> p h j", h=NCH), s_stgb[ct % 2])
        gb_store.append(sv2)
        if ct >= 1:
            pass
    gate_all = s_ve.n
    g2.close()
    gx.close()
    gv.close()

    g3a = ExitStack()
    w1fb = sbg(g3a, "w1fb", [128, KT * 4096], BF16)
    wstg3 = [sbg(g3a, f"wstg3{i}", [128, 256]) for i in range(2)]
    wf1_cv = {}
    for par in range(2):
        wait("gpsimd", s_stgb[par], s_stgb[par].n)
    for idx1 in range(128):
        f2, k = idx1 // KT, idx1 % KT
        col = k * 4096 + f2 * 256
        buf = wstg3[idx1 % 2]
        wv1 = s_wst[idx1 % 2].res(16)
        cv = s_po.res()
        def w1p(p, col=col, buf=buf, wv1=wv1, si=idx1 % 2):
            p.dma_start(out=buf[:], in_=wf1_d[:, col:col + 256]).then_inc(
                s_wst[si].h, 16)
            p.wait_ge(s_wst[si].h, wv1)
            p.tensor_copy(w1fb[:, col:col + 256], buf[:]).then_inc(s_po.h, 1)
        em("gpsimd", w1p)
        if k == KT - 1:
            wf1_cv[f2] = cv
    w1_stream_done = cv

    ssum = sbg(g2b, "ssum", [1, NT])
    ssq = sbg(g2b, "ssq", [1, NT])
    r1ch = [sbg(g2b, f"r1ch{i}", [128, KT * CH]) for i in range(2)]
    r1bch = sbg(g2b, "r1bch", [128, KT * CH], BF16)
    w2tb = sbg(g2be, "w2tb", [128, KT * D], BF16)
    stg2 = [sbg(g2be, f"stg2{i}", [128, KT * CH // 2]) for i in range(2)]
    gbch = [sbg(g2be, f"gbch{i}", [128, KT * CH], BF16) for i in range(2)]
    sqch = sbg(g2be, "sqch", [128, KT * CH], BF16)

    for par in range(2):
        wait("vector", s_stgb[par], s_stgb[par].n)
        wait("gpsimd", s_stgb[par], s_stgb[par].n)
        wait("sync", s_stgb[par], s_stgb[par].n)
    w2cv = 0
    for piece in range(4):
        if piece >= 2:
            wait("sync", s_ve, w2cv)
        ldv = dma("sync", stg2[piece % 2][:],
                  w2t_d[:, piece * 2 * D:(piece + 1) * 2 * D], s_stg2s[piece % 2])
        wait("vector", s_stg2s[piece % 2], ldv)
        cv = s_ve.res()
        em("vector", lambda v, piece=piece: v.tensor_copy(
            w2tb[:, piece * 2 * D:(piece + 1) * 2 * D],
            stg2[piece % 2][:]).then_inc(s_ve.h, 1))
        w2cv = cv

    bank_free = {b: (None, 0) for b in range(4)}
    wait("tensor", s_ve, w2cv)
    st_hist = {}
    ev_half = {}
    stat_evac = []
    r1_store = []
    gb_loads = []
    for ch in range(NCH):
        if ch >= 2:
            wait("sync", s_str1, r1_store[ch - 2])
        ld_xh = []
        for hf in range(2):
            gev = ev_half.get((ch - 1, hf), 0)
            if gev:
                wait("sync", s_ve, gev)
            elif ch == 0:
                wait("sync", s_ve, w2cv)
            ld_xh.append(dma("sync", stg2[(ch * 2 + hf) % 2][:],
                             xf_d[:, ch, hf * 4:(hf + 1) * 4],
                             s_stg2s[(ch * 2 + hf) % 2]))
        if ch >= 2:
            wait("sync", s_ve, gb_loads[ch - 2])
        ld_g = dma("sync", gbch[ch % 2][:], gb_d[:, ch], s_gbc[ch % 2])
        for ct in range(KT):
            bank = ct % 2
            sem, val = bank_free[bank]
            mmv = s_mm.res()
            def f(t, bank=bank, ct=ct, ch=ch, sem=sem, val=val, ld_g=ld_g):
                if ct == 0:
                    t.wait_ge(s_gbc[ch % 2].h, ld_g)
                if sem is not None:
                    t.wait_ge(sem, val)
                for k in range(KT):
                    ins = t.matmul(mm_ps[bank][:, 0:CH],
                                   w2tb[:, k * D + ct * 128:k * D + (ct + 1) * 128],
                                   gbch[ch % 2][:, k * CH:(k + 1) * CH],
                                   start=(k == 0), stop=(k == KT - 1))
                    if k == KT - 1:
                        ins.then_inc(s_mm.h, 1)
            em("tensor", f)
            vv = s_ve.res()
            def ev(v, mmv=mmv, bank=bank, ct=ct, ch=ch,
                   hf=ct // 4, ld_x=ld_xh[ct // 4]):
                v.wait_ge(s_mm.h, mmv)
                v.wait_ge(s_stg2s[(ch * 2 + hf) % 2].h, ld_x)
                v.scalar_tensor_tensor(
                    r1ch[ch % 2][:, ct * CH:(ct + 1) * CH],
                    mm_ps[bank][:, 0:CH], cc[:, 16 + ct:17 + ct],
                    stg2[(ch * 2 + hf) % 2][:, (ct % 4) * CH:(ct % 4 + 1) * CH],
                    OP.add, OP.add).then_inc(s_ve.h, 1)
            em("vector", ev)
            if ct % 4 == 3:
                ev_half[(ch, ct // 4)] = s_ve.n
            bank_free[bank] = (s_ve.h, vv)
            bv = s_ve.res()
            def bcp(v, ct=ct, ch=ch,
                    gst=(st_hist[ch - 1] if (ct == 0 and ch > 0) else 0)):
                if gst:
                    v.wait_ge(s_mm.h, gst)
                v.tensor_copy(r1bch[:, ct * CH:(ct + 1) * CH],
                              r1ch[ch % 2][:, ct * CH:(ct + 1) * CH]).then_inc(s_ve.h, 1)
            em("vector", bcp)
            av = s_ac.res()
            em("scalar", lambda a, n=bv, ct=ct, ch=ch: (
                a.wait_ge(s_ve.h, n),
                a.activation(sqch[:, ct * CH:(ct + 1) * CH],
                             r1bch[:, ct * CH:(ct + 1) * CH],
                             AF.Square).then_inc(s_ac.h, 1)))
            sq_last = av
        gb_loads.append(s_ve.n)
        stv = s_mm.res()
        def stmm_b(t, ch=ch, av=sq_last,
                   prev=(stat_evac[ch - 1] if ch > 0 else 0)):
            if prev:
                t.wait_ge(s_ve.h, prev)
            t.wait_ge(s_ac.h, av)
            for ct2 in range(KT):
                t.matmul(st_ps[0][0:1, 0:CH], ones_bf[:],
                         r1bch[:, ct2 * CH:(ct2 + 1) * CH],
                         start=(ct2 == 0), stop=(ct2 == KT - 1))
                ins = t.matmul(st_ps[1][0:1, 0:CH], ones_bf[:],
                               sqch[:, ct2 * CH:(ct2 + 1) * CH],
                               start=(ct2 == 0), stop=(ct2 == KT - 1))
                if ct2 == KT - 1:
                    ins.then_inc(s_mm.h, 1)
        em("tensor", stmm_b)
        st_done = stv
        st_hist[ch] = stv
        sev = s_ve.res()
        def stev(v, ch=ch, n=st_done):
            v.wait_ge(s_mm.h, n)
            v.tensor_copy(ssum[0:1, ch * CH:(ch + 1) * CH], st_ps[0][0:1, 0:CH])
            v.tensor_copy(ssq[0:1, ch * CH:(ch + 1) * CH],
                          st_ps[1][0:1, 0:CH]).then_inc(s_ve.h, 1)
        em("vector", stev)
        stat_evac.append(sev)
        if ch < NCH - 2:
            wait("sync", s_ve, sev)
            stv2 = dma("sync", r1f_d[:, ch],
                       r1ch[ch % 2][:].rearrange("p (k j) -> p k j", k=KT), s_str1)
            r1_store.append(stv2)

    g2be.close()
    istd_row = sbg(g2d, "istd_row", [1, NT])
    mi_row = sbg(g2d, "mi_row", [1, NT])
    y1ch = [sbg(g2d, f"y1ch{i}", [128, KT * CH // 2]) for i in range(2)]
    y1bch = [sbg(g2d, f"y1bch{i}", [128, KT * CH // 2], BF16) for i in range(2)]

    wait("vector", s_ve, stat_evac[-1])
    r0 = s_ve.res()
    em("vector", lambda v, m=r0: (v.tensor_scalar(ssum[:], ssum[:], float(1.0 / D),
                                                  None, OP.mult).then_inc(s_ve.h, 1),
                                  v.wait_ge(s_ve.h, m)))
    r1v = s_ve.res()
    em("vector", lambda v, m=r1v: (v.tensor_mul(mi_row[:], ssum[:], ssum[:]).then_inc(s_ve.h, 1),
                                   v.wait_ge(s_ve.h, m)))
    r2v = s_ve.res()
    em("vector", lambda v, m=r2v: (v.tensor_scalar(ssq[:], ssq[:], float(1.0 / D),
                                                   None, OP.mult).then_inc(s_ve.h, 1),
                                   v.wait_ge(s_ve.h, m)))
    varv = s_ve.res()
    em("vector", lambda v, m=varv: (v.tensor_sub(ssq[:], ssq[:], mi_row[:]).then_inc(s_ve.h, 1),
                                    v.wait_ge(s_ve.h, m)))
    wait("scalar", s_ve, varv)
    sqv = s_ac.res()
    em("scalar", lambda a: a.activation(ssq[:], ssq[:], AF.Sqrt,
                                        bias=cst[0:1, 1:2]).then_inc(s_ac.h, 1))
    wait("vector", s_ac, sqv)
    r3v = s_ve.res()
    em("vector", lambda v, m=r3v: (v.reciprocal(istd_row[:], ssq[:]).then_inc(s_ve.h, 1),
                                   v.wait_ge(s_ve.h, m)))
    ln1_rows = s_ve.res()
    em("vector", lambda v: v.tensor_mul(mi_row[:], ssum[:], istd_row[:]).then_inc(s_ve.h, 1))

    wait("tensor", s_ve, ln1_rows)
    y1_store = []
    y1h_store = {}
    nz_last = {}
    for i, ch in enumerate([6, 7, 0, 1, 2, 3, 4, 5]):
        if ch >= NCH - 2:
            ld_r1 = 0
        else:
            if i < 4:
                wait("sync", s_ve, nz_last[6 if ch == 0 else 7])
            else:
                wait("sync", s_sty1[i % 2], y1h_store[(i - 2) * 2 + 1])
            ld_r1 = dma("sync", r1ch[ch % 2][:].rearrange("p (h j) -> p h j", h=KT),
                        r1f_d[:, ch], s_r1l[ch % 2])
        bcv = s_mm.res()
        def bc(t, ch=ch):
            t.matmul(bc_ps[0][:, 0:CH], ones1[:], istd_row[0:1, ch * CH:(ch + 1) * CH],
                     start=True, stop=True)
            t.matmul(bc_ps[1][:, 0:CH], ones1[:], mi_row[0:1, ch * CH:(ch + 1) * CH],
                     start=True, stop=True).then_inc(s_mm.h, 1)
        em("tensor", bc)
        for hf in range(2):
            bi = (i * 2 + hf) % 2
            gy = y1h_store.get((i * 2 + hf) - 2, 0)
            for ct in range(hf * 4, hf * 4 + 4):
                nz0 = s_ve.res()
                nzv = s_ve.res()
                def nz(v, ct=ct, ch=ch, bi=bi, bcv=bcv, ld_r1=ld_r1, m=nz0,
                       gy=(gy if ct % 4 == 0 else 0), gp=(i * 2 + hf) % 2):
                    v.wait_ge(s_mm.h, bcv)
                    if ld_r1:
                        v.wait_ge(s_r1l[ch % 2].h, ld_r1)
                    if gy:
                        v.wait_ge(s_sty1[gp].h, gy)
                    sl = slice(ct * CH, (ct + 1) * CH)
                    so = slice((ct % 4) * CH, (ct % 4 + 1) * CH)
                    v.tensor_tensor(y1ch[bi][:, so], r1ch[ch % 2][:, sl],
                                    bc_ps[0][:, 0:CH], OP.mult).then_inc(s_ve.h, 1)
                    v.wait_ge(s_ve.h, m)
                    v.tensor_sub(y1ch[bi][:, so], y1ch[bi][:, so],
                                 bc_ps[1][:, 0:CH]).then_inc(s_ve.h, 1)
                em("vector", nz)
                pv = s_ve.res()
                em("vector", lambda v, ct=ct, bi=bi: v.tensor_copy(
                    y1bch[bi][:, (ct % 4) * CH:(ct % 4 + 1) * CH],
                    y1ch[bi][:, (ct % 4) * CH:(ct % 4 + 1) * CH]).then_inc(s_ve.h, 1))
            wait("sync", s_ve, s_ve.n)
            dma("sync", y1f_d[:, ch, hf * 4:(hf + 1) * 4],
                y1ch[bi][:].rearrange("p (h j) -> p h j", h=KT // 2),
                s_sty1[(i * 2 + hf) % 2])
            yv = dma("sync", y1b_d[:, ch, hf * 4:(hf + 1) * 4],
                     y1bch[bi][:].rearrange("p (h j) -> p h j", h=KT // 2),
                     s_sty1[(i * 2 + hf) % 2])
            y1h_store[i * 2 + hf] = yv
        nz_last[ch] = s_ve.n
        wait("tensor", s_ve, s_ve.n)
        y1_store.append(yv)
    g2d.close()
    g2b.close()

    w2fb = sbg(g3, "w2fb", [128, FT * D], BF16)
    y3b = [sbg(g3, f"y3b{i}", [128, KT * C3], BF16) for i in range(2)]
    y3f = [sbg(g3, f"y3f{i}", [128, KT * C3]) for i in range(2)]
    actb = [sbg(g3, f"actb{i}", [128, FT * C3], BF16) for i in range(2)]
    wq = [sbg(g3, f"wq{i}", [128, C3]) for i in range(2)]
    s1q = [sbg(g3, f"s1q{i}", [128, C3]) for i in range(3)]
    r2ch = sbg(g3, "r2ch", [128, KT * C3])
    sq3f = sbg(g3, "sq3f", [128, KT * C3], BF16)

    wait("sync", s_sty1[0], s_sty1[0].n)
    wait("sync", s_sty1[1], s_sty1[1].n)
    wf2_cv = {}
    for idx2 in range(128):
        c2, f = idx2 // FT, idx2 % FT
        col = f * D + c2 * 256
        buf = wstg3[idx2 % 2]
        wv2 = s_wst[idx2 % 2].res(16)
        cv = s_po.res()
        def w2p(p, col=col, buf=buf, wv2=wv2, si=idx2 % 2,
                g1s=(w1_stream_done if idx2 == 0 else 0),
                gy=(s_sty1[0].n if idx2 == 0 else 0),
                gy1=(s_sty1[1].n if idx2 == 0 else 0)):
            if g1s:
                p.wait_ge(s_po.h, g1s)
                p.wait_ge(s_sty1[0].h, gy)
                p.wait_ge(s_sty1[1].h, gy1)
            p.dma_start(out=buf[:], in_=wf2_d[:, col:col + 256]).then_inc(
                s_wst[si].h, 16)
            p.wait_ge(s_wst[si].h, wv2)
            p.tensor_copy(w2fb[:, col:col + 256], buf[:]).then_inc(s_po.h, 1)
        em("gpsimd", w2p)
        if f == FT - 1:
            wf2_cv[c2] = cv
    stream_done = cv
    s2um = sbg(g3, "s2um", [1, C3])
    r2t1 = sbg(g3, "r2t1", [1, C3])
    r2t2 = sbg(g3, "r2t2", [1, C3])

    out_sts = []
    hbanks = [mm_ps[0], mm_ps[1], bc_ps[0], bc_ps[1]]
    hid_bank_free = [(None, 0)] * 4
    w2banks = [mm_ps[2], mm_ps[3]]
    w2bank_free = [(None, 0)] * 2
    prev_stev3 = 0
    wmm_last = {}
    ev3_done = {}

    def emit_fmm(i3, ld_y, ft):
        bank = ft % 4
        sem, val = hid_bank_free[bank]
        mmv = s_mm.res()
        def fmm(t, bank=bank, ft=ft, i3=i3, sem=sem, val=val, ld_y=ld_y,
                wcv=(wf1_cv[ft // 2] if i3 == 0 else 0)):
            if ft < 2:
                t.wait_ge(s_y3[i3 % 2].h, ld_y)
            if wcv:
                t.wait_ge(s_po.h, wcv)
            if sem is not None:
                t.wait_ge(sem, val)
            for k in range(KT):
                ins = t.matmul(hbanks[bank][:, 0:C3],
                               w1fb[:, k * 4096 + ft * 128:k * 4096 + (ft + 1) * 128],
                               y3b[i3 % 2][:, k * C3:(k + 1) * C3],
                               start=(k == 0), stop=(k == KT - 1))
                if k == KT - 1:
                    ins.then_inc(s_mm.h, 1)
        em("tensor", fmm)
        return mmv

    def emit_stage_qv1(i3, ft, mmv):
        bank = ft % 4
        qi = ft % 2
        wv = s_ve.res()
        def qv1(v, n=mmv, ft=ft, qi=qi, bank=bank):
            v.wait_ge(s_mm.h, n)
            v.tensor_scalar(wq[qi][:], hbanks[bank][:, 0:C3],
                            float(0.1 * 262144.0 / TWO_PI),
                            cw[:, ft:ft + 1], OP.mult, OP.add).then_inc(s_ve.h, 1)
        em("vector", qv1)
        return wv

    def emit_stage_mid(i3, ft, sin_guard):
        qi = ft % 3
        qiw = ft % 2
        def mid(v, ft=ft, qi=qi, qiw=qiw, sg=sin_guard):
            if sg:
                v.wait_ge(s_ac.h, sg)
            v.tensor_copy(s1q[qi][:].bitcast(I32), wq[qiw][:])
            v.tensor_scalar(s1q[qi][:].bitcast(I32), s1q[qi][:].bitcast(I32),
                            262143, None, OP.bitwise_and).then_inc(s_ve.h, 1)
        em("vector", mid)
        return s_ve.res()

    def emit_stage_sin(i3, ft, c2v, aguard):
        av = s_ac.res()
        def qa(a, n=c2v, qi=ft % 3, ft=ft, i3=i3, ag=aguard):
            a.wait_ge(s_ve.h, n)
            if ag:
                a.wait_ge(s_mm.h, ag)
            a.activation(actb[i3 % 2][:, ft * C3:(ft + 1) * C3],
                         s1q[qi][:].bitcast(I32),
                         AF.Sin, scale=cst[:, 3:4],
                         bias=cst[:, 2:3]).then_inc(s_ac.h, 1)
        em("scalar", qa)
        return av

    def emit_stage_qv3(i3, ft, av):
        bank = ft % 4
        a2v0 = s_ve.res()
        a2v = s_ve.res()
        def qv3(v, n=av, bank=bank, ft=ft, i3=i3, m=a2v0):
            v.wait_ge(s_ac.h, n)
            asl = actb[i3 % 2][:, ft * C3:(ft + 1) * C3]
            v.scalar_tensor_tensor(asl, asl, alp[:, ft:ft + 1],
                                   hbanks[bank][:, 0:C3],
                                   OP.mult, OP.add).then_inc(s_ve.h, 1)
            v.wait_ge(s_ve.h, m)
            v.tensor_scalar(asl, asl, fc[:, ft:ft + 1], 0.0,
                            OP.add, OP.max).then_inc(s_ve.h, 1)
        em("vector", qv3)
        hid_bank_free[bank] = (s_ve.h, a2v)
        return a2v

    def emit_tail_piece(p_i3, p_act, ct, piece):
        bi = ct % 2
        if piece == 0:
            sem, val = w2bank_free[bi]
        else:
            sem, val = (None, 0)
        mmv = s_mm.res() if piece == 3 else 0
        def wmm(t, bi=bi, ct=ct, p_i3=p_i3, piece=piece, na=p_act,
                sem=sem, val=val,
                w2cv2=(wf2_cv[ct // 2] if (p_i3 == 0 and piece == 0) else 0)):
            if piece == 0:
                t.wait_ge(s_ve.h, na)
            if w2cv2:
                t.wait_ge(s_po.h, w2cv2)
            if sem is not None:
                t.wait_ge(sem, val)
            for f in range(piece * 8, piece * 8 + 8):
                ins = t.matmul(w2banks[bi][:, 0:C3],
                               w2fb[:, f * D + ct * 128:f * D + (ct + 1) * 128],
                               actb[p_i3 % 2][:, f * C3:(f + 1) * C3],
                               start=(f == 0), stop=(f == FT - 1))
                if f == FT - 1:
                    ins.then_inc(s_mm.h, 1)
        em("tensor", wmm)
        if piece == 3:
            wmm_last[p_i3] = mmv
        return mmv

    def emit_tail_unit(p_i3, p_ldy, p_act, ct, mmv):
        bi = ct % 2
        vv = s_ve.res()
        def ev3(v, mmv=mmv, bi=bi, ct=ct, p_i3=p_i3, p_ldy=p_ldy,
                stguard=(out_sts[p_i3 - 1][1] if (p_i3 >= 1 and ct == 0) else 0)):
            v.wait_ge(s_mm.h, mmv)
            v.wait_ge(s_y3[p_i3 % 2].h, p_ldy)
            if stguard:
                v.wait_ge(s_out.h, stguard)
            sl = slice(ct * C3, (ct + 1) * C3)
            v.scalar_tensor_tensor(r2ch[:, sl], w2banks[bi][:, 0:C3],
                                   cc[:, 24 + ct:25 + ct],
                                   y3f[p_i3 % 2][:, sl], OP.add, OP.add).then_inc(s_ve.h, 1)
        em("vector", ev3)
        w2bank_free[bi] = (s_ve.h, vv)
        ev3_done[p_i3] = vv
        av = s_ac.res()
        em("scalar", lambda a, n=vv, ct=ct: (
            a.wait_ge(s_ve.h, n),
            a.activation(sq3f[:, ct * C3:(ct + 1) * C3],
                         r2ch[:, ct * C3:(ct + 1) * C3],
                         AF.Square).then_inc(s_ac.h, 1)))
        stv = s_mm.res()
        def stmm3(t, ct=ct, vv=vv, av=av, prevv=(prev_stev3 if ct == 0 else 0)):
            if prevv:
                t.wait_ge(s_ve.h, prevv)
            t.wait_ge(s_ve.h, vv)
            t.matmul(st_ps[0][0:1, 0:C3], ones_f[:],
                     r2ch[:, ct * C3:(ct + 1) * C3],
                     start=(ct == 0), stop=(ct == KT - 1))
            t.wait_ge(s_ac.h, av)
            t.matmul(st_ps[1][0:1, 0:C3], ones_bf[:],
                     sq3f[:, ct * C3:(ct + 1) * C3],
                     start=(ct == 0), stop=(ct == KT - 1)).then_inc(s_mm.h, 1)
        em("tensor", stmm3)
        return stv

    def emit_ln2_tail(p_i3):
        nonlocal_vals = {}
        sev0 = s_ve.res(1)
        def stev3a(v, n=st3_holder[0], m=sev0,
                   sd=(stream_done if p_i3 == 0 else 0)):
            if sd:
                v.wait_ge(s_po.h, sd)
            v.wait_ge(s_mm.h, n)
            v.tensor_copy(s2um[:], st_ps[0][0:1, 0:C3])
            v.tensor_copy(r2t2[:], st_ps[1][0:1, 0:C3]).then_inc(s_ve.h, 1)
            v.wait_ge(s_ve.h, m)
        em("vector", stev3a)
        sev1 = s_ve.res(1)
        def stev3b(v, m=sev1):
            v.tensor_scalar(s2um[:], s2um[:], float(1.0 / D), None, OP.mult)
            v.tensor_scalar(r2t2[:], r2t2[:], float(1.0 / D), None,
                            OP.mult).then_inc(s_ve.h, 1)
            v.wait_ge(s_ve.h, m)
        em("vector", stev3b)
        sev = s_ve.res(1)
        def stev3c(v, m=sev):
            v.tensor_mul(r2t1[:], s2um[:], s2um[:])
            v.engine_nop()
            v.engine_nop()
            v.tensor_sub(r2t2[:], r2t2[:], r2t1[:]).then_inc(s_ve.h, 1)
            v.wait_ge(s_ve.h, m)
        em("vector", stev3c)
        wait("scalar", s_ve, sev)
        a3 = s_ac.res()
        em("scalar", lambda a: a.activation(r2t1[:], r2t2[:], AF.Sqrt,
                                            bias=cst[0:1, 1:2]).then_inc(s_ac.h, 1))
        wait("vector", s_ac, a3)
        rv0 = s_ve.res()
        em("vector", lambda v, m=rv0: (v.reciprocal(r2t2[:], r2t1[:]).then_inc(s_ve.h, 1),
                                       v.wait_ge(s_ve.h, m)))
        rv = s_ve.res()
        em("vector", lambda v: v.tensor_mul(r2t1[:], s2um[:], r2t2[:]).then_inc(s_ve.h, 1))
        bcv = s_mm.res()
        def bc3(t, n=rv):
            t.wait_ge(s_ve.h, n)
            t.matmul(st_ps[0][:, 0:C3], ones1[:], r2t2[0:1, :], start=True, stop=True)
            t.matmul(st_ps[1][:, 0:C3], ones1[:], r2t1[0:1, :],
                     start=True, stop=True).then_inc(s_mm.h, 1)
        em("tensor", bc3)
        for ct in range(KT):
            nz30 = s_ve.res()
            nzv = s_ve.res()
            def nz3(v, ct=ct, bcv=bcv, m=nz30):
                v.wait_ge(s_mm.h, bcv)
                sl = slice(ct * C3, (ct + 1) * C3)
                v.tensor_tensor(r2ch[:, sl], r2ch[:, sl], st_ps[0][:, 0:C3],
                                OP.mult).then_inc(s_ve.h, 1)
                v.wait_ge(s_ve.h, m)
                v.tensor_sub(r2ch[:, sl], r2ch[:, sl],
                             st_ps[1][:, 0:C3]).then_inc(s_ve.h, 1)
            em("vector", nz3)
        nz_all = s_ve.n
        ov = s_out.res(16)
        def sto(a, n=nz_all, p_i3=p_i3):
            a.wait_ge(s_ve.h, n)
            a.dma_start(out=out_d[:, p_i3],
                        in_=r2ch[:].rearrange("p (h j) -> p h j", h=KT)).then_inc(s_out.h, 16)
        em("scalar", sto)
        out_sts.append((nz_all, ov))
        return max(sev, nz_all)

    st3_holder = [0]
    prev = None
    for i3 in range(NC3):
        if i3 >= 2:
            wait("sync", s_ve, ev3_done[i3 - 2])
        half = i3 % 2
        dma("sync", y3b[i3 % 2][:].rearrange("p (k j) -> p k j", k=KT),
            y1b_d[:, i3 // 2, :, half * C3:(half + 1) * C3], s_y3[i3 % 2])
        ld_y = dma("sync", y3f[i3 % 2][:].rearrange("p (k j) -> p k j", k=KT),
                   y1f_d[:, i3 // 2, :, half * C3:(half + 1) * C3], s_y3[i3 % 2])
        aguard = wmm_last.get(i3 - 2, 0)
        mmvs = {}
        midv = {}
        c2v = {}
        sinv = {}
        acts = {}
        tct = 0
        for slot in range(FT + 3):
            if slot - 3 >= 0:
                acts[slot - 3] = emit_stage_qv3(i3, slot - 3, sinv[slot - 3])
            if slot < FT:
                mmvs[slot] = emit_fmm(i3, ld_y, slot)
                emit_stage_qv1(i3, slot, mmvs[slot])
            if 0 <= slot - 1 < FT:
                ft = slot - 1
                sg = sinv.get(ft - 3, 0)
                midv[ft] = emit_stage_mid(i3, ft, sg)
            if 0 <= slot - 2 < FT:
                ft = slot - 2
                sinv[ft] = emit_stage_sin(i3, ft, midv[ft], aguard)
            if prev is not None and tct < KT * 4:
                p_i3, p_ldy, p_act = prev
                pm = emit_tail_piece(p_i3, p_act, tct // 4, tct % 4)
                if tct % 4 == 3:
                    st3_holder[0] = emit_tail_unit(p_i3, p_ldy, p_act,
                                                   tct // 4, pm)
                tct += 1
        if prev is not None:
            while tct < KT * 4:
                p_i3, p_ldy, p_act = prev
                pm = emit_tail_piece(p_i3, p_act, tct // 4, tct % 4)
                if tct % 4 == 3:
                    st3_holder[0] = emit_tail_unit(p_i3, p_ldy, p_act,
                                                   tct // 4, pm)
                tct += 1
            t_ln = emit_ln2_tail(p_i3)
            prev_stev3 = t_ln
        act_last = acts[FT - 1]
        prev = (i3, ld_y, act_last)

    p_i3, p_ldy, p_act = prev
    for ct in range(KT):
        for piece in range(4):
            pm = emit_tail_piece(p_i3, p_act, ct, piece)
        st3_holder[0] = emit_tail_unit(p_i3, p_ldy, p_act, ct, pm)
    emit_ln2_tail(p_i3)

    final_out = s_out.n
    em("scalar", lambda a: a.wait_ge(s_out.h, final_out))
    em("sync", lambda e: e.wait_ge(s_out.h, final_out))

    block = es.enter_context(nc.Block())

    @block.sync
    def _(sync):
        for fn in E["sync"]:
            fn(sync)

    @block.tensor
    def _(tensor):
        for fn in E["tensor"]:
            fn(tensor)

    @block.vector
    def _(vector):
        for fn in E["vector"]:
            fn(vector)

    @block.scalar
    def _(scalar):
        for fn in E["scalar"]:
            fn(scalar)

    @block.gpsimd
    def _(gpsimd):
        for fn in E["gpsimd"]:
            fn(gpsimd)

    g3.close()
    g3a.close()
    es.close()
    return nc


def kernel(**inputs):
    ins = {k: np.asarray(v, dtype=np.float32) for k, v in inputs.items()}
    x = ins["L0_attn_x"]

    def pm(a, nt):
        X = a.shape[1]
        return np.ascontiguousarray(a.reshape(nt, 128, X).transpose(1, 0, 2).reshape(128, nt * X))

    w1t = pm(np.ascontiguousarray(ins["L0_attn_w1_w"].T), KT)
    v1t = pm(np.ascontiguousarray(ins["L0_attn_v1_w"].T), KT)
    w2t = pm(np.ascontiguousarray(ins["L0_attn_w2_w"].T), KT)
    wf1 = pm(ins["L0_ffn_W1"], KT)
    wf2 = pm(ins["L0_ffn_W2"], FT)

    def col8(a):
        return np.ascontiguousarray(a.reshape(KT, 128).T)

    def col128(a):
        return np.ascontiguousarray(a.reshape(KT, 128, 16).transpose(1, 0, 2).reshape(128, 128))

    cc = np.zeros((128, 448), np.float32)
    cc[:, 0:8] = col8(ins["L0_attn_w1_b"])
    cc[:, 8:16] = col8(ins["L0_attn_v1_b"])
    cc[:, 16:24] = col8(ins["L0_attn_w2_b"])
    cc[:, 24:32] = col8(ins["L0_ffn_b2"])
    cc[:, 32:40] = col8(ins["L0_attn_D_ssm"])
    conv_w = ins["L0_attn_conv_w"][:, 0, :]
    cc[:, 40:48] = col8(conv_w[:, 0])
    cc[:, 48:56] = col8(conv_w[:, 1])
    cc[:, 56:64] = col8(conv_w[:, 2])
    cc[:, 64:192] = col128(ins["L0_attn_A"])
    cc[:, 192:320] = col128(ins["L0_attn_B_ssm"])
    cc[:, 320:448] = col128(ins["L0_attn_C_ssm"])

    def fcol(a):
        return np.ascontiguousarray(a.reshape(FT, 128).T)

    fc = np.zeros((128, 128), np.float32)
    fc[:, 0:32] = fcol(ins["L0_ffn_b1"])
    fc[:, 32:64] = fcol(ins["L0_ffn_quantum_phases"])
    fc[:, 64:96] = fcol(ins["L0_ffn_quantum_weights_real"])
    fc[:, 96:128] = fcol(ins["L0_ffn_quantum_weights_imag"])

    cst = np.zeros((128, 8), np.float32)
    cst[:, 0] = TWO_PI
    cst[:, 1] = 1e-5
    cst[:, 2] = -np.pi
    cst[:, 3] = TWO_PI / 262144.0

    nc = build_program()

    in_maps = []
    for core in range(8):
        xs = x[2 * core:2 * core + 2]
        xF = np.ascontiguousarray(xs.transpose(2, 0, 1)).reshape(D, NT)
        xfc = np.ascontiguousarray(
            xF.reshape(KT, 128, NCH, CH).transpose(1, 2, 0, 3))
        in_maps.append({
            "xf": xfc, "w1t": w1t, "v1t": v1t, "w2t": w2t,
            "wf1": wf1, "wf2": wf2, "cc": cc, "fc": fc, "cst": cst,
        })

    res = run_bass_kernel_spmd(nc, in_maps, core_ids=list(range(8)))

    outs = []
    for core in range(8):
        o = np.asarray(res.results[core]["out"], np.float32)
        oF = o.transpose(2, 0, 1, 3).reshape(D, NT)
        o2 = oF.reshape(D, NSEQ, T).transpose(1, 2, 0)
        outs.append(o2)
    return np.ascontiguousarray(np.concatenate(outs, axis=0))


# version 69
# speedup vs baseline: 1.1459x; 1.0276x over previous
import numpy as np
import ml_dtypes
from contextlib import ExitStack
import concourse.bass as bass
import concourse.mybir as mybir
from concourse.bass_utils import run_bass_kernel_spmd

F32 = mybir.dt.float32
BF16 = mybir.dt.bfloat16
I32 = mybir.dt.int32
AF = mybir.ActivationFunctionType
OP = mybir.AluOpType

D = 1024
T = 2048
NSEQ = 2
NT = NSEQ * T
KT = 8
FT = 32
CH = 512
NCH = NT // CH
C3 = 256
NC3 = NT // C3
TWO_PI = float(2.0 * np.pi)
POOL_S = []


class Sem:
    def __init__(self, es, nc, name):
        self.h = es.enter_context(nc.semaphore(name))
        self.n = 0

    def res(self, amt=1):
        self.n += amt
        return self.n


def build_program():
    nc = bass.Bass("TRN2", target_bir_lowering=False)

    xf_d = nc.dram_tensor("xf", [128, NCH, KT, CH], F32, kind="ExternalInput").ap()
    w1t_d = nc.dram_tensor("w1t", [128, KT * D], F32, kind="ExternalInput").ap()
    v1t_d = nc.dram_tensor("v1t", [128, KT * D], F32, kind="ExternalInput").ap()
    w2t_d = nc.dram_tensor("w2t", [128, KT * D], F32, kind="ExternalInput").ap()
    wf1_d = nc.dram_tensor("wf1", [128, KT * 4096], F32, kind="ExternalInput").ap()
    wf2_d = nc.dram_tensor("wf2", [128, FT * D], F32, kind="ExternalInput").ap()
    cc_d = nc.dram_tensor("cc", [128, 448], F32, kind="ExternalInput").ap()
    fc_d = nc.dram_tensor("fc", [128, 128], F32, kind="ExternalInput").ap()
    cst_d = nc.dram_tensor("cst", [128, 8], F32, kind="ExternalInput").ap()
    out_d = nc.dram_tensor("out", [128, NC3, KT, C3], F32, kind="ExternalOutput").ap()
    xbf_d = nc.dram_tensor("xbf_d", [128, NCH, KT, CH], BF16).ap()
    gb_d = nc.dram_tensor("gb_d", [128, NCH, KT, CH], BF16).ap()
    r1f_d = nc.dram_tensor("r1f", [128, NCH, KT, CH], F32).ap()
    y1f_d = nc.dram_tensor("y1f", [128, NCH, KT, CH], F32).ap()
    y1b_d = nc.dram_tensor("y1bd", [128, NCH, KT, CH], BF16).ap()

    es = ExitStack()
    g1 = ExitStack()
    gv = ExitStack()
    gx = ExitStack()
    g2 = ExitStack()
    g2b = ExitStack()
    g2be = ExitStack()
    g2d = ExitStack()
    g3 = ExitStack()

    def sbg(g, name, shape, dt=F32):
        return g.enter_context(nc.sbuf_tensor(name, shape, dt))

    cc = sbg(es, "cc_s", [128, 64])
    fc = sbg(es, "fc_s", [128, 128])
    cst = sbg(es, "cst_s", [128, 8])
    cw = sbg(es, "cw", [128, 32])
    alp = sbg(es, "alp", [128, 32])
    beta = sbg(es, "beta", [128, 32])
    ones_bf = sbg(es, "ones_bf", [128, 1], BF16)
    ones_f = sbg(es, "ones_f", [128, 1])
    ones1 = sbg(es, "ones1", [1, 128])
    cabc = sbg(gv, "cabc", [128, 384])
    abar = sbg(gv, "abar", [128, 128])
    gg = sbg(gv, "gg", [128, 128])
    tfc1 = sbg(gv, "tfc1", [128, 32])
    tfc2 = sbg(gv, "tfc2", [128, 32])
    tfc3 = sbg(gv, "tfc3", [128, 32])
    tfci = sbg(gv, "tfci", [128, 32])
    tfcf = sbg(gv, "tfcf", [128, 32])

    v1tb = sbg(gv, "v1tb", [128, KT * D], BF16)
    xmix = sbg(gx, "xmix", [128, KT * NT], BF16)
    w1tb = sbg(g1, "w1tb", [128, KT * D], BF16)
    xstg = sbg(g1, "xstg", [128, KT * CH])
    xbfst = [sbg(g1, f"xbfst{i}", [128, KT * CH], BF16) for i in range(2)]
    convb = sbg(g1, "convb", [128, T])

    mm_ps = [es.enter_context(nc.psum_tensor(f"mmps{i}", [128, 512], F32)) for i in range(4)]
    st_ps = [es.enter_context(nc.psum_tensor(f"stps{i}", [128, 512], F32)) for i in range(2)]
    bc_ps = [es.enter_context(nc.psum_tensor(f"bcps{i}", [128, 512], F32)) for i in range(2)]

    import os
    if os.environ.get("KDBG3"):
        DBG3 = [nc.dram_tensor("dbg3r", [128, KT * C3], F32, kind="ExternalOutput").ap(),
                nc.dram_tensor("dbg3s", [1, 2 * C3], F32, kind="ExternalOutput").ap(),
                nc.dram_tensor("dbg3a", [128, FT * C3], BF16, kind="ExternalOutput").ap(),
                nc.dram_tensor("dbg3y", [128, KT * C3], BF16, kind="ExternalOutput").ap()]
    else:
        DBG3 = None
    s_ld = Sem(es, nc, "s_ld")
    s_xstg = Sem(es, nc, "s_xstg")
    s_xbv = [Sem(es, nc, f"s_xbv{i}") for i in range(2)]
    s_stg2s = [Sem(es, nc, f"s_stg2s{i}") for i in range(2)]
    s_gbc = [Sem(es, nc, f"s_gbc{i}") for i in range(2)]
    s_r1l = [Sem(es, nc, f"s_r1l{i}") for i in range(2)]
    s_y3 = [Sem(es, nc, f"s_y3{i}") for i in range(2)]
    s_wst = [Sem(es, nc, f"s_wst{i}") for i in range(2)]
    s_stxbf = [Sem(es, nc, f"s_stxbf{i}") for i in range(2)]
    s_stgb = [Sem(es, nc, f"s_stgb{i}") for i in range(2)]
    s_sty1 = [Sem(es, nc, f"s_sty1{i}") for i in range(2)]
    s_str1 = Sem(es, nc, "s_str1")
    s_st = Sem(es, nc, "s_st")
    s_out = Sem(es, nc, "s_out")
    s_mm = Sem(es, nc, "s_mm")
    s_ve = Sem(es, nc, "s_ve")
    s_ac = Sem(es, nc, "s_ac")
    s_po = Sem(es, nc, "s_po")

    E = {k: [] for k in ("sync", "tensor", "vector", "scalar", "gpsimd")}

    def em(eng, fn):
        E[eng].append(fn)

    def wait(eng, sem, val):
        if val > 0:
            em(eng, lambda e, s=sem.h, v=val: e.wait_ge(s, v))

    def dma(eng, out, in_, sem):
        v = sem.res(16)
        em(eng, lambda e, o=out, i=in_, s=sem.h: e.dma_start(out=o, in_=i).then_inc(s, 16))
        return v

    dma("sync", cc[:], cc_d[:, 0:64], s_ld)
    dma("sync", cabc[:], cc_d[:, 64:448], s_ld)
    dma("sync", fc[:], fc_d, s_ld)
    ld_consts = dma("sync", cst[:], cst_d, s_ld)

    ve_wconv = 0
    for wd, wb in ((w1t_d, w1tb), (v1t_d, v1tb)):
        for piece in range(2):
            ldv = dma("sync", xstg[:],
                      wd[:, piece * 4 * D:(piece + 1) * 4 * D], s_xstg)
            wait("vector", s_xstg, ldv)
            cv = s_ve.res()
            em("vector", lambda v, dst=wb, piece=piece: v.tensor_copy(
                dst[:, piece * 4 * D:(piece + 1) * 4 * D], xstg[:]).then_inc(s_ve.h, 1))
            wait("sync", s_ve, cv)
            ve_wconv = cv

    wait("scalar", s_ld, ld_consts)
    wait("vector", s_ld, ld_consts)
    em("scalar", lambda a: a.activation(abar[:], cabc[:, 0:128], AF.Sigmoid).then_inc(s_ac.h, 1))
    s_ac.res()

    def emv(fn):
        v0 = s_ve.res()
        em("vector", lambda e, f=fn: f(e).then_inc(s_ve.h, 1))
        wait("vector", s_ve, v0)
        return v0

    emv(lambda v: v.tensor_mul(gg[:], cabc[:, 128:256], cabc[:, 256:384]))

    TH = fc[:, 32:64]
    WR = fc[:, 64:96]
    WI = fc[:, 96:128]
    B1C = fc[:, 0:32]

    def trig(dst, shift):
        emv(lambda v, sh=shift: v.tensor_scalar(
            tfc3[:], TH, float(262144.0 / TWO_PI),
            float((64.5 + sh / TWO_PI) * 262144.0), OP.mult, OP.add))
        emv(lambda v: v.tensor_copy(tfci[:].bitcast(I32), tfc3[:]))
        emv(lambda v: v.tensor_scalar(tfci[:].bitcast(I32), tfci[:].bitcast(I32),
                                      262143, None, OP.bitwise_and))
        dv = emv(lambda v: v.tensor_copy(tfcf[:], tfci[:].bitcast(I32)))
        wait("scalar", s_ve, dv)
        av = s_ac.res()
        em("scalar", lambda a, d2=dst: a.activation(
            d2, tfcf[:], AF.Sin, scale=cst[:, 3:4],
            bias=cst[:, 2:3]).then_inc(s_ac.h, 1))
        wait("vector", s_ac, av)
        return av

    trig(tfc1[:], 0.0)
    actrig = trig(tfc2[:], float(np.pi / 2.0))
    wait("vector", s_ac, actrig)
    emv(lambda v: v.tensor_mul(tfc3[:], tfc2[:], WR))
    emv(lambda v: v.tensor_mul(tfc1[:], tfc1[:], WI))
    emv(lambda v: v.tensor_add(tfc3[:], tfc3[:], tfc1[:]))
    emv(lambda v: v.tensor_mul(tfc3[:], tfc3[:], WI))
    emv(lambda v: v.tensor_scalar_mul(alp[:], tfc3[:], 0.1))
    emv(lambda v: v.tensor_scalar(tfc2[:], TH, float(262144.0 / TWO_PI),
                                  float(64.5 * 262144.0), OP.mult, OP.add))
    emv(lambda v: v.scalar_tensor_tensor(
        cw[:], B1C, float(0.1 * 262144.0 / TWO_PI), tfc2[:], OP.mult, OP.add))
    emv(lambda v: v.scalar_tensor_tensor(
        beta[:], B1C, float(-0.1 * 262144.0 / TWO_PI), cw[:], OP.mult, OP.add))
    emv(lambda v: v.tensor_scalar(beta[:], beta[:],
                                  float(-TWO_PI / (0.1 * 262144.0)), None, OP.mult))
    emv(lambda v: v.memset(ones_bf[:], 1.0))
    emv(lambda v: v.memset(ones_f[:], 1.0))
    emv(lambda v: v.memset(ones1[:], 1.0))

    bank_free = {b: (None, 0) for b in range(4)}
    xcvt = []
    xbf_store = []
    x1_evac = {}
    for ch in range(NCH):
        if ch >= 1:
            wait("sync", s_po, xcvt[ch - 1])
        else:
            wait("sync", s_ve, ve_wconv)
        ldv = dma("sync", xstg[:], xf_d[:, ch], s_xstg)
        if ch >= 2:
            wait("gpsimd", s_ac, x1_evac[(KT - 1, ch - 2)])
            wait("gpsimd", s_stxbf[ch % 2], xbf_store[ch - 2])
        wait("gpsimd", s_xstg, ldv)
        pv = s_po.res()
        em("gpsimd", lambda p, ch=ch: p.tensor_copy(
            xbfst[ch % 2][:], xstg[:]).then_inc(s_po.h, 1))
        xcvt.append(pv)
        wait("sync", s_po, pv)
        sv = dma("sync", xbf_d[:, ch], xbfst[ch % 2][:].rearrange("p (k j) -> p k j", k=KT),
                 s_stxbf[ch % 2])
        xbf_store.append(sv)

        for ct in range(KT):
            bank = ct % 2
            sem, val = bank_free[bank]
            mmv = s_mm.res()
            def f1(t, bank=bank, ct=ct, ch=ch, sem=sem, val=val,
                   pv=pv, wcv=(ve_wconv if (ch == 0 and ct == 0) else 0)):
                if ct == 0:
                    t.wait_ge(s_po.h, pv)
                if wcv:
                    t.wait_ge(s_ve.h, wcv)
                if sem is not None:
                    t.wait_ge(sem, val)
                for k in range(KT):
                    ins = t.matmul(mm_ps[bank][:, 0:CH],
                                   w1tb[:, k * D + ct * 128:k * D + (ct + 1) * 128],
                                   xbfst[ch % 2][:, k * CH:(k + 1) * CH],
                                   start=(k == 0), stop=(k == KT - 1))
                    if k == KT - 1:
                        ins.then_inc(s_mm.h, 1)
            em("tensor", f1)
            av = s_ac.res()
            def ev(a, mmv=mmv, bank=bank, ct=ct, ch=ch):
                a.wait_ge(s_mm.h, mmv)
                a.add(xmix[:, ct * NT + ch * CH:ct * NT + (ch + 1) * CH],
                      mm_ps[bank][:, 0:CH], cc[:, ct:ct + 1]).then_inc(s_ac.h, 1)
            em("scalar", ev)
            bank_free[bank] = (s_ac.h, av)
            x1_evac[(ct, ch)] = av
    mm_phase1 = s_mm.n

    xc_done_ct = {}
    for ct in range(KT):
        for seq in range(NSEQ):
            need = x1_evac[(ct, (seq + 1) * (NCH // NSEQ) - 1)]
            cv0 = s_ve.res()
            cv1 = s_ve.res()
            cv = s_ve.res()
            def cvf(v, need=need, seq=seq, ct=ct, m0=cv0, m1=cv1):
                base = ct * NT + seq * T
                v.wait_ge(s_ac.h, need)
                v.tensor_scalar(convb[:], xmix[:, base:base + T],
                                cc[:, 48 + ct:49 + ct], None, OP.mult).then_inc(s_ve.h, 1)
                v.wait_ge(s_ve.h, m0)
                v.scalar_tensor_tensor(convb[:, 1:T], xmix[:, base:base + T - 1],
                                       cc[:, 40 + ct:41 + ct], convb[:, 1:T],
                                       OP.mult, OP.add).then_inc(s_ve.h, 1)
                v.wait_ge(s_ve.h, m1)
                v.scalar_tensor_tensor(convb[:, 0:T - 1], xmix[:, base + 1:base + T],
                                       cc[:, 56 + ct:57 + ct], convb[:, 0:T - 1],
                                       OP.mult, OP.add).then_inc(s_ve.h, 1)
            em("vector", cvf)
            av = s_ac.res()
            def slf(a, cv=cv, seq=seq, ct=ct):
                a.wait_ge(s_ve.h, cv)
                a.activation(xmix[:, ct * NT + seq * T:ct * NT + (seq + 1) * T],
                             convb[:], AF.Silu).then_inc(s_ac.h, 1)
            em("scalar", slf)
            wait("vector", s_ac, av)
        xc_done_ct[ct] = s_ac.n
    g1.close()

    hbuf = sbg(g2, "hbuf", [128, T])
    phbuf = [sbg(g2, f"phbuf{i}", [128, T]) for i in range(2)]
    ydve = sbg(g2, "ydve", [128, T])
    ypool = sbg(g2, "ypool", [128, T])
    ptmp = sbg(g2, "ptmp", [128, T])
    xbfv = [sbg(g2, f"xbfv{i}", [128, KT * CH], BF16) for i in range(2)]
    vtile = [sbg(g2, f"vtile{i}", [128, NT], BF16) for i in range(2)]

    wait("vector", s_mm, mm_phase1)
    wait("gpsimd", s_mm, mm_phase1)
    wait("tensor", s_ac, xc_done_ct[KT - 1])

    pb_guard = [0, 0]
    pool_cnt = 0
    v_done_ct = {}
    gb_store = []
    for ct in range(KT):
        for ch in range(NCH):
            gi = (ct * NCH + ch) % 2
            if ct * NCH + ch >= 2:
                prev = v_done_ct.get((ct * NCH + ch - 2))
                wait("sync", s_ac, prev)
            else:
                wait("sync", s_ac, xc_done_ct[KT - 1])
            if ct == 0 and ch == 0:
                wait("sync", s_stxbf[0], s_stxbf[0].n)
                wait("sync", s_stxbf[1], s_stxbf[1].n)
            ldv = dma("sync", xbfv[gi][:].rearrange("p (k j) -> p k j", k=KT),
                      xbf_d[:, ch], s_xbv[gi])
            bank = ch % 2
            sem, val = bank_free[bank]
            mmv = s_mm.res()
            def fv(t, bank=bank, ct=ct, ch=ch, gi=gi, sem=sem, val=val, ldv=ldv):
                t.wait_ge(s_xbv[gi].h, ldv)
                if sem is not None:
                    t.wait_ge(sem, val)
                for k in range(KT):
                    ins = t.matmul(mm_ps[bank][:, 0:CH],
                                   v1tb[:, k * D + ct * 128:k * D + (ct + 1) * 128],
                                   xbfv[gi][:, k * CH:(k + 1) * CH],
                                   start=(k == 0), stop=(k == KT - 1))
                    if k == KT - 1:
                        ins.then_inc(s_mm.h, 1)
            em("tensor", fv)
            av = s_ac.res()
            def evv(a, mmv=mmv, bank=bank, ct=ct, ch=ch,
                    stg=(gb_store[ct - 2] if (ct >= 2 and ch == 0) else 0)):
                a.wait_ge(s_mm.h, mmv)
                if stg:
                    a.wait_ge(s_stgb[ct % 2].h, stg)
                a.activation(vtile[ct % 2][:, ch * CH:(ch + 1) * CH],
                             mm_ps[bank][:, 0:CH], AF.Silu,
                             bias=cc[:, 8 + ct:9 + ct]).then_inc(s_ac.h, 1)
            em("scalar", evv)
            bank_free[bank] = (s_ac.h, av)
            v_done_ct[ct * NCH + ch] = av
        v_ready = v_done_ct[ct * NCH + NCH - 1]

        wait("vector", s_ac, xc_done_ct[ct])
        wait("gpsimd", s_ac, xc_done_ct[ct])
        for seq in range(NSEQ):
            src = xmix[:, ct * NT + seq * T:ct * NT + (seq + 1) * T]
            first_dve = True
            first_pool = True
            for s in range(16):
                col = ct * 16 + s
                if s not in POOL_S:
                    sc0 = s_ve.res()
                    em("vector", lambda v, col=col, src=src, m=sc0: (
                        v.tensor_tensor_scan(
                            hbuf[:], abar[:, col:col + 1].broadcast_to([128, T]), src,
                            0.0, OP.mult, OP.add).then_inc(s_ve.h, 1),
                        v.wait_ge(s_ve.h, m)))
                    if first_dve:
                        em("vector", lambda v, col=col: v.tensor_scalar(
                            ydve[:], hbuf[:], gg[:, col:col + 1], None, OP.mult))
                        first_dve = False
                    else:
                        em("vector", lambda v, col=col: v.scalar_tensor_tensor(
                            ydve[:], hbuf[:], gg[:, col:col + 1], ydve[:], OP.mult, OP.add))
                else:
                    pi = pool_cnt % 2
                    pool_cnt += 1
                    if pb_guard[pi]:
                        wait("vector", s_po, pb_guard[pi])
                    sv = s_ve.res()
                    em("vector", lambda v, col=col, src=src, pi=pi: v.tensor_tensor_scan(
                        phbuf[pi][:], abar[:, col:col + 1].broadcast_to([128, T]), src,
                        0.0, OP.mult, OP.add).then_inc(s_ve.h, 1))
                    pv = s_po.res()
                    if first_pool:
                        def pf(p, n=sv, pi=pi, col=col):
                            p.wait_ge(s_ve.h, n)
                            p.tensor_tensor(ypool[:], phbuf[pi][:],
                                            gg[:, col:col + 1].broadcast_to([128, T]),
                                            OP.mult).then_inc(s_po.h, 1)
                        first_pool = False
                    else:
                        def pf(p, n=sv, pi=pi, col=col):
                            p.wait_ge(s_ve.h, n)
                            p.tensor_tensor(ptmp[:], phbuf[pi][:],
                                            gg[:, col:col + 1].broadcast_to([128, T]), OP.mult)
                            p.tensor_tensor(ypool[:], ypool[:], ptmp[:],
                                            OP.add).then_inc(s_po.h, 1)
                    em("gpsimd", pf)
                    pb_guard[pi] = pv
            dt0 = s_ve.res()
            em("vector", lambda v, ct=ct, src=src, m=dt0: (
                v.scalar_tensor_tensor(
                    ydve[:], src, cc[:, 32 + ct:33 + ct], ydve[:],
                    OP.mult, OP.add).then_inc(s_ve.h, 1),
                v.wait_ge(s_ve.h, m)))
            if POOL_S:
                wait("vector", s_po, s_po.n)
                dt1 = s_ve.res()
                em("vector", lambda v, m=dt1: (
                    v.tensor_add(ydve[:], ydve[:], ypool[:]).then_inc(s_ve.h, 1),
                    v.wait_ge(s_ve.h, m)))
            wait("vector", s_ac, v_ready)
            gvv = s_ve.res()
            em("vector", lambda v, ct=ct, seq=seq: v.tensor_tensor(
                vtile[ct % 2][:, seq * T:(seq + 1) * T], ydve[:],
                vtile[ct % 2][:, seq * T:(seq + 1) * T],
                OP.mult).then_inc(s_ve.h, 1))
            wait("gpsimd", s_ve, gvv)
        gate_ct = s_ve.n
        wait("sync", s_ve, gate_ct)
        sv2 = dma("sync", gb_d[:, :, ct, :],
                  vtile[ct % 2][:].rearrange("p (h j) -> p h j", h=NCH), s_stgb[ct % 2])
        gb_store.append(sv2)
        if ct >= 1:
            pass
    gate_all = s_ve.n
    g2.close()
    gx.close()
    gv.close()

    ssum = sbg(g2b, "ssum", [1, NT])
    ssq = sbg(g2b, "ssq", [1, NT])
    rtmp1 = sbg(g2b, "rtmp1", [1, NT])
    istd_row = sbg(g2b, "istd_row", [1, NT])
    mi_row = sbg(g2b, "mi_row", [1, NT])
    r1ch = [sbg(g2b, f"r1ch{i}", [128, KT * CH]) for i in range(2)]
    r1bch = [sbg(g2b, f"r1bch{i}", [128, KT * CH], BF16) for i in range(2)]
    w2tb = sbg(g2be, "w2tb", [128, KT * D], BF16)
    stg2 = [sbg(g2be, f"stg2{i}", [128, KT * CH]) for i in range(2)]
    gbch = [sbg(g2be, f"gbch{i}", [128, KT * CH], BF16) for i in range(2)]
    sqch = sbg(g2be, "sqch", [128, KT * CH], BF16)

    for par in range(2):
        wait("vector", s_stgb[par], s_stgb[par].n)
        wait("gpsimd", s_stgb[par], s_stgb[par].n)
        wait("sync", s_stgb[par], s_stgb[par].n)
    w2cv = 0
    for piece in range(2):
        ldv = dma("sync", stg2[piece][:, 0:4 * D],
                  w2t_d[:, piece * 4 * D:(piece + 1) * 4 * D], s_stg2s[piece])
        wait("vector", s_stg2s[piece], ldv)
        cv = s_ve.res()
        em("vector", lambda v, piece=piece: v.tensor_copy(
            w2tb[:, piece * 4 * D:(piece + 1) * 4 * D],
            stg2[piece][:, 0:4 * D]).then_inc(s_ve.h, 1))
        w2cv = cv

    bank_free = {b: (None, 0) for b in range(4)}
    wait("tensor", s_ve, w2cv)
    stat_evac = []
    r1_store = []
    gb_loads = []
    for ch in range(NCH):
        if ch >= 2:
            wait("sync", s_str1, r1_store[ch - 2])
        else:
            wait("sync", s_ve, w2cv)
        ld_x = dma("sync", stg2[ch % 2][:], xf_d[:, ch], s_stg2s[ch % 2])
        if ch >= 2:
            wait("sync", s_ve, gb_loads[ch - 2])
        ld_g = dma("sync", gbch[ch % 2][:], gb_d[:, ch], s_gbc[ch % 2])
        for ct in range(KT):
            bank = ct % 2
            sem, val = bank_free[bank]
            mmv = s_mm.res()
            def f(t, bank=bank, ct=ct, ch=ch, sem=sem, val=val, ld_g=ld_g):
                if ct == 0:
                    t.wait_ge(s_gbc[ch % 2].h, ld_g)
                if sem is not None:
                    t.wait_ge(sem, val)
                for k in range(KT):
                    ins = t.matmul(mm_ps[bank][:, 0:CH],
                                   w2tb[:, k * D + ct * 128:k * D + (ct + 1) * 128],
                                   gbch[ch % 2][:, k * CH:(k + 1) * CH],
                                   start=(k == 0), stop=(k == KT - 1))
                    if k == KT - 1:
                        ins.then_inc(s_mm.h, 1)
            em("tensor", f)
            vv = s_ve.res()
            def ev(v, mmv=mmv, bank=bank, ct=ct, ch=ch, ld_x=ld_x):
                v.wait_ge(s_mm.h, mmv)
                v.wait_ge(s_stg2s[ch % 2].h, ld_x)
                v.scalar_tensor_tensor(
                    r1ch[ch % 2][:, ct * CH:(ct + 1) * CH],
                    mm_ps[bank][:, 0:CH], cc[:, 16 + ct:17 + ct],
                    stg2[ch % 2][:, ct * CH:(ct + 1) * CH],
                    OP.add, OP.add).then_inc(s_ve.h, 1)
            em("vector", ev)
            bank_free[bank] = (s_ve.h, vv)
            bv = s_ve.res()
            em("vector", lambda v, ct=ct, ch=ch: v.tensor_copy(
                r1bch[ch % 2][:, ct * CH:(ct + 1) * CH],
                r1ch[ch % 2][:, ct * CH:(ct + 1) * CH]).then_inc(s_ve.h, 1))
            av = s_ac.res()
            em("scalar", lambda a, n=bv, ct=ct, ch=ch: (
                a.wait_ge(s_ve.h, n),
                a.activation(sqch[:, ct * CH:(ct + 1) * CH],
                             r1bch[ch % 2][:, ct * CH:(ct + 1) * CH],
                             AF.Square).then_inc(s_ac.h, 1)))
            sq_last = av
        gb_loads.append(s_ve.n)
        stv = s_mm.res()
        def stmm_b(t, ch=ch, av=sq_last,
                   prev=(stat_evac[ch - 1] if ch > 0 else 0)):
            if prev:
                t.wait_ge(s_ve.h, prev)
            t.wait_ge(s_ac.h, av)
            for ct2 in range(KT):
                t.matmul(st_ps[0][0:1, 0:CH], ones_bf[:],
                         r1bch[ch % 2][:, ct2 * CH:(ct2 + 1) * CH],
                         start=(ct2 == 0), stop=(ct2 == KT - 1))
                ins = t.matmul(st_ps[1][0:1, 0:CH], ones_bf[:],
                               sqch[:, ct2 * CH:(ct2 + 1) * CH],
                               start=(ct2 == 0), stop=(ct2 == KT - 1))
                if ct2 == KT - 1:
                    ins.then_inc(s_mm.h, 1)
        em("tensor", stmm_b)
        st_done = stv
        sev = s_ve.res()
        def stev(v, ch=ch, n=st_done):
            v.wait_ge(s_mm.h, n)
            v.tensor_copy(ssum[0:1, ch * CH:(ch + 1) * CH], st_ps[0][0:1, 0:CH])
            v.tensor_copy(ssq[0:1, ch * CH:(ch + 1) * CH],
                          st_ps[1][0:1, 0:CH]).then_inc(s_ve.h, 1)
        em("vector", stev)
        stat_evac.append(sev)
        if ch < NCH - 2:
            wait("sync", s_ve, sev)
            stv2 = dma("sync", r1f_d[:, ch],
                       r1ch[ch % 2][:].rearrange("p (k j) -> p k j", k=KT), s_str1)
            r1_store.append(stv2)

    wait("vector", s_ve, stat_evac[-1])
    r0 = s_ve.res()
    em("vector", lambda v, m=r0: (v.tensor_scalar(ssum[:], ssum[:], float(1.0 / D),
                                                  None, OP.mult).then_inc(s_ve.h, 1),
                                  v.wait_ge(s_ve.h, m)))
    r1v = s_ve.res()
    em("vector", lambda v, m=r1v: (v.tensor_mul(rtmp1[:], ssum[:], ssum[:]).then_inc(s_ve.h, 1),
                                   v.wait_ge(s_ve.h, m)))
    r2v = s_ve.res()
    em("vector", lambda v, m=r2v: (v.tensor_scalar(ssq[:], ssq[:], float(1.0 / D),
                                                   None, OP.mult).then_inc(s_ve.h, 1),
                                   v.wait_ge(s_ve.h, m)))
    varv = s_ve.res()
    em("vector", lambda v, m=varv: (v.tensor_sub(ssq[:], ssq[:], rtmp1[:]).then_inc(s_ve.h, 1),
                                    v.wait_ge(s_ve.h, m)))
    wait("scalar", s_ve, varv)
    sqv = s_ac.res()
    em("scalar", lambda a: a.activation(rtmp1[:], ssq[:], AF.Sqrt,
                                        bias=cst[0:1, 1:2]).then_inc(s_ac.h, 1))
    wait("vector", s_ac, sqv)
    r3v = s_ve.res()
    em("vector", lambda v, m=r3v: (v.reciprocal(istd_row[:], rtmp1[:]).then_inc(s_ve.h, 1),
                                   v.wait_ge(s_ve.h, m)))
    ln1_rows = s_ve.res()
    em("vector", lambda v: v.tensor_mul(mi_row[:], ssum[:], istd_row[:]).then_inc(s_ve.h, 1))

    g2be.close()
    y1ch = [sbg(g2d, f"y1ch{i}", [128, KT * CH]) for i in range(2)]
    y1bch = [sbg(g2d, f"y1bch{i}", [128, KT * CH], BF16) for i in range(2)]

    wait("tensor", s_ve, ln1_rows)
    y1_store = []
    nz_last = {}
    for i, ch in enumerate([6, 7, 0, 1, 2, 3, 4, 5]):
        if ch >= NCH - 2:
            ld_r1 = 0
        else:
            if i < 4:
                wait("sync", s_ve, nz_last[6 if ch == 0 else 7])
            else:
                wait("sync", s_sty1[i % 2], y1_store[i - 2])
            ld_r1 = dma("sync", r1ch[ch % 2][:].rearrange("p (h j) -> p h j", h=KT),
                        r1f_d[:, ch], s_r1l[ch % 2])
        bcv = s_mm.res()
        def bc(t, ch=ch):
            t.matmul(bc_ps[0][:, 0:CH], ones1[:], istd_row[0:1, ch * CH:(ch + 1) * CH],
                     start=True, stop=True)
            t.matmul(bc_ps[1][:, 0:CH], ones1[:], mi_row[0:1, ch * CH:(ch + 1) * CH],
                     start=True, stop=True).then_inc(s_mm.h, 1)
        em("tensor", bc)
        for ct in range(KT):
            nz0 = s_ve.res()
            nzv = s_ve.res()
            def nz(v, ct=ct, ch=ch, bcv=bcv, ld_r1=ld_r1, m=nz0):
                v.wait_ge(s_mm.h, bcv)
                if ld_r1:
                    v.wait_ge(s_r1l[ch % 2].h, ld_r1)
                sl = slice(ct * CH, (ct + 1) * CH)
                v.tensor_tensor(y1ch[ch % 2][:, sl], r1ch[ch % 2][:, sl],
                                bc_ps[0][:, 0:CH], OP.mult).then_inc(s_ve.h, 1)
                v.wait_ge(s_ve.h, m)
                v.tensor_sub(y1ch[ch % 2][:, sl], y1ch[ch % 2][:, sl],
                             bc_ps[1][:, 0:CH]).then_inc(s_ve.h, 1)
            em("vector", nz)
            pv = s_ve.res()
            em("vector", lambda v, ct=ct, ch=ch: v.tensor_copy(
                y1bch[ch % 2][:, ct * CH:(ct + 1) * CH],
                y1ch[ch % 2][:, ct * CH:(ct + 1) * CH]).then_inc(s_ve.h, 1))
        nz_last[ch] = s_ve.n
        wait("tensor", s_ve, s_ve.n)
        wait("sync", s_ve, s_ve.n)
        dma("sync", y1f_d[:, ch], y1ch[ch % 2][:].rearrange("p (h j) -> p h j", h=KT),
            s_sty1[ch % 2])
        wait("sync", s_ve, s_ve.n)
        yv = dma("sync", y1b_d[:, ch], y1bch[ch % 2][:].rearrange("p (h j) -> p h j", h=KT),
                 s_sty1[ch % 2])
        y1_store.append(yv)
    g2d.close()
    g2b.close()

    g3w = ExitStack()
    w1fb = sbg(g3, "w1fb", [128, KT * 4096], BF16)
    w2fb = sbg(g3, "w2fb", [128, FT * D], BF16)
    y3b = [sbg(g3, f"y3b{i}", [128, KT * C3], BF16) for i in range(2)]
    y3f = [sbg(g3, f"y3f{i}", [128, KT * C3]) for i in range(2)]
    actb = [sbg(g3, f"actb{i}", [128, FT * C3], BF16) for i in range(2)]
    s1q = [sbg(g3, f"s1q{i}", [128, C3]) for i in range(3)]
    r2ch = sbg(g3, "r2ch", [128, KT * C3])
    sq3f = sbg(g3, "sq3f", [128, KT * C3], BF16)
    wstg3 = [sbg(g3w, f"wstg3{i}", [128, 256]) for i in range(4)]

    wait("sync", s_sty1[0], s_sty1[0].n)
    wait("sync", s_sty1[1], s_sty1[1].n)
    wf_ready = []
    wf1_cv = {}
    wf2_cv = {}
    idx = 0
    pieces = []
    for f2 in range(16):
        for k in range(KT):
            pieces.append(("w1", f2, k * 4096 + f2 * 256))
    for c2 in range(4):
        for f in range(FT):
            pieces.append(("w2", c2, f * D + c2 * 256))
    for idx, (kind, grp, col) in enumerate(pieces):
        buf = wstg3[idx % 4]
        if idx >= 4:
            wait("sync", s_po, wf_ready[idx - 4])
        src_d = wf1_d if kind == "w1" else wf2_d
        dst = w1fb if kind == "w1" else w2fb
        ldv = dma("sync", buf[:], src_d[:, col:col + 256], s_wst[idx % 2])
        wait("gpsimd", s_wst[idx % 2], ldv)

        if idx == 0:
            wait("gpsimd", s_sty1[0], s_sty1[0].n)
            wait("gpsimd", s_sty1[1], s_sty1[1].n)
        cv = s_po.res()
        em("gpsimd", lambda p, col=col, buf=buf, dst=dst: p.tensor_copy(
            dst[:, col:col + 256], buf[:]).then_inc(s_po.h, 1))
        wf_ready.append(cv)
        if kind == "w1" and idx % KT == KT - 1:
            wf1_cv[grp] = cv
        if kind == "w2" and (idx - 128) % FT == FT - 1:
            wf2_cv[grp] = cv
    stream_done = wf_ready[-1]
    w2_all_cv = 0
    w1_all_cv = 0
    g3w.close()
    s2um = sbg(g3, "s2um", [1, C3])
    r2t1 = sbg(g3, "r2t1", [1, C3])
    r2t2 = sbg(g3, "r2t2", [1, C3])

    out_sts = []
    hbanks = [mm_ps[0], mm_ps[1], bc_ps[0], bc_ps[1]]
    hid_bank_free = [(None, 0)] * 4
    w2banks = [mm_ps[2], mm_ps[3]]
    w2bank_free = [(None, 0)] * 2
    prev_stev3 = 0
    wmm_last = {}
    ev3_done = {}

    def emit_fmm(i3, ld_y, ft):
        bank = ft % 4
        sem, val = hid_bank_free[bank]
        mmv = s_mm.res()
        def fmm(t, bank=bank, ft=ft, i3=i3, sem=sem, val=val, ld_y=ld_y,
                wcv=(wf1_cv[ft // 2] if i3 == 0 else 0)):
            if ft < 2:
                t.wait_ge(s_y3[i3 % 2].h, ld_y)
            if wcv:
                t.wait_ge(s_po.h, wcv)
            if sem is not None:
                t.wait_ge(sem, val)
            for k in range(KT):
                ins = t.matmul(hbanks[bank][:, 0:C3],
                               w1fb[:, k * 4096 + ft * 128:k * 4096 + (ft + 1) * 128],
                               y3b[i3 % 2][:, k * C3:(k + 1) * C3],
                               start=(k == 0), stop=(k == KT - 1))
                if k == KT - 1:
                    ins.then_inc(s_mm.h, 1)
        em("tensor", fmm)
        return mmv

    def emit_stage_qv1(i3, ft, mmv, sin_guard):
        bank = ft % 4
        qi = ft % 3
        wv = s_ve.res()
        def qv1(v, n=mmv, ft=ft, qi=qi, bank=bank, sg=sin_guard):
            v.wait_ge(s_mm.h, n)
            if sg:
                v.wait_ge(s_ac.h, sg)
            v.tensor_scalar(s1q[qi][:].bitcast(I32), hbanks[bank][:, 0:C3],
                            float(0.1 * 262144.0 / TWO_PI),
                            cw[:, ft:ft + 1], OP.mult, OP.add).then_inc(s_ve.h, 1)
        em("vector", qv1)
        return wv

    def emit_stage_mid(i3, ft):
        qi = ft % 3
        def mid(v, ft=ft, qi=qi):
            v.tensor_scalar(s1q[qi][:].bitcast(I32), s1q[qi][:].bitcast(I32),
                            262143, None, OP.bitwise_and).then_inc(s_ve.h, 1)
        em("vector", mid)
        return s_ve.res()

    def emit_stage_sin(i3, ft, c2v, aguard):
        av = s_ac.res()
        def qa(a, n=c2v, qi=ft % 3, ft=ft, i3=i3, ag=aguard):
            a.wait_ge(s_ve.h, n)
            if ag:
                a.wait_ge(s_mm.h, ag)
            a.activation(actb[i3 % 2][:, ft * C3:(ft + 1) * C3],
                         s1q[qi][:].bitcast(I32),
                         AF.Sin, scale=cst[:, 3:4],
                         bias=cst[:, 2:3]).then_inc(s_ac.h, 1)
        em("scalar", qa)
        return av

    def emit_stage_qv3(i3, ft, av):
        bank = ft % 4
        a2v0 = s_ve.res()
        a2v = s_ve.res()
        def qv3(v, n=av, bank=bank, ft=ft, i3=i3, m=a2v0):
            v.wait_ge(s_ac.h, n)
            asl = actb[i3 % 2][:, ft * C3:(ft + 1) * C3]
            v.scalar_tensor_tensor(asl, asl, alp[:, ft:ft + 1],
                                   hbanks[bank][:, 0:C3],
                                   OP.mult, OP.add).then_inc(s_ve.h, 1)
            v.wait_ge(s_ve.h, m)
            v.tensor_scalar(asl, asl, fc[:, ft:ft + 1], 0.0,
                            OP.add, OP.max).then_inc(s_ve.h, 1)
        em("vector", qv3)
        hid_bank_free[bank] = (s_ve.h, a2v)
        return a2v

    def emit_tail_piece(p_i3, p_act, ct, piece):
        bi = ct % 2
        if piece == 0:
            sem, val = w2bank_free[bi]
        else:
            sem, val = (None, 0)
        mmv = s_mm.res() if piece == 3 else 0
        def wmm(t, bi=bi, ct=ct, p_i3=p_i3, piece=piece, na=p_act,
                sem=sem, val=val,
                w2cv2=(wf2_cv[ct // 2] if (p_i3 == 0 and piece == 0) else 0)):
            if piece == 0:
                t.wait_ge(s_ve.h, na)
            if w2cv2:
                t.wait_ge(s_po.h, w2cv2)
            if sem is not None:
                t.wait_ge(sem, val)
            for f in range(piece * 8, piece * 8 + 8):
                ins = t.matmul(w2banks[bi][:, 0:C3],
                               w2fb[:, f * D + ct * 128:f * D + (ct + 1) * 128],
                               actb[p_i3 % 2][:, f * C3:(f + 1) * C3],
                               start=(f == 0), stop=(f == FT - 1))
                if f == FT - 1:
                    ins.then_inc(s_mm.h, 1)
        em("tensor", wmm)
        if piece == 3:
            wmm_last[p_i3] = mmv
        return mmv

    def emit_tail_unit(p_i3, p_ldy, p_act, ct, mmv):
        bi = ct % 2
        vv = s_ve.res()
        def ev3(v, mmv=mmv, bi=bi, ct=ct, p_i3=p_i3, p_ldy=p_ldy,
                stguard=(out_sts[p_i3 - 1][1] if (p_i3 >= 1 and ct == 0) else 0)):
            v.wait_ge(s_mm.h, mmv)
            v.wait_ge(s_y3[p_i3 % 2].h, p_ldy)
            if stguard:
                v.wait_ge(s_out.h, stguard)
            sl = slice(ct * C3, (ct + 1) * C3)
            v.scalar_tensor_tensor(r2ch[:, sl], w2banks[bi][:, 0:C3],
                                   cc[:, 24 + ct:25 + ct],
                                   y3f[p_i3 % 2][:, sl], OP.add, OP.add).then_inc(s_ve.h, 1)
        em("vector", ev3)
        w2bank_free[bi] = (s_ve.h, vv)
        ev3_done[p_i3] = vv
        av = s_ac.res()
        em("scalar", lambda a, n=vv, ct=ct: (
            a.wait_ge(s_ve.h, n),
            a.activation(sq3f[:, ct * C3:(ct + 1) * C3],
                         r2ch[:, ct * C3:(ct + 1) * C3],
                         AF.Square).then_inc(s_ac.h, 1)))
        stv = s_mm.res()
        def stmm3(t, ct=ct, vv=vv, av=av, prevv=(prev_stev3 if ct == 0 else 0)):
            if prevv:
                t.wait_ge(s_ve.h, prevv)
            t.wait_ge(s_ve.h, vv)
            t.matmul(st_ps[0][0:1, 0:C3], ones_f[:],
                     r2ch[:, ct * C3:(ct + 1) * C3],
                     start=(ct == 0), stop=(ct == KT - 1))
            t.wait_ge(s_ac.h, av)
            t.matmul(st_ps[1][0:1, 0:C3], ones_bf[:],
                     sq3f[:, ct * C3:(ct + 1) * C3],
                     start=(ct == 0), stop=(ct == KT - 1)).then_inc(s_mm.h, 1)
        em("tensor", stmm3)
        return stv

    def emit_ln2_tail(p_i3):
        nonlocal_vals = {}
        sev0 = s_ve.res(1)
        def stev3a(v, n=st3_holder[0], m=sev0,
                   sd=(stream_done if p_i3 == 0 else 0)):
            if sd:
                v.wait_ge(s_po.h, sd)
            v.wait_ge(s_mm.h, n)
            v.tensor_copy(s2um[:], st_ps[0][0:1, 0:C3])
            v.tensor_copy(r2t2[:], st_ps[1][0:1, 0:C3]).then_inc(s_ve.h, 1)
            v.wait_ge(s_ve.h, m)
        em("vector", stev3a)
        sev1 = s_ve.res(1)
        def stev3b(v, m=sev1):
            v.tensor_scalar(s2um[:], s2um[:], float(1.0 / D), None, OP.mult)
            v.tensor_scalar(r2t2[:], r2t2[:], float(1.0 / D), None,
                            OP.mult).then_inc(s_ve.h, 1)
            v.wait_ge(s_ve.h, m)
        em("vector", stev3b)
        sev = s_ve.res(1)
        def stev3c(v, m=sev):
            v.tensor_mul(r2t1[:], s2um[:], s2um[:])
            v.engine_nop()
            v.engine_nop()
            v.tensor_sub(r2t2[:], r2t2[:], r2t1[:]).then_inc(s_ve.h, 1)
            v.wait_ge(s_ve.h, m)
        em("vector", stev3c)
        wait("scalar", s_ve, sev)
        a3 = s_ac.res()
        em("scalar", lambda a: a.activation(r2t1[:], r2t2[:], AF.Sqrt,
                                            bias=cst[0:1, 1:2]).then_inc(s_ac.h, 1))
        wait("vector", s_ac, a3)
        rv0 = s_ve.res()
        em("vector", lambda v, m=rv0: (v.reciprocal(r2t2[:], r2t1[:]).then_inc(s_ve.h, 1),
                                       v.wait_ge(s_ve.h, m)))
        rv = s_ve.res()
        em("vector", lambda v: v.tensor_mul(r2t1[:], s2um[:], r2t2[:]).then_inc(s_ve.h, 1))
        bcv = s_mm.res()
        def bc3(t, n=rv):
            t.wait_ge(s_ve.h, n)
            t.matmul(st_ps[0][:, 0:C3], ones1[:], r2t2[0:1, :], start=True, stop=True)
            t.matmul(st_ps[1][:, 0:C3], ones1[:], r2t1[0:1, :],
                     start=True, stop=True).then_inc(s_mm.h, 1)
        em("tensor", bc3)
        for ct in range(KT):
            nz30 = s_ve.res()
            nzv = s_ve.res()
            def nz3(v, ct=ct, bcv=bcv, m=nz30):
                v.wait_ge(s_mm.h, bcv)
                sl = slice(ct * C3, (ct + 1) * C3)
                v.tensor_tensor(r2ch[:, sl], r2ch[:, sl], st_ps[0][:, 0:C3],
                                OP.mult).then_inc(s_ve.h, 1)
                v.wait_ge(s_ve.h, m)
                v.tensor_sub(r2ch[:, sl], r2ch[:, sl],
                             st_ps[1][:, 0:C3]).then_inc(s_ve.h, 1)
            em("vector", nz3)
        nz_all = s_ve.n
        ov = s_out.res(16)
        def sto(a, n=nz_all, p_i3=p_i3):
            a.wait_ge(s_ve.h, n)
            a.dma_start(out=out_d[:, p_i3],
                        in_=r2ch[:].rearrange("p (h j) -> p h j", h=KT)).then_inc(s_out.h, 16)
        em("scalar", sto)
        out_sts.append((nz_all, ov))
        return max(sev, nz_all)

    st3_holder = [0]
    prev = None
    for i3 in range(NC3):
        if i3 >= 2:
            wait("sync", s_ve, ev3_done[i3 - 2])
        half = i3 % 2
        dma("sync", y3b[i3 % 2][:].rearrange("p (k j) -> p k j", k=KT),
            y1b_d[:, i3 // 2, :, half * C3:(half + 1) * C3], s_y3[i3 % 2])
        ld_y = dma("sync", y3f[i3 % 2][:].rearrange("p (k j) -> p k j", k=KT),
                   y1f_d[:, i3 // 2, :, half * C3:(half + 1) * C3], s_y3[i3 % 2])
        aguard = wmm_last.get(i3 - 2, 0)
        mmvs = {}
        midv = {}
        c2v = {}
        sinv = {}
        acts = {}
        tct = 0
        for slot in range(FT + 3):
            if slot - 3 >= 0:
                acts[slot - 3] = emit_stage_qv3(i3, slot - 3, sinv[slot - 3])
            if slot < FT:
                mmvs[slot] = emit_fmm(i3, ld_y, slot)
                emit_stage_qv1(i3, slot, mmvs[slot], sinv.get(slot - 3, 0))
            if 0 <= slot - 1 < FT:
                ft = slot - 1
                midv[ft] = emit_stage_mid(i3, ft)
            if 0 <= slot - 2 < FT:
                ft = slot - 2
                sinv[ft] = emit_stage_sin(i3, ft, midv[ft], aguard)
            if prev is not None and tct < KT * 4:
                p_i3, p_ldy, p_act = prev
                pm = emit_tail_piece(p_i3, p_act, tct // 4, tct % 4)
                if tct % 4 == 3:
                    st3_holder[0] = emit_tail_unit(p_i3, p_ldy, p_act,
                                                   tct // 4, pm)
                tct += 1
        if prev is not None:
            while tct < KT * 4:
                p_i3, p_ldy, p_act = prev
                pm = emit_tail_piece(p_i3, p_act, tct // 4, tct % 4)
                if tct % 4 == 3:
                    st3_holder[0] = emit_tail_unit(p_i3, p_ldy, p_act,
                                                   tct // 4, pm)
                tct += 1
            t_ln = emit_ln2_tail(p_i3)
            prev_stev3 = t_ln
        act_last = acts[FT - 1]
        prev = (i3, ld_y, act_last)

    p_i3, p_ldy, p_act = prev
    for ct in range(KT):
        for piece in range(4):
            pm = emit_tail_piece(p_i3, p_act, ct, piece)
        st3_holder[0] = emit_tail_unit(p_i3, p_ldy, p_act, ct, pm)
    emit_ln2_tail(p_i3)

    final_out = s_out.n
    em("scalar", lambda a: a.wait_ge(s_out.h, final_out))
    em("sync", lambda e: e.wait_ge(s_out.h, final_out))

    block = es.enter_context(nc.Block())

    @block.sync
    def _(sync):
        for fn in E["sync"]:
            fn(sync)

    @block.tensor
    def _(tensor):
        for fn in E["tensor"]:
            fn(tensor)

    @block.vector
    def _(vector):
        for fn in E["vector"]:
            fn(vector)

    @block.scalar
    def _(scalar):
        for fn in E["scalar"]:
            fn(scalar)

    @block.gpsimd
    def _(gpsimd):
        for fn in E["gpsimd"]:
            fn(gpsimd)

    g3.close()
    es.close()
    return nc


def kernel(**inputs):
    ins = {k: np.asarray(v, dtype=np.float32) for k, v in inputs.items()}
    x = ins["L0_attn_x"]

    def pm(a, nt):
        X = a.shape[1]
        return np.ascontiguousarray(a.reshape(nt, 128, X).transpose(1, 0, 2).reshape(128, nt * X))

    w1t = pm(np.ascontiguousarray(ins["L0_attn_w1_w"].T), KT)
    v1t = pm(np.ascontiguousarray(ins["L0_attn_v1_w"].T), KT)
    w2t = pm(np.ascontiguousarray(ins["L0_attn_w2_w"].T), KT)
    wf1 = pm(ins["L0_ffn_W1"], KT)
    wf2 = pm(ins["L0_ffn_W2"], FT)

    def col8(a):
        return np.ascontiguousarray(a.reshape(KT, 128).T)

    def col128(a):
        return np.ascontiguousarray(a.reshape(KT, 128, 16).transpose(1, 0, 2).reshape(128, 128))

    cc = np.zeros((128, 448), np.float32)
    cc[:, 0:8] = col8(ins["L0_attn_w1_b"])
    cc[:, 8:16] = col8(ins["L0_attn_v1_b"])
    cc[:, 16:24] = col8(ins["L0_attn_w2_b"])
    cc[:, 24:32] = col8(ins["L0_ffn_b2"])
    cc[:, 32:40] = col8(ins["L0_attn_D_ssm"])
    conv_w = ins["L0_attn_conv_w"][:, 0, :]
    cc[:, 40:48] = col8(conv_w[:, 0])
    cc[:, 48:56] = col8(conv_w[:, 1])
    cc[:, 56:64] = col8(conv_w[:, 2])
    cc[:, 64:192] = col128(ins["L0_attn_A"])
    cc[:, 192:320] = col128(ins["L0_attn_B_ssm"])
    cc[:, 320:448] = col128(ins["L0_attn_C_ssm"])

    def fcol(a):
        return np.ascontiguousarray(a.reshape(FT, 128).T)

    fc = np.zeros((128, 128), np.float32)
    fc[:, 0:32] = fcol(ins["L0_ffn_b1"])
    fc[:, 32:64] = fcol(ins["L0_ffn_quantum_phases"])
    fc[:, 64:96] = fcol(ins["L0_ffn_quantum_weights_real"])
    fc[:, 96:128] = fcol(ins["L0_ffn_quantum_weights_imag"])

    cst = np.zeros((128, 8), np.float32)
    cst[:, 0] = TWO_PI
    cst[:, 1] = 1e-5
    cst[:, 2] = -np.pi
    cst[:, 3] = TWO_PI / 262144.0

    nc = build_program()

    in_maps = []
    for core in range(8):
        xs = x[2 * core:2 * core + 2]
        xF = np.ascontiguousarray(xs.transpose(2, 0, 1)).reshape(D, NT)
        xfc = np.ascontiguousarray(
            xF.reshape(KT, 128, NCH, CH).transpose(1, 2, 0, 3))
        in_maps.append({
            "xf": xfc, "w1t": w1t, "v1t": v1t, "w2t": w2t,
            "wf1": wf1, "wf2": wf2, "cc": cc, "fc": fc, "cst": cst,
        })

    res = run_bass_kernel_spmd(nc, in_maps, core_ids=list(range(8)))

    outs = []
    for core in range(8):
        o = np.asarray(res.results[core]["out"], np.float32)
        oF = o.transpose(2, 0, 1, 3).reshape(D, NT)
        o2 = oF.reshape(D, NSEQ, T).transpose(1, 2, 0)
        outs.append(o2)
    return np.ascontiguousarray(np.concatenate(outs, axis=0))
